# version 22
# baseline (speedup 1.0000x reference)
import numpy as np
import ml_dtypes
import concourse.bass as bass
import concourse.mybir as mybir
import concourse.bass_utils as _bu
from concourse.bass_utils import run_bass_kernel_spmd

if not getattr(_bu, '_ldw_opt_patched', False):
    _orig_get_walrus_args = _bu.get_walrus_args

    def _get_walrus_args_ldw(arch, tmpdir, **kw):
        return _orig_get_walrus_args(arch, tmpdir, **kw) + ["--enable-ldw-opt=true"]

    _bu.get_walrus_args = _get_walrus_args_ldw
    _bu._ldw_opt_patched = True

F32 = mybir.dt.float32
BF16 = mybir.dt.bfloat16
I32 = mybir.dt.int32
AL = mybir.AluOpType
AX = mybir.AxisListType
AF = mybir.ActivationFunctionType

D = 1024
H = 2048
DC = 8
HC = 16
MAGIC = 12582912.0
EPS_LN = 1e-5
EPS_RMS = 1e-6
N_CORES = 8
T_CORE = 4096
TILE = 256
SL = TILE // 128
PREV_T = 1024
DEBUG_SKIP = set()
TILE_STOP = 99
DEBUG_DUMP = []


class Sched:

    def __init__(self, sem_names):
        self.items = []
        self.cnt = {s: 0 for s in sem_names}
        self.tiny_mode = False
        self.tiny_cnt = 0

    def op(self, eng, fn, inc=None, inc_n=1):
        if inc is not None:
            self.cnt[inc] += inc_n
        par = self.par_ref[0] if getattr(self, 'par_ref', None) is not None else 0
        tiny = self.tiny_mode and eng == 'vector'
        if tiny:
            self.tiny_cnt += 1
        self.items.append((eng, 'op', (fn, inc, inc_n, par, tiny, self.tiny_cnt)))

    def wait(self, eng, sem, val):
        self.items.append((eng, 'wait', (sem, val)))

    def v(self, sem):
        return self.cnt[sem]


def build_program(t_core=T_CORE, prev_t=PREV_T):
    nt = t_core // TILE
    n_prev = prev_t // TILE

    nc = bass.Bass('TRN2', target_bir_lowering=False, detect_race_conditions=False)

    xc = nc.dram_tensor('xc', [t_core, D], F32, kind='ExternalInput').ap()
    xp = nc.dram_tensor('xp', [prev_t, D], F32, kind='ExternalInput').ap()
    fcT_h = nc.dram_tensor('fcT_h', [D, H], F32, kind='ExternalInput').ap()
    projT_h = nc.dram_tensor('projT_h', [H, D], F32, kind='ExternalInput').ap()
    rhsT_h = nc.dram_tensor('rhsT_h', [D, D], F32, kind='ExternalInput').ap()
    inw_h = nc.dram_tensor('inw_h', [128, DC], F32, kind='ExternalInput').ap()
    alog_h = nc.dram_tensor('alog_h', [128, DC], F32, kind='ExternalInput').ap()
    cout_h = nc.dram_tensor('cout_h', [128, DC], F32, kind='ExternalInput').ap()
    dskip_h = nc.dram_tensor('dskip_h', [128, DC], F32, kind='ExternalInput').ap()
    ffw_h = nc.dram_tensor('ffw_h', [128, DC], F32, kind='ExternalInput').ap()
    idf_h = nc.dram_tensor('idf_h', [128, 128], F32, kind='ExternalInput').ap()
    idb_h = nc.dram_tensor('idb_h', [128, 128], BF16, kind='ExternalInput').ap()
    ones_h = nc.dram_tensor('ones_h', [1, 128], F32, kind='ExternalInput').ap()
    iota_h = nc.dram_tensor('iota_h', [1, prev_t], F32, kind='ExternalInput').ap()
    out_d = nc.dram_tensor('out', [t_core, D], F32, kind='ExternalOutput').ap()

    sems = ['sx', 'so', 'sg', 'sv', 'ss', 'st', 'svd']
    sch = Sched(sems)

    ctxs = []
    pro_ctxs = []

    def sb(name, shape, dt=F32, scope=None):
        g = nc.sbuf_tensor(name, shape, dt)
        t = g.__enter__()
        (ctxs if scope is None else scope).append(g)
        return t

    def psum(name, shape):
        g = nc.psum_tensor(name, shape, F32)
        t = g.__enter__()
        ctxs.append(g)
        return t

    fcT = sb('fcT', [128, DC, H], BF16)
    projT = sb('projT', [128, HC, D], BF16)
    qrhs = sb('qrhs', [128, DC, D], BF16)
    rs_repl = sb('rs_repl', [128, D], F32)
    idf = sb('idf', [128, 128], F32)
    idb = sb('idb', [128, 128], BF16)
    ones_c = sb('ones_c', [1, 128], F32)
    iota_r = sb('iota_r', [1, prev_t], F32)
    inw_v = sb('inw_v', [128, DC], F32)
    alog_v = sb('alog_v', [128, DC], F32)
    cout_v = sb('cout_v', [128, DC], F32)
    dskip_v = sb('dskip_v', [128, DC], F32)
    ffw_v = sb('ffw_v', [128, DC], F32)
    e_v = sb('e_v', [128, DC], F32)
    a_v = sb('a_v', [128, DC], F32)
    cw_v = sb('cw_v', [128, DC], F32)
    dw_v = sb('dw_v', [128, DC], F32)
    carry0 = sb('carry0', [128, DC], F32)
    carryB = sb('carryB', [128, DC], F32)
    rs_e = sb('rs_e', [128, DC], F32)
    rrs_e = sb('rrs_e', [128, DC], F32)

    pb = [psum('pb%d' % i, [128, 512]) for i in range(8)]

    xbuf = [sb('xb%d' % i, [128, SL, D], F32) for i in range(2)]
    qi_t = sb('qi_t', [128, D], I32)
    stats = sb('stats', [128, 128], F32)
    trash_s = sb('trash_s', [128, D], F32, scope=pro_ctxs)
    wscr = [sb('wscr%d' % i, [128, H], F32, scope=pro_ctxs) for i in range(2)]
    powt = sb('powt', [128, DC, prev_t], F32, scope=pro_ctxs)
    xpb = [sb('xpb%d' % i, [128, SL, D], F32, scope=pro_ctxs) for i in range(2)]
    rstg = [sb('rstg%d' % i, [128, D], F32, scope=pro_ctxs) for i in range(2)]
    upb = sb('upb', [128, SL, D], F32, scope=pro_ctxs)
    trash_f = sb('trash_f', [128, TILE], F32, scope=pro_ctxs)
    qrt = sb('qrt', [128, D], F32, scope=pro_ctxs)
    rowscr = sb('rowscr', [1, 128], F32, scope=pro_ctxs)

    ST = stats

    COL = {}
    _next = [0]
    _par = [0]
    sch.par_ref = _par

    def scol(name, n=SL):
        if name not in COL:
            COL[name] = _next[0]
            _next[0] += n
        c = COL[name] + 64 * _par[0]
        return ST[:, c:c + n]

    def scol1(name, sl):
        c = COL[name] + 64 * _par[0]
        return ST[:, c + sl:c + sl + 1]

    for nmm in ['sx1', 'sq1', 'm1', 'msq1', 'var1', 'w1', 'wc', 's1', 'cc',
                'sq2', 'msq2', 'w2', 'r2', 'sx3', 'sq3', 'm3', 'msq3', 'var3',
                'w3', 's3', 'mx', 'mn', 'az', 'ls', 'c3', 'nr1', 'nr2', 'rc1', 'rc2']:
        scol(nmm)

    def nr_polish(eng_ops, yname, wname, iters=2):
        assert iters % 2 == 0
        cols = [yname, 'nr2']
        for it in range(iters):
            t1, t2 = scol('nr1'), scol('nr1')
            ya, yb_ = scol(cols[it % 2]), scol(cols[(it + 1) % 2])
            w = scol(wname)
            sch.op('vector', lambda e, a=ya: e.tensor_tensor(out=scol('nr1'), in0=a, in1=a, op=AL.mult))
            sch.op('vector', lambda e, w=w: e.tensor_tensor(out=scol('rc1'), in0=scol('nr1'), in1=w, op=AL.mult))
            sch.op('vector', lambda e: e.tensor_scalar(out=scol('nr1'), in0=scol('rc1'), scalar1=-0.5,
                                                       scalar2=1.5, op0=AL.mult, op1=AL.add))
            sch.op('vector', lambda e, a=ya, b=yb_: e.tensor_tensor(out=b, in0=a, in1=scol('nr1'), op=AL.mult))

    def rsqrt_seq(wname, yname, inc_tag=None):
        w = scol(wname)
        y = scol(yname)
        rec = scol('nr2')
        sch.op('vector', lambda e: e.reciprocal(out=rec, in_=w), inc='sv')
        sch.wait('scalar', 'sv', sch.v('sv'))
        sch.op('scalar', lambda e: e.activation(out=y, in_=rec, func=AF.Sqrt), inc='ss')
        sch.wait('vector', 'ss', sch.v('ss'))

    def x_ap(j):
        return xc[j * TILE:(j + 1) * TILE, :].rearrange('(s p) d -> p s d', p=128)

    def o_ap(j):
        return out_d[j * TILE:(j + 1) * TILE, :].rearrange('(s p) d -> p s d', p=128)

    def xp_ap(j):
        return xp[j * TILE:(j + 1) * TILE, :].rearrange('(s p) d -> p s d', p=128)

    for dst, src in ((idf, idf_h), (idb, idb_h), (ones_c, ones_h), (iota_r, iota_h),
                     (inw_v, inw_h), (alog_v, alog_h), (cout_v, cout_h),
                     (dskip_v, dskip_h), (ffw_v, ffw_h)):
        sch.op('sync', lambda e, d=dst, s=src: e.dma_start(out=d[:], in_=s), inc='sx', inc_n=16)
    sx_consts = sch.v('sx')
    x_sx = {}
    sch.op('sync', lambda e: e.dma_start(out=xbuf[0][:], in_=x_ap(0)), inc='sx', inc_n=16)
    x_sx[0] = sch.v('sx')
    if nt > 1:
        sch.op('sync', lambda e: e.dma_start(out=xbuf[1][:], in_=x_ap(1)), inc='sx', inc_n=16)
        x_sx[1] = sch.v('sx')

    for hc in range(HC if 'projT' not in DEBUG_SKIP else 0):
        ap_in = projT_h[hc * 128:(hc + 1) * 128, :]
        sch.op('gpsimd', lambda e, a=ap_in, hc=hc: e.dma_start(
            out=projT[:, hc, :], in_=a), inc='sg', inc_n=16)
    sg_proj = sch.v('sg')

    sch.wait('scalar', 'sx', sx_consts)
    sch.op('scalar', lambda e: e.activation(out=e_v[:], in_=alog_v[:], func=AF.Exp))
    sch.op('scalar', lambda e: e.activation(out=a_v[:], in_=e_v[:], func=AF.Exp,
                                            scale=-1.0), inc='ss')
    ss_av = sch.v('ss')
    sch.wait('vector', 'sx', sx_consts)
    sch.tiny_mode = True
    sch.op('vector', lambda e: e.tensor_tensor(out=cw_v[:], in0=cout_v[:], in1=inw_v[:], op=AL.mult))
    sch.op('vector', lambda e: e.tensor_tensor(out=dw_v[:], in0=dskip_v[:], in1=inw_v[:], op=AL.mult))
    sch.op('vector', lambda e: e.memset(carry0[:], 0.0), inc='sv')
    sch.tiny_mode = False
    sv_vecs = sch.v('sv')

    n_half = (prev_t // 512) if 'powtile' not in DEBUG_SKIP else 0
    for hf in range(n_half):
        bank = pb[6 + (hf % 2)]
        sch.wait('tensor', 'sx', sx_consts)
        if hf >= 2:
            sch.wait('tensor', 'sv', sv_vecs + (hf - 1) * DC)
        sch.op('tensor', lambda e, b=bank, hf=hf: e.matmul(
            b[:], ones_c[:], iota_r[0:1, hf * 512:(hf + 1) * 512],
            start=True, stop=True), inc='st')
        st_iota = sch.v('st')
        sch.wait('vector', 'st', st_iota)
        sch.wait('vector', 'ss', ss_av)
        for dc in range(DC):
            if hf > 0 or dc >= 2:
                sch.wait('vector', 'ss', sch.v('ss'))
            sch.op('vector', lambda e, b=bank, dc=dc, hf=hf: e.tensor_scalar(
                out=wscr[0][:, (dc % 2) * 512:(dc % 2) * 512 + 512], in0=b[:],
                scalar1=float(prev_t - 1), scalar2=e_v[:, dc:dc + 1],
                op0=AL.subtract, op1=AL.mult), inc='sv')
            sch.wait('scalar', 'sv', sch.v('sv'))
            sch.op('scalar', lambda e, dc=dc, hf=hf: e.activation(
                out=powt[:, dc, hf * 512:(hf + 1) * 512],
                in_=wscr[0][:, (dc % 2) * 512:(dc % 2) * 512 + 512],
                func=AF.Exp), inc='ss')
    ss_pow = sch.v('ss')

    for dc in range(DC if 'fcT' not in DEBUG_SKIP else 0):
        ap_in = fcT_h[dc * 128:(dc + 1) * 128, :]
        sch.op('gpsimd', lambda e, a=ap_in, dc=dc: e.dma_start(
            out=fcT[:, dc, :], in_=a), inc='sg', inc_n=16)

    rstate = {'sv_q': sch.v('sv'), 'ev': []}

    def emit_rhsT_ec(ec):
        buf = rstg[ec % 2]
        if ec >= 2:
            sch.wait('sync', 'sv', rstate['sv_q'])
        ap_in = rhsT_h[ec * 128:(ec + 1) * 128, :]
        sch.op('sync', lambda e, a=ap_in, b=buf: e.dma_start(out=b[:], in_=a),
               inc='sx', inc_n=16)
        sxv = sch.v('sx')
        sch.wait('vector', 'sx', sxv)
        sch.tiny_mode = True
        sch.op('vector', lambda e, b=buf, ec=ec: e.reduce_max(
            out=rs_e[:, ec:ec + 1], in_=b[:, 0:D], axis=AX.X, apply_absolute_value=True))
        sch.op('vector', lambda e, ec=ec: e.tensor_scalar(
            out=rs_e[:, ec:ec + 1], in0=rs_e[:, ec:ec + 1], scalar1=1e-12,
            scalar2=1.0 / 127.0, op0=AL.max, op1=AL.mult))
        sch.op('vector', lambda e, ec=ec: e.reciprocal(
            out=rrs_e[:, ec:ec + 1], in_=rs_e[:, ec:ec + 1]))
        sch.tiny_mode = False
        sch.op('vector', lambda e, b=buf, ec=ec: e.tensor_scalar(
            out=b[:, 0:D], in0=b[:, 0:D], scalar1=rrs_e[:, ec:ec + 1], scalar2=None,
            op0=AL.mult))
        sch.op('vector', lambda e, b=buf: e.tensor_scalar(
            out=b[:, 0:D], in0=b[:, 0:D], scalar1=127.0, scalar2=-127.0,
            op0=AL.min, op1=AL.max))
        sch.op('vector', lambda e, b=buf: e.tensor_scalar(
            out=b[:, 0:D], in0=b[:, 0:D], scalar1=MAGIC, scalar2=MAGIC,
            op0=AL.add, op1=AL.subtract))
        sch.op('vector', lambda e, b=buf: e.tensor_copy(out=qi_t[:], in_=b[:, 0:D]))
        sch.op('vector', lambda e: e.tensor_copy(out=qrt[:], in_=qi_t[:]), inc='sv')
        sv_q = sch.v('sv')
        for dc in range(DC):
            bank = pb[dc % 2]
            i = ec * DC + dc
            if i >= 2:
                sch.wait('tensor', 'ss', rstate['ev'][i - 2])
            sch.wait('tensor', 'sv', sv_q)
            sch.op('tensor', lambda e, b=bank, dc=dc: e.transpose(
                b[:, 0:128], qrt[:, dc * 128:(dc + 1) * 128], idf[:]), inc='st')
            stv = sch.v('st')
            sch.wait('scalar', 'st', stv)
            sch.op('scalar', lambda e, b=bank, dc=dc, ec=ec: e.activation(
                out=qrhs[:, dc, ec * 128:(ec + 1) * 128], in_=b[:, 0:128],
                func=AF.Copy), inc='ss')
            rstate['ev'].append(sch.v('ss'))
        sch.wait('tensor', 'ss', rstate['ev'][-1])
        sch.op('tensor', lambda e, ec=ec: e.transpose(
            pb[2][0:1, 0:128], rs_e[:, ec:ec + 1], idf[:]), inc='st')
        stv = sch.v('st')
        sch.wait('vector', 'st', stv)
        sch.op('vector', lambda e: e.tensor_copy(out=rowscr[:], in_=pb[2][0:1, 0:128]),
               inc='sv')
        svv = sch.v('sv')
        sch.wait('tensor', 'sv', svv)
        sch.op('tensor', lambda e: e.matmul(pb[3][:, 0:128], ones_c[:], rowscr[:],
                                            start=True, stop=True), inc='st')
        stv = sch.v('st')
        sch.wait('vector', 'st', stv)
        sch.op('vector', lambda e, ec=ec: e.tensor_copy(
            out=rs_repl[:, ec * 128:(ec + 1) * 128], in_=pb[3][:, 0:128]), inc='sv')
        rstate['sv_q'] = sch.v('sv')

    cstate = {'cur': carry0, 'alt': carryB}

    def emit_carry_mt(mt):
        carry_cur, carry_alt = cstate['cur'], cstate['alt']
        xpc = xpb[mt % 2]
        sch.wait('scalar', 'sx', xp_sx[mt])
        for sl in range(SL):
            sch.op('scalar', lambda e, sl=sl, xpc=xpc: e.activation(
                out=trash_s[:], in_=xpc[:, sl, :], func=AF.Copy,
                accum_out=scol1('sx1', sl)))
            sch.op('scalar', lambda e, sl=sl, xpc=xpc: e.activation(
                out=trash_s[:], in_=xpc[:, sl, :], func=AF.Square,
                accum_out=scol1('sq1', sl)), inc='ss' if sl == SL - 1 else None)
        ssv = sch.v('ss')
        sch.wait('vector', 'ss', ssv)
        sch.tiny_mode = True
        sch.op('vector', lambda e: e.tensor_scalar(out=scol('m1', 4), in0=scol('sx1', 4),
                                                   scalar1=1.0 / D, scalar2=None, op0=AL.mult))
        sch.op('vector', lambda e: e.tensor_tensor(out=scol('rc1'), in0=scol('m1'),
                                                   in1=scol('m1'), op=AL.mult))
        sch.op('vector', lambda e: e.tensor_tensor(out=scol('var1'), in0=scol('msq1'),
                                                   in1=scol('rc1'), op=AL.subtract))
        sch.op('vector', lambda e: e.tensor_scalar(out=scol('wc'), in0=scol('var1'),
                                                   scalar1=1.0 + EPS_RMS,
                                                   scalar2=EPS_RMS * EPS_LN,
                                                   op0=AL.mult, op1=AL.add))
        rsqrt_seq('wc', 'cc')
        sch.tiny_mode = False
        for sl in range(SL):
            sch.op('vector', lambda e, sl=sl, xpc=xpc: e.tensor_scalar(
                out=upb[:, sl, :], in0=xpc[:, sl, :], scalar1=scol1('m1', sl),
                scalar2=scol1('cc', sl), op0=AL.subtract, op1=AL.mult),
                inc='sv' if sl == SL - 1 else None)
        sv_u = sch.v('sv')
        for dc in range(DC):
            bank, reg = pb[4 + dc % 4], 0
            if dc >= 4:
                sch.wait('tensor', 'sv', sch.v('sv'))
            sch.wait('tensor', 'sv', sv_u)
            for sl in range(SL):
                sch.op('tensor', lambda e, b=bank, r=reg, dc=dc, sl=sl: e.transpose(
                    b[:, r + sl * 128:r + (sl + 1) * 128],
                    upb[:, sl, dc * 128:(dc + 1) * 128], idf[:]),
                    inc='st' if sl == SL - 1 else None)
            stv = sch.v('st')
            sch.wait('vector', 'st', stv)
            sch.op('vector', lambda e, b=bank, r=reg, dc=dc, mt=mt: e.tensor_tensor(
                out=trash_f[:, 0:TILE], in0=b[:, r:r + TILE],
                in1=powt[:, dc, mt * TILE:(mt + 1) * TILE], op=AL.mult))
            sch.tiny_mode = True
            sch.op('vector', lambda e: e.reduce_sum(
                out=scol('nr2')[:, 0:1], in_=trash_f[:, 0:TILE], axis=AX.X))
            sch.op('vector', lambda e, dc=dc, cc_=carry_cur, ca=carry_alt: e.tensor_tensor(
                out=ca[:, dc:dc + 1], in0=cc_[:, dc:dc + 1],
                in1=scol('nr2')[:, 0:1], op=AL.add), inc='sv')
            sch.tiny_mode = False
        cstate['cur'], cstate['alt'] = carry_alt, carry_cur

    n_mt = n_prev if 'carry' not in DEBUG_SKIP else 0
    n_ec = DC if 'qrhs' not in DEBUG_SKIP else 0
    xp_sx = {}
    for mt in range(min(n_mt, 2)):
        sch.op('sync', lambda e, mt=mt: e.dma_start(out=xpb[mt % 2][:], in_=xp_ap(mt)),
               inc='sx', inc_n=16)
        xp_sx[mt] = sch.v('sx')
    for k in range(max(n_ec, n_mt)):
        if k < n_ec:
            emit_rhsT_ec(k)
        if k < n_mt:
            emit_carry_mt(k)
            if k + 2 < n_mt:
                sch.wait('sync', 'sv', sch.v('sv'))
                sch.op('sync', lambda e, mt=k + 2: e.dma_start(
                    out=xpb[mt % 2][:], in_=xp_ap(mt)), inc='sx', inc_n=16)
                xp_sx[k + 2] = sch.v('sx')
    if cstate['cur'] is not carry0 and n_mt:
        sch.tiny_mode = True
        sch.op('vector', lambda e: e.tensor_copy(out=carry0[:], in_=carryB[:]), inc='sv')
        sch.tiny_mode = False
    sv_carry = sch.v('sv')

    PRO = dict(sv=sch.v('sv'), ss=sch.v('ss'), st=sch.v('st'),
               sx=sch.v('sx'), sg=sch.v('sg'))
    for eng in ('sync', 'vector', 'scalar', 'tensor'):
        for s in ('sv', 'ss', 'st', 'sg'):
            sch.wait(eng, s, PRO[s])

    mark = len(sch.items)
    sem_handles = {}
    sem_ctxs = []
    for s in sems:
        g = nc.semaphore(s)
        sem_handles[s] = g.__enter__()
        sem_ctxs.append(g)

    def make_engine_fn(name, lo, hi):
        def fn(eng):
            for (e, kind, payload) in sch.items[lo:hi]:
                if e != name:
                    continue
                if kind == 'wait':
                    s, v = payload
                    if v > 0:
                        eng.wait_ge(sem_handles[s], v)
                else:
                    f, inc, inc_n, par, tiny, tc = payload
                    _par[0] = par
                    if name == 'vector' and tc > 0:
                        eng.wait_ge(sem_handles['svd'], tc - (1 if tiny else 0))
                    ins = f(eng)
                    if inc is not None:
                        ins.then_inc(sem_handles[inc], inc_n)
                        if tiny:
                            eng.engine_nop().then_inc(sem_handles['svd'], 1)
                    elif tiny:
                        ins.then_inc(sem_handles['svd'], 1)
        return fn

    def emit_block(lo, hi):
        g = nc.Block()
        blk = g.__enter__()
        blk.sync(make_engine_fn('sync', lo, hi))
        blk.vector(make_engine_fn('vector', lo, hi))
        blk.scalar(make_engine_fn('scalar', lo, hi))
        blk.tensor(make_engine_fn('tensor', lo, hi))
        blk.gpsimd(make_engine_fn('gpsimd', lo, hi))
        g.__exit__(None, None, None)

    emit_block(0, mark)
    for g in reversed(pro_ctxs):
        g.__exit__(None, None, None)

    tmA = sb('tmA', [128, SL, D], F32)
    u_t = sb('u_t', [128, SL, D], F32)
    ud_fm = sb('ud_fm', [128, DC, TILE], F32)
    trash_d = ud_fm[:].rearrange('p a b -> p (a b)')
    hbuf = [sb('hb%d' % i, [128, DC, TILE], F32) for i in range(2)]
    so_fm = sb('so_fm', [128, DC, TILE], F32)
    ybuf = [sb('yb%d' % i, [128, SL, D], F32) for i in range(2)]
    x2_t = sb('x2_t', [128, SL, D], F32)
    n2_tm = sb('n2_tm', [128, SL, D], F32)
    n2_fm = sb('n2_fm', [128, DC, TILE], BF16)
    hid_fm = sb('hid_fm', [128, HC, TILE], BF16)
    q_tm = n2_tm
    q_fm = sb('q_fm', [128, DC, TILE], BF16)

    y1_done = {}
    out_done = {}
    x2_sv = {}
    quant_sv = {}
    sostt_sv = {}
    scan_sv = {}
    xau_sv = {}
    mom1_ss = {}
    mom2_ss = {}
    mom3_ss = {}
    s1cc_ss = {}
    r2_ss = {}
    n2e_ss = {}
    ud_ss = {}
    silu_ss = {}
    qe_ss = {}
    sv_n2 = {}
    n2T_st = {}
    soT_st = {}
    qT_st = {}
    uT_st = {}
    out_sl = {}

    def emit_mom1(t):
        _par[0] = t % 2
        xs = xbuf[t % 2]
        sch.wait('scalar', 'sx', x_sx[t])
        for sl in range(SL):
            sch.op('scalar', lambda e, sl=sl, xs=xs: e.activation(
                out=trash_d[:, 0:D], in_=xs[:, sl, :], func=AF.Copy, accum_out=scol1('sx1', sl)))
            sch.op('scalar', lambda e, sl=sl, xs=xs: e.activation(
                out=trash_d[:, 0:D], in_=xs[:, sl, :], func=AF.Square, accum_out=scol1('sq1', sl)),
                inc='ss' if sl == SL - 1 else None)
        mom1_ss[t] = sch.v('ss')

    def emit_tiny1(t):
        _par[0] = t % 2
        sch.wait('vector', 'ss', mom1_ss[t])
        sch.tiny_mode = True
        sch.op('vector', lambda e: e.tensor_scalar(out=scol('m1', 4), in0=scol('sx1', 4),
                                                   scalar1=1.0 / D, scalar2=None, op0=AL.mult))
        sch.op('vector', lambda e: e.tensor_tensor(out=scol('rc1'), in0=scol('m1'),
                                                   in1=scol('m1'), op=AL.mult))
        sch.op('vector', lambda e: e.tensor_tensor(out=scol('var1'), in0=scol('msq1'),
                                                   in1=scol('rc1'), op=AL.subtract))
        sch.op('vector', lambda e: e.tensor_scalar(out=scol('w1'), in0=scol('var1'),
                                                   scalar1=EPS_LN, scalar2=None, op0=AL.add))
        sch.op('vector', lambda e: e.tensor_scalar(out=scol('wc'), in0=scol('var1'),
                                                   scalar1=1.0 + EPS_RMS,
                                                   scalar2=EPS_RMS * EPS_LN,
                                                   op0=AL.mult, op1=AL.add))
        sch.op('vector', lambda e: e.reciprocal(out=scol('rc1', 4), in_=scol('w1', 4)),
               inc='sv')
        sch.tiny_mode = False
        sch.wait('scalar', 'sv', sch.v('sv'))
        sch.op('scalar', lambda e: e.activation(out=scol('s1', 4), in_=scol('rc1', 4),
                                                func=AF.Sqrt), inc='ss')
        s1cc_ss[t] = sch.v('ss')

    def emit_tiny2(t):
        _par[0] = t % 2
        sch.wait('vector', 'ss', mom2_ss[t])
        sch.tiny_mode = True
        sch.op('vector', lambda e: e.tensor_scalar(out=scol('w2'), in0=scol('sq2'),
                                                   scalar1=1.0 / D, scalar2=EPS_RMS,
                                                   op0=AL.mult, op1=AL.add))
        sch.op('vector', lambda e: e.reciprocal(out=scol('rc1'), in_=scol('w2')), inc='sv')
        sch.tiny_mode = False
        sch.wait('scalar', 'sv', sch.v('sv'))
        sch.op('scalar', lambda e: e.activation(out=scol('r2'), in_=scol('rc1'),
                                                func=AF.Sqrt), inc='ss')
        r2_ss[t] = sch.v('ss')

    def emit_n2(t):
        _par[0] = t % 2
        sch.wait('vector', 'ss', r2_ss[t])
        sch.wait('vector', 'st', qT_st.get(t - 2, 0))
        for sl in range(SL):
            sch.op('vector', lambda e, sl=sl: e.tensor_scalar(
                out=n2_tm[:, sl, :], in0=tmA[:, sl, :], scalar1=scol1('r2', sl),
                scalar2=None, op0=AL.mult), inc='sv' if sl == SL - 1 else None)
        sv_n2[t] = sch.v('sv')

    N2T_BANKS = (4, 5, 0, 1)

    def emit_n2T(t):
        n2e = {}
        for dc in range(DC):
            bank = pb[N2T_BANKS[dc % 4]]
            sch.wait('tensor', 'sv', sv_n2[t])
            if dc < 2:
                sch.wait('tensor', 'sv', x2_sv.get(t - 1, 0))
            elif dc >= 4:
                sch.wait('tensor', 'ss', n2e[dc - 4])
            for sl in range(SL):
                sch.op('tensor', lambda e, b=bank, dc=dc, sl=sl: e.transpose(
                    b[:, sl * 128:(sl + 1) * 128],
                    n2_tm[:, sl, dc * 128:(dc + 1) * 128], idf[:]),
                    inc='st' if sl == SL - 1 else None)
            stv = sch.v('st')
            sch.wait('scalar', 'st', stv)
            sch.op('scalar', lambda e, b=bank, dc=dc: e.activation(
                out=n2_fm[:, dc, :], in_=b[:, 0:TILE], func=AF.Copy,
                scale=ffw_v[:, dc:dc + 1]), inc='ss')
            n2e[dc] = sch.v('ss')
        n2T_st[t] = sch.v('st')
        n2e_ss[t] = n2e

    def emit_xau(t):
        _par[0] = t % 2
        xs = xbuf[t % 2]
        sch.wait('vector', 'ss', s1cc_ss[t])
        sch.wait('vector', 'st', uT_st.get(t - 1, 0))
        for sl in range(SL):
            sch.op('vector', lambda e, sl=sl, xs=xs: e.tensor_scalar(
                out=u_t[:, sl, :], in0=xs[:, sl, :], scalar1=scol1('m1', sl),
                scalar2=scol1('cc', sl), op0=AL.subtract, op1=AL.mult),
                inc='sv' if sl == SL - 1 else None)
        xau_sv[t] = sch.v('sv')

    UT_BANKS = (0, 1, 4, 5)

    def emit_scanblock(t):
        hb = hbuf[t % 2]
        hprev = hbuf[(t + 1) % 2]
        scan_dc = {}
        ud_dc = {}
        for dc in range(DC):
            bank = pb[UT_BANKS[dc % 4]]
            sch.wait('tensor', 'sv', xau_sv[t])
            if dc < 4:
                sch.wait('tensor', 'sv', y1_done.get(t - 1, 0))
                sch.wait('tensor', 'sv', x2_sv.get(t - 1, 0))
                if t >= 1:
                    sch.wait('tensor', 'ss', n2e_ss[t - 1][DC - 1])
            else:
                sch.wait('tensor', 'sv', scan_dc[dc - 4])
                sch.wait('tensor', 'ss', ud_dc[dc - 4])
            for sl in range(SL):
                sch.op('tensor', lambda e, b=bank, dc=dc, sl=sl: e.transpose(
                    b[:, sl * 128:(sl + 1) * 128],
                    u_t[:, sl, dc * 128:(dc + 1) * 128], idf[:]),
                    inc='st' if sl == SL - 1 else None)
            stv = sch.v('st')
            sch.wait('vector', 'st', stv)
            if t == 0:
                init_ap = carry0[:, dc:dc + 1]
            else:
                init_ap = hprev[:, dc, TILE - 1:TILE]
            sch.op('vector', lambda e, b=bank, dc=dc, ia=init_ap, hb=hb: e.tensor_tensor_scan(
                out=hb[:, dc, :], data0=a_v[:, dc:dc + 1].broadcast_to((128, TILE)),
                data1=b[:, 0:TILE], initial=ia, op0=AL.mult, op1=AL.add), inc='sv')
            scan_dc[dc] = sch.v('sv')
            sch.wait('scalar', 'sv', scan_dc[dc])
            sch.op('scalar', lambda e, b=bank, dc=dc: e.activation(
                out=ud_fm[:, dc, :], in_=b[:, 0:TILE], func=AF.Copy,
                scale=dw_v[:, dc:dc + 1]), inc='ss')
            ud_dc[dc] = sch.v('ss')
        scan_sv[t] = scan_dc[DC - 1]
        ud_ss[t] = ud_dc[DC - 1]
        uT_st[t] = sch.v('st')

    def emit_M1(t, mom2_tile=None):
        silu_hc = {}
        sch.wait('tensor', 'ss', n2e_ss[t][DC - 1])
        if t >= 1:
            sch.wait('tensor', 'ss', qe_ss[t - 1][DC - 1])
        for hp in range(HC // 2):
            bank = pb[2 + (hp % 2)]
            if hp >= 2:
                sch.wait('tensor', 'ss', silu_hc[2 * (hp - 2)])
            for sub in range(2):
                hc = 2 * hp + sub
                for dc in range(DC):
                    sch.op('tensor', lambda e, b=bank, hc=hc, sub=sub, dc=dc: e.matmul(
                        b[:, sub * TILE:(sub + 1) * TILE],
                        fcT[:, dc, hc * 128:(hc + 1) * 128],
                        n2_fm[:, dc, :], start=(dc == 0), stop=(dc == DC - 1)),
                        inc='st' if dc == DC - 1 else None)
            stv = sch.v('st')
            sch.wait('scalar', 'st', stv)
            sch.op('scalar', lambda e, b=bank, hp=hp: e.activation(
                out=hid_fm[:, 2 * hp:2 * hp + 2, :].rearrange('p a b -> p (a b)'),
                in_=b[:, 0:2 * TILE], func=AF.Silu), inc='ss')
            v = sch.v('ss')
            silu_hc[2 * hp] = v
            silu_hc[2 * hp + 1] = v
            if hp == 1 and mom2_tile is not None:
                emit_mom2(mom2_tile)
        silu_ss[t] = silu_hc

    def emit_sostt(t):
        hb = hbuf[t % 2]
        sch.wait('vector', 'ss', ud_ss[t])
        for dc in range(DC):
            sch.op('vector', lambda e, dc=dc, hb=hb: e.scalar_tensor_tensor(
                out=so_fm[:, dc, :], in0=hb[:, dc, :], scalar=cw_v[:, dc:dc + 1],
                in1=ud_fm[:, dc, :], op0=AL.mult, op1=AL.add),
                inc='sv' if dc == DC - 1 else None)
        sostt_sv[t] = sch.v('sv')

    def emit_M2x2(t):
        _par[0] = t % 2
        yb = ybuf[t % 2]
        sch.wait('tensor', 'sv', x2_sv.get(t - 1, 0))
        if t + 1 in scan_sv:
            sch.wait('tensor', 'sv', scan_sv[t + 1])
            sch.wait('tensor', 'ss', ud_ss[t + 1])
        for sl in range(SL):
            for hc in range(HC):
                if sl == 0:
                    sch.wait('tensor', 'ss', silu_ss[t][hc])
                for hf in range(2):
                    b = pb[4 + hf]
                    sch.op('tensor', lambda e, b=b, hc=hc, hf=hf, sl=sl: e.matmul(
                        b[:], hid_fm[:, hc, sl * 128:(sl + 1) * 128],
                        projT[:, hc, hf * 512:(hf + 1) * 512],
                        start=(hc == 0), stop=(hc == HC - 1)),
                        inc='st' if (hc == HC - 1 and hf == 1) else None)
            stv = sch.v('st')
            sch.wait('vector', 'st', stv)
            for hf in range(2):
                b = pb[4 + hf]
                sch.op('vector', lambda e, b=b, sl=sl, hf=hf, yb=yb: e.tensor_tensor(
                    out=x2_t[:, sl, hf * 512:(hf + 1) * 512], in0=b[:],
                    in1=yb[:, sl, hf * 512:(hf + 1) * 512], op=AL.add),
                    inc='sv' if hf == 1 else None)
            x2sl = sch.v('sv')
            sch.tiny_mode = True
            sch.op('vector', lambda e, sl=sl: e.reduce_max(
                out=scol1('mx', sl), in_=x2_t[:, sl, :], axis=AX.X))
            sch.op('vector', lambda e, sl=sl: e.tensor_reduce(
                out=scol1('mn', sl), in_=x2_t[:, sl, :], axis=AX.X, op=AL.min))
            sch.tiny_mode = False
            sch.wait('scalar', 'sv', x2sl)
            sch.op('scalar', lambda e, sl=sl: e.activation(
                out=trash_d[:, 0:D], in_=x2_t[:, sl, :], func=AF.Copy,
                accum_out=scol1('sx3', sl)))
            sch.op('scalar', lambda e, sl=sl: e.activation(
                out=trash_d[:, 0:D], in_=x2_t[:, sl, :], func=AF.Square,
                accum_out=scol1('sq3', sl)), inc='ss' if sl == SL - 1 else None)
            if sl == 0:
                sch.wait('tensor', 'sv', x2sl)
        x2_sv[t] = sch.v('sv')
        mom3_ss[t] = sch.v('ss')

    def emit_soT(t):
        sch.wait('tensor', 'sv', sostt_sv[t])
        sch.wait('tensor', 'sv', scan_sv[t])
        sch.wait('tensor', 'ss', ud_ss[t])
        sch.wait('tensor', 'sv', out_done.get(t - 2, 0))
        for sl in range(SL):
            banks = (pb[0], pb[1]) if sl == 0 else (pb[6], pb[7])
            for dc in range(DC):
                b = banks[dc // 4]
                r = (dc % 4) * 128
                sch.op('tensor', lambda e, b=b, r=r, dc=dc, sl=sl: e.transpose(
                    b[:, r:r + 128], so_fm[:, dc, sl * 128:(sl + 1) * 128], idf[:]),
                    inc='st' if (sl == SL - 1 and dc == DC - 1) else None)
        soT_st[t] = sch.v('st')

    def emit_x1y1(t):
        _par[0] = t % 2
        xs = xbuf[t % 2]
        yb = ybuf[t % 2]
        sch.wait('vector', 'st', soT_st[t])
        if t >= 2:
            sch.wait('vector', 'so', 32 * (t - 1))
        for sl in range(SL):
            banks = (pb[0], pb[1]) if sl == 0 else (pb[6], pb[7])
            sch.op('vector', lambda e, sl=sl, xs=xs: e.tensor_scalar(
                out=tmA[:, sl, :], in0=xs[:, sl, :], scalar1=scol1('m1', sl),
                scalar2=scol1('s1', sl), op0=AL.subtract, op1=AL.mult))
            for hf in range(2):
                b = banks[hf]
                sch.op('vector', lambda e, b=b, sl=sl, hf=hf: e.tensor_tensor(
                    out=tmA[:, sl, hf * 512:(hf + 1) * 512], in0=b[:],
                    in1=tmA[:, sl, hf * 512:(hf + 1) * 512], op=AL.add))
            sch.op('vector', lambda e, sl=sl, xs=xs, yb=yb: e.tensor_tensor(
                out=yb[:, sl, :], in0=tmA[:, sl, :], in1=xs[:, sl, :], op=AL.add),
                inc='sv')
        y1_done[t] = sch.v('sv')

    def emit_mom2(t):
        _par[0] = t % 2
        sch.wait('scalar', 'sv', y1_done[t])
        for sl in range(SL):
            sch.op('scalar', lambda e, sl=sl: e.activation(
                out=trash_d[:, 0:D], in_=tmA[:, sl, :], func=AF.Square,
                accum_out=scol1('sq2', sl)), inc='ss' if sl == SL - 1 else None)
        mom2_ss[t] = sch.v('ss')

    def emit_tiny3(t):
        _par[0] = t % 2
        sch.tiny_mode = True
        sch.wait('vector', 'ss', mom3_ss[t])
        sch.op('vector', lambda e: e.tensor_scalar(out=scol('m3', 4), in0=scol('sx3', 4),
                                                   scalar1=1.0 / D, scalar2=None, op0=AL.mult))
        sch.op('vector', lambda e: e.tensor_tensor(out=scol('rc2'), in0=scol('m3'),
                                                   in1=scol('m3'), op=AL.mult))
        sch.op('vector', lambda e: e.tensor_tensor(out=scol('var3'), in0=scol('msq3'),
                                                   in1=scol('rc2'), op=AL.subtract))
        sch.op('vector', lambda e: e.tensor_scalar(out=scol('w3'), in0=scol('var3'),
                                                   scalar1=EPS_LN, scalar2=None, op0=AL.add))
        sch.op('vector', lambda e: e.reciprocal(out=scol('rc1'), in_=scol('w3')), inc='sv')
        sch.wait('scalar', 'sv', sch.v('sv'))
        sch.op('scalar', lambda e: e.activation(out=scol('s3'), in_=scol('rc1'), func=AF.Sqrt),
               inc='ss')
        sch.wait('vector', 'ss', sch.v('ss'))
        sch.op('vector', lambda e: e.tensor_tensor(out=scol('rc1'), in0=scol('mx'),
                                                   in1=scol('m3'), op=AL.subtract))
        sch.op('vector', lambda e: e.tensor_tensor(out=scol('rc2'), in0=scol('m3'),
                                                   in1=scol('mn'), op=AL.subtract))
        sch.op('vector', lambda e: e.tensor_tensor(out=scol('nr1'), in0=scol('rc1'),
                                                   in1=scol('rc2'), op=AL.max))
        sch.op('vector', lambda e: e.tensor_tensor(out=scol('nr2'), in0=scol('nr1'),
                                                   in1=scol('s3'), op=AL.mult))
        sch.op('vector', lambda e: e.tensor_scalar(out=scol('az'), in0=scol('nr2'),
                                                   scalar1=1e-12, scalar2=None, op0=AL.max))
        sch.op('vector', lambda e: e.tensor_scalar(out=scol('ls'), in0=scol('az'),
                                                   scalar1=1.0 / 127.0, scalar2=None,
                                                   op0=AL.mult))
        sch.op('vector', lambda e: e.reciprocal(out=scol('rc1'), in_=scol('az')))
        sch.op('vector', lambda e: e.tensor_scalar(out=scol('rc1'), in0=scol('rc1'),
                                                   scalar1=127.0, scalar2=None, op0=AL.mult))
        sch.op('vector', lambda e: e.tensor_tensor(out=scol('c3'), in0=scol('rc1'),
                                                   in1=scol('s3'), op=AL.mult))
        sch.tiny_mode = False

    def emit_quant(t):
        _par[0] = t % 2
        sch.wait('vector', 'st', n2T_st.get(t + 1, 0))
        sch.wait('vector', 'ss', mom2_ss.get(t + 1, 0))
        for sl in range(SL):
            sch.op('vector', lambda e, sl=sl: e.tensor_scalar(
                out=tmA[:, sl, :], in0=x2_t[:, sl, :], scalar1=scol1('m3', sl),
                scalar2=scol1('c3', sl), op0=AL.subtract, op1=AL.mult))
            sch.op('vector', lambda e, sl=sl: e.tensor_scalar(
                out=tmA[:, sl, :], in0=tmA[:, sl, :], scalar1=127.0, scalar2=-127.0,
                op0=AL.min, op1=AL.max))
            sch.op('vector', lambda e, sl=sl: e.tensor_scalar(
                out=tmA[:, sl, :], in0=tmA[:, sl, :], scalar1=MAGIC, scalar2=MAGIC,
                op0=AL.add, op1=AL.subtract))
            sch.op('vector', lambda e, sl=sl: e.tensor_copy(out=qi_t[:], in_=tmA[:, sl, :]))
            sch.op('vector', lambda e, sl=sl: e.tensor_copy(out=q_tm[:, sl, :], in_=qi_t[:]),
                   inc='sv' if sl == SL - 1 else None)
        quant_sv[t] = sch.v('sv')

    def emit_qT(t):
        qe = {}
        sch.wait('tensor', 'sv', quant_sv[t])
        sch.wait('tensor', 'ss', silu_ss[t][HC - 1])
        for dp in range(DC // 2):
            bank = pb[2 + dp % 2]
            if dp >= 2:
                sch.wait('tensor', 'ss', qe[2 * (dp - 2)])
            for sub in range(2):
                dc = 2 * dp + sub
                for sl in range(SL):
                    sch.op('tensor', lambda e, b=bank, dc=dc, sub=sub, sl=sl: e.transpose(
                        b[:, sub * TILE + sl * 128:sub * TILE + (sl + 1) * 128],
                        q_tm[:, sl, dc * 128:(dc + 1) * 128], idf[:]),
                        inc='st' if sl == SL - 1 else None)
            stv = sch.v('st')
            sch.wait('scalar', 'st', stv)
            sch.op('scalar', lambda e, b=bank, dp=dp: e.activation(
                out=q_fm[:, 2 * dp:2 * dp + 2, :].rearrange('p a b -> p (a b)'),
                in_=b[:, 0:2 * TILE], func=AF.Copy), inc='ss')
            v = sch.v('ss')
            qe[2 * dp] = v
            qe[2 * dp + 1] = v
        qT_st[t] = sch.v('st')
        qe_ss[t] = qe

    def emit_M3out(t):
        _par[0] = t % 2
        yb = ybuf[t % 2]
        sch.wait('tensor', 'sv', out_done.get(t - 1, 0))
        sch.wait('tensor', 'sv', y1_done.get(t + 1, 0))
        for sl in range(SL):
            for dc in range(DC):
                if sl == 0:
                    sch.wait('tensor', 'ss', qe_ss[t][dc])
                for hf in range(2):
                    b = pb[6 + hf]
                    sch.op('tensor', lambda e, b=b, dc=dc, hf=hf, sl=sl: e.matmul(
                        b[:], q_fm[:, dc, sl * 128:(sl + 1) * 128],
                        qrhs[:, dc, hf * 512:(hf + 1) * 512],
                        start=(dc == 0), stop=(dc == DC - 1)),
                        inc='st' if (dc == DC - 1 and hf == 1) else None)
            stv = sch.v('st')
            sch.wait('vector', 'st', stv)
            for hf in range(2):
                b = pb[6 + hf]
                sch.op('vector', lambda e, b=b, sl=sl, hf=hf: e.scalar_tensor_tensor(
                    out=n2_tm[:, sl, hf * 512:(hf + 1) * 512], in0=b[:],
                    scalar=scol1('ls', sl), in1=rs_repl[:, hf * 512:(hf + 1) * 512],
                    op0=AL.mult, op1=AL.mult))
                sch.op('vector', lambda e, sl=sl, hf=hf, yb=yb: e.tensor_tensor(
                    out=yb[:, sl, hf * 512:(hf + 1) * 512],
                    in0=n2_tm[:, sl, hf * 512:(hf + 1) * 512],
                    in1=x2_t[:, sl, hf * 512:(hf + 1) * 512], op=AL.add),
                    inc='sv' if hf == 1 else None)
            out_sl.setdefault(t, {})[sl] = sch.v('sv')
            if sl == 0:
                sch.wait('tensor', 'sv', sch.v('sv'))
        out_done[t] = sch.v('sv')

    def emit_outdma(t):
        yb = ybuf[t % 2]
        for sl in range(SL):
            sch.wait('sync', 'sv', out_sl[t][sl])
            rows = out_d[t * TILE + sl * 128:t * TILE + (sl + 1) * 128, :]
            sch.op('sync', lambda e, r=rows, yb=yb, sl=sl: e.dma_start(
                out=r, in_=yb[:, sl, :]), inc='so', inc_n=16)

    def emit_prefetch(t):
        sch.wait('sync', 'sv', y1_done[t - 2])
        sch.op('sync', lambda e, t=t: e.dma_start(out=xbuf[t % 2][:], in_=x_ap(t)),
               inc='sx', inc_n=16)
        x_sx[t] = sch.v('sx')

    n_iters = nt + 2 if 'tiles' not in DEBUG_SKIP else 0
    if n_iters:
        emit_mom1(0)
        emit_tiny1(0)
        emit_xau(0)
        if nt > 1:
            emit_mom1(1)
    for it in range(n_iters):
        pre = it if it < nt else None
        mid = it - 1 if 1 <= it <= nt else None
        fin = it - 2 if it >= 2 else None
        if pre is not None:
            emit_scanblock(pre)
        if fin is not None:
            emit_qT(fin)
        if pre is not None:
            emit_sostt(pre)
        if fin is not None:
            emit_M3out(fin)
            emit_outdma(fin)
        if pre is not None and pre + 1 < nt:
            emit_tiny1(pre + 1)
        if pre is not None:
            emit_soT(pre)
        if pre is not None and pre + 1 < nt:
            emit_xau(pre + 1)
        if pre is not None:
            emit_x1y1(pre)
        if mid is not None:
            emit_M1(mid, pre)
        elif pre is not None:
            emit_mom2(pre)
        if pre is not None:
            emit_tiny2(pre)
            emit_n2(pre)
        if mid is not None:
            emit_M2x2(mid)
            emit_tiny3(mid)
        if pre is not None:
            emit_n2T(pre)
        if mid is not None:
            emit_quant(mid)
        if pre is not None:
            if pre + 2 < nt:
                emit_prefetch(pre + 2)
                emit_mom1(pre + 2)
    if DEBUG_DUMP:
        dbg = {'stats': stats, 'carry0': carry0, 'a_v': a_v, 'tmA': tmA,
               'xa_t': xa_t, 'ud_fm': ud_fm, 'h0': hbuf[0], 'so_fm': so_fm,
               'y0': ybuf[0], 'x2_t': x2_t, 'n2_tm': n2_tm, 'n2_fm': n2_fm,
               'hid_fm': hid_fm, 'q_fm': q_fm, 'qrhs': qrhs, 'rs_repl': rs_repl,
               'fcT': fcT, 'projT': projT, 'xb0': xbuf[0]}
        for s in ('sv', 'ss', 'st', 'sg'):
            sch.wait('sync', s, sch.v(s))
        for i, nm in enumerate(DEBUG_DUMP):
            t = dbg[nm]
            ncols = 1
            for d_ in t.shape[1:]:
                ncols *= d_
            ncols = min(ncols, D)
            ap = t[:].rearrange('p a b -> p (a b)') if len(t.shape) == 3 else t[:]
            sch.op('sync', lambda e, i=i, ap=ap, ncols=ncols: e.dma_start(
                out=out_d[i * 128:(i + 1) * 128, 0:ncols], in_=ap[:, 0:ncols]),
                inc='so', inc_n=16)

    sch.wait('sync', 'so', sch.v('so'))

    emit_block(mark, len(sch.items))
    for g in sem_ctxs:
        g.__exit__(None, None, None)
    for g in reversed(ctxs):
        g.__exit__(None, None, None)
    nc.finalize()
    return nc


def _vec_layout(v):
    return np.ascontiguousarray(v.reshape(DC, 128).T.astype(np.float32))


def kernel(L0_attn_x, L0_attn_input_norm_w, L0_attn_a_log, L0_attn_c_out,
           L0_attn_d_skip, L0_attn_ff_norm_w, L0_attn_ff_fc_w, L0_attn_ff_proj_w,
           L0_ffn_rhs):
    B, T, _ = L0_attn_x.shape
    half = T // 2
    x = np.ascontiguousarray(L0_attn_x.astype(np.float32, copy=False))

    fcT_h = np.ascontiguousarray(L0_attn_ff_fc_w.T.astype(np.float32, copy=False))
    projT_h = np.ascontiguousarray(L0_attn_ff_proj_w.T.astype(np.float32, copy=False))
    rhsT_h = np.ascontiguousarray(L0_ffn_rhs.T.astype(np.float32, copy=False))
    shared = {
        'fcT_h': fcT_h, 'projT_h': projT_h, 'rhsT_h': rhsT_h,
        'inw_h': _vec_layout(L0_attn_input_norm_w),
        'alog_h': _vec_layout(L0_attn_a_log),
        'cout_h': _vec_layout(L0_attn_c_out),
        'dskip_h': _vec_layout(L0_attn_d_skip),
        'ffw_h': _vec_layout(L0_attn_ff_norm_w),
        'idf_h': np.eye(128, dtype=np.float32),
        'idb_h': np.eye(128, dtype=ml_dtypes.bfloat16),
        'ones_h': np.ones((1, 128), np.float32),
        'iota_h': np.arange(PREV_T, dtype=np.float32)[None, :],
    }
    zeros_prev = np.zeros((PREV_T, D), np.float32)
    in_maps = []
    for c in range(N_CORES):
        b, hf = c // 2, c % 2
        xc_c = np.ascontiguousarray(x[b, hf * half:(hf + 1) * half])
        xp_c = zeros_prev if hf == 0 else np.ascontiguousarray(
            x[b, half - PREV_T:half])
        in_maps.append({'xc': xc_c, 'xp': xp_c, **shared})

    nc = build_program(T_CORE, PREV_T)
    res = run_bass_kernel_spmd(nc, in_maps, core_ids=list(range(N_CORES)))

    out = np.empty((B, T, D), np.float32)
    for c in range(N_CORES):
        b, hf = c // 2, c % 2
        out[b, hf * half:(hf + 1) * half] = res.results[c]['out']
    return out


# version 23
# speedup vs baseline: 1.0266x; 1.0266x over previous
import numpy as np
import ml_dtypes
import concourse.bass as bass
import concourse.mybir as mybir
import concourse.bass_utils as _bu
from concourse.bass_utils import run_bass_kernel_spmd

if not getattr(_bu, '_ldw_opt_patched', False):
    _orig_get_walrus_args = _bu.get_walrus_args

    def _get_walrus_args_ldw(arch, tmpdir, **kw):
        return _orig_get_walrus_args(arch, tmpdir, **kw) + ["--enable-ldw-opt=true"]

    _bu.get_walrus_args = _get_walrus_args_ldw
    _bu._ldw_opt_patched = True

F32 = mybir.dt.float32
BF16 = mybir.dt.bfloat16
I32 = mybir.dt.int32
AL = mybir.AluOpType
AX = mybir.AxisListType
AF = mybir.ActivationFunctionType

D = 1024
H = 2048
DC = 8
HC = 16
MAGIC = 12582912.0
EPS_LN = 1e-5
EPS_RMS = 1e-6
N_CORES = 8
T_CORE = 4096
TILE = 256
SL = TILE // 128
PREV_T = 1024
DEBUG_SKIP = set()
TILE_STOP = 99
DEBUG_DUMP = []


class Sched:

    def __init__(self, sem_names):
        self.items = []
        self.cnt = {s: 0 for s in sem_names}
        self.tiny_mode = False
        self.tiny_cnt = 0

    def op(self, eng, fn, inc=None, inc_n=1):
        if inc is not None:
            self.cnt[inc] += inc_n
        par = self.par_ref[0] if getattr(self, 'par_ref', None) is not None else 0
        tiny = self.tiny_mode and eng == 'vector'
        if tiny:
            self.tiny_cnt += 1
        self.items.append((eng, 'op', (fn, inc, inc_n, par, tiny, self.tiny_cnt)))

    def wait(self, eng, sem, val):
        self.items.append((eng, 'wait', (sem, val)))

    def v(self, sem):
        return self.cnt[sem]


def build_program(t_core=T_CORE, prev_t=PREV_T):
    nt = t_core // TILE
    n_prev = prev_t // TILE

    nc = bass.Bass('TRN2', target_bir_lowering=False, detect_race_conditions=False)

    xc = nc.dram_tensor('xc', [t_core, D], F32, kind='ExternalInput').ap()
    xp = nc.dram_tensor('xp', [prev_t, D], F32, kind='ExternalInput').ap()
    fcT_h = nc.dram_tensor('fcT_h', [D, H], F32, kind='ExternalInput').ap()
    projT_h = nc.dram_tensor('projT_h', [H, D], F32, kind='ExternalInput').ap()
    rhsT_h = nc.dram_tensor('rhsT_h', [D, D], F32, kind='ExternalInput').ap()
    inw_h = nc.dram_tensor('inw_h', [128, DC], F32, kind='ExternalInput').ap()
    alog_h = nc.dram_tensor('alog_h', [128, DC], F32, kind='ExternalInput').ap()
    cout_h = nc.dram_tensor('cout_h', [128, DC], F32, kind='ExternalInput').ap()
    dskip_h = nc.dram_tensor('dskip_h', [128, DC], F32, kind='ExternalInput').ap()
    ffw_h = nc.dram_tensor('ffw_h', [128, DC], F32, kind='ExternalInput').ap()
    idf_h = nc.dram_tensor('idf_h', [128, 128], F32, kind='ExternalInput').ap()
    idb_h = nc.dram_tensor('idb_h', [128, 128], BF16, kind='ExternalInput').ap()
    ones_h = nc.dram_tensor('ones_h', [1, 128], F32, kind='ExternalInput').ap()
    iota_h = nc.dram_tensor('iota_h', [1, prev_t], F32, kind='ExternalInput').ap()
    out_d = nc.dram_tensor('out', [t_core, D], F32, kind='ExternalOutput').ap()

    sems = ['sx', 'so', 'sg', 'sv', 'ss', 'st', 'svd']
    sch = Sched(sems)

    ctxs = []
    pro_ctxs = []

    def sb(name, shape, dt=F32, scope=None):
        g = nc.sbuf_tensor(name, shape, dt)
        t = g.__enter__()
        (ctxs if scope is None else scope).append(g)
        return t

    def psum(name, shape):
        g = nc.psum_tensor(name, shape, F32)
        t = g.__enter__()
        ctxs.append(g)
        return t

    fcT = sb('fcT', [128, DC, H], BF16)
    projT = sb('projT', [128, HC, D], BF16)
    qrhs = sb('qrhs', [128, DC, D], BF16)
    rs_repl = sb('rs_repl', [128, D], F32)
    idf = sb('idf', [128, 128], F32)
    idb = sb('idb', [128, 128], BF16)
    ones_c = sb('ones_c', [1, 128], F32)
    iota_r = sb('iota_r', [1, prev_t], F32)
    inw_v = sb('inw_v', [128, DC], F32)
    alog_v = sb('alog_v', [128, DC], F32)
    cout_v = sb('cout_v', [128, DC], F32)
    dskip_v = sb('dskip_v', [128, DC], F32)
    ffw_v = sb('ffw_v', [128, DC], F32)
    e_v = sb('e_v', [128, DC], F32)
    a_v = sb('a_v', [128, DC], F32)
    cw_v = sb('cw_v', [128, DC], F32)
    dw_v = sb('dw_v', [128, DC], F32)
    carry0 = sb('carry0', [128, DC], F32)
    carryB = sb('carryB', [128, DC], F32)
    rs_e = sb('rs_e', [128, DC], F32)
    rrs_e = sb('rrs_e', [128, DC], F32)

    pb = [psum('pb%d' % i, [128, 512]) for i in range(8)]

    xbuf = [sb('xb%d' % i, [128, SL, D], F32) for i in range(2)]
    qi_t = sb('qi_t', [128, D], I32)
    stats = sb('stats', [128, 128], F32)
    trash_s = sb('trash_s', [128, D], F32, scope=pro_ctxs)
    wscr = [sb('wscr%d' % i, [128, H], F32, scope=pro_ctxs) for i in range(2)]
    powt = sb('powt', [128, DC, prev_t], F32, scope=pro_ctxs)
    xpb = [sb('xpb%d' % i, [128, SL, D], F32, scope=pro_ctxs) for i in range(2)]
    rstg = [sb('rstg%d' % i, [128, D], F32, scope=pro_ctxs) for i in range(2)]
    upb = sb('upb', [128, SL, D], F32, scope=pro_ctxs)
    trash_f = sb('trash_f', [128, TILE], F32, scope=pro_ctxs)
    qrt = sb('qrt', [128, D], F32, scope=pro_ctxs)
    rowscr = sb('rowscr', [1, 128], F32, scope=pro_ctxs)

    ST = stats

    COL = {}
    _next = [0]
    _par = [0]
    sch.par_ref = _par

    def scol(name, n=SL):
        if name not in COL:
            COL[name] = _next[0]
            _next[0] += n
        c = COL[name] + 64 * _par[0]
        return ST[:, c:c + n]

    def scol1(name, sl):
        c = COL[name] + 64 * _par[0]
        return ST[:, c + sl:c + sl + 1]

    for nmm in ['sx1', 'sq1', 'm1', 'msq1', 'var1', 'w1', 'wc', 's1', 'cc',
                'sq2', 'msq2', 'w2', 'r2', 'sx3', 'sq3', 'm3', 'msq3', 'var3',
                'w3', 's3', 'mx', 'mn', 'az', 'ls', 'c3', 'nr1', 'nr2', 'rc1', 'rc2']:
        scol(nmm)

    def nr_polish(eng_ops, yname, wname, iters=2):
        assert iters % 2 == 0
        cols = [yname, 'nr2']
        for it in range(iters):
            t1, t2 = scol('nr1'), scol('nr1')
            ya, yb_ = scol(cols[it % 2]), scol(cols[(it + 1) % 2])
            w = scol(wname)
            sch.op('vector', lambda e, a=ya: e.tensor_tensor(out=scol('nr1'), in0=a, in1=a, op=AL.mult))
            sch.op('vector', lambda e, w=w: e.tensor_tensor(out=scol('rc1'), in0=scol('nr1'), in1=w, op=AL.mult))
            sch.op('vector', lambda e: e.tensor_scalar(out=scol('nr1'), in0=scol('rc1'), scalar1=-0.5,
                                                       scalar2=1.5, op0=AL.mult, op1=AL.add))
            sch.op('vector', lambda e, a=ya, b=yb_: e.tensor_tensor(out=b, in0=a, in1=scol('nr1'), op=AL.mult))

    def rsqrt_seq(wname, yname, inc_tag=None):
        w = scol(wname)
        y = scol(yname)
        rec = scol('nr2')
        sch.op('vector', lambda e: e.reciprocal(out=rec, in_=w), inc='sv')
        sch.wait('scalar', 'sv', sch.v('sv'))
        sch.op('scalar', lambda e: e.activation(out=y, in_=rec, func=AF.Sqrt), inc='ss')
        sch.wait('vector', 'ss', sch.v('ss'))

    def x_ap(j):
        return xc[j * TILE:(j + 1) * TILE, :].rearrange('(s p) d -> p s d', p=128)

    def o_ap(j):
        return out_d[j * TILE:(j + 1) * TILE, :].rearrange('(s p) d -> p s d', p=128)

    def xp_ap(j):
        return xp[j * TILE:(j + 1) * TILE, :].rearrange('(s p) d -> p s d', p=128)

    for dst, src in ((idf, idf_h), (idb, idb_h), (ones_c, ones_h), (iota_r, iota_h),
                     (inw_v, inw_h), (alog_v, alog_h), (cout_v, cout_h),
                     (dskip_v, dskip_h), (ffw_v, ffw_h)):
        sch.op('sync', lambda e, d=dst, s=src: e.dma_start(out=d[:], in_=s), inc='sx', inc_n=16)
    sx_consts = sch.v('sx')
    x_sx = {}
    sch.op('sync', lambda e: e.dma_start(out=xbuf[0][:], in_=x_ap(0)), inc='sx', inc_n=16)
    x_sx[0] = sch.v('sx')
    if nt > 1:
        sch.op('sync', lambda e: e.dma_start(out=xbuf[1][:], in_=x_ap(1)), inc='sx', inc_n=16)
        x_sx[1] = sch.v('sx')

    for hc in range(HC if 'projT' not in DEBUG_SKIP else 0):
        ap_in = projT_h[hc * 128:(hc + 1) * 128, :]
        sch.op('gpsimd', lambda e, a=ap_in, hc=hc: e.dma_start(
            out=projT[:, hc, :], in_=a), inc='sg', inc_n=16)
    sg_proj = sch.v('sg')

    sch.wait('scalar', 'sx', sx_consts)
    sch.op('scalar', lambda e: e.activation(out=e_v[:], in_=alog_v[:], func=AF.Exp))
    sch.op('scalar', lambda e: e.activation(out=a_v[:], in_=e_v[:], func=AF.Exp,
                                            scale=-1.0), inc='ss')
    ss_av = sch.v('ss')
    sch.wait('vector', 'sx', sx_consts)
    sch.tiny_mode = True
    sch.op('vector', lambda e: e.tensor_tensor(out=cw_v[:], in0=cout_v[:], in1=inw_v[:], op=AL.mult))
    sch.op('vector', lambda e: e.tensor_tensor(out=dw_v[:], in0=dskip_v[:], in1=inw_v[:], op=AL.mult))
    sch.op('vector', lambda e: e.memset(carry0[:], 0.0), inc='sv')
    sch.tiny_mode = False
    sv_vecs = sch.v('sv')

    n_half = (prev_t // 512) if 'powtile' not in DEBUG_SKIP else 0
    for hf in range(n_half):
        bank = pb[6 + (hf % 2)]
        sch.wait('tensor', 'sx', sx_consts)
        if hf >= 2:
            sch.wait('tensor', 'sv', sv_vecs + (hf - 1) * DC)
        sch.op('tensor', lambda e, b=bank, hf=hf: e.matmul(
            b[:], ones_c[:], iota_r[0:1, hf * 512:(hf + 1) * 512],
            start=True, stop=True), inc='st')
        st_iota = sch.v('st')
        sch.wait('vector', 'st', st_iota)
        sch.wait('vector', 'ss', ss_av)
        for dc in range(DC):
            if hf > 0 or dc >= 2:
                sch.wait('vector', 'ss', sch.v('ss'))
            sch.op('vector', lambda e, b=bank, dc=dc, hf=hf: e.tensor_scalar(
                out=wscr[0][:, (dc % 2) * 512:(dc % 2) * 512 + 512], in0=b[:],
                scalar1=float(prev_t - 1), scalar2=e_v[:, dc:dc + 1],
                op0=AL.subtract, op1=AL.mult), inc='sv')
            sch.wait('scalar', 'sv', sch.v('sv'))
            sch.op('scalar', lambda e, dc=dc, hf=hf: e.activation(
                out=powt[:, dc, hf * 512:(hf + 1) * 512],
                in_=wscr[0][:, (dc % 2) * 512:(dc % 2) * 512 + 512],
                func=AF.Exp), inc='ss')
    ss_pow = sch.v('ss')

    for dc in range(DC if 'fcT' not in DEBUG_SKIP else 0):
        ap_in = fcT_h[dc * 128:(dc + 1) * 128, :]
        sch.op('gpsimd', lambda e, a=ap_in, dc=dc: e.dma_start(
            out=fcT[:, dc, :], in_=a), inc='sg', inc_n=16)

    rstate = {'sv_q': sch.v('sv'), 'ev': []}

    def emit_rhsT_ec(ec):
        buf = rstg[ec % 2]
        if ec >= 2:
            sch.wait('sync', 'sv', rstate['sv_q'])
        ap_in = rhsT_h[ec * 128:(ec + 1) * 128, :]
        sch.op('sync', lambda e, a=ap_in, b=buf: e.dma_start(out=b[:], in_=a),
               inc='sx', inc_n=16)
        sxv = sch.v('sx')
        sch.wait('vector', 'sx', sxv)
        sch.tiny_mode = True
        sch.op('vector', lambda e, b=buf, ec=ec: e.reduce_max(
            out=rs_e[:, ec:ec + 1], in_=b[:, 0:D], axis=AX.X, apply_absolute_value=True))
        sch.op('vector', lambda e, ec=ec: e.tensor_scalar(
            out=rs_e[:, ec:ec + 1], in0=rs_e[:, ec:ec + 1], scalar1=1e-12,
            scalar2=1.0 / 127.0, op0=AL.max, op1=AL.mult))
        sch.op('vector', lambda e, ec=ec: e.reciprocal(
            out=rrs_e[:, ec:ec + 1], in_=rs_e[:, ec:ec + 1]))
        sch.tiny_mode = False
        sch.op('vector', lambda e, b=buf, ec=ec: e.tensor_scalar(
            out=b[:, 0:D], in0=b[:, 0:D], scalar1=rrs_e[:, ec:ec + 1], scalar2=None,
            op0=AL.mult))
        sch.op('vector', lambda e, b=buf: e.tensor_scalar(
            out=b[:, 0:D], in0=b[:, 0:D], scalar1=127.0, scalar2=-127.0,
            op0=AL.min, op1=AL.max))
        sch.op('vector', lambda e, b=buf: e.tensor_scalar(
            out=b[:, 0:D], in0=b[:, 0:D], scalar1=MAGIC, scalar2=MAGIC,
            op0=AL.add, op1=AL.subtract))
        sch.op('vector', lambda e, b=buf: e.tensor_copy(out=qi_t[:], in_=b[:, 0:D]))
        sch.op('vector', lambda e: e.tensor_copy(out=qrt[:], in_=qi_t[:]), inc='sv')
        sv_q = sch.v('sv')
        for dc in range(DC):
            bank = pb[dc % 2]
            i = ec * DC + dc
            if i >= 2:
                sch.wait('tensor', 'ss', rstate['ev'][i - 2])
            sch.wait('tensor', 'sv', sv_q)
            sch.op('tensor', lambda e, b=bank, dc=dc: e.transpose(
                b[:, 0:128], qrt[:, dc * 128:(dc + 1) * 128], idf[:]), inc='st')
            stv = sch.v('st')
            sch.wait('scalar', 'st', stv)
            sch.op('scalar', lambda e, b=bank, dc=dc, ec=ec: e.activation(
                out=qrhs[:, dc, ec * 128:(ec + 1) * 128], in_=b[:, 0:128],
                func=AF.Copy), inc='ss')
            rstate['ev'].append(sch.v('ss'))
        sch.wait('tensor', 'ss', rstate['ev'][-1])
        sch.op('tensor', lambda e, ec=ec: e.transpose(
            pb[2][0:1, 0:128], rs_e[:, ec:ec + 1], idf[:]), inc='st')
        stv = sch.v('st')
        sch.wait('vector', 'st', stv)
        sch.op('vector', lambda e: e.tensor_copy(out=rowscr[:], in_=pb[2][0:1, 0:128]),
               inc='sv')
        svv = sch.v('sv')
        sch.wait('tensor', 'sv', svv)
        sch.op('tensor', lambda e: e.matmul(pb[3][:, 0:128], ones_c[:], rowscr[:],
                                            start=True, stop=True), inc='st')
        stv = sch.v('st')
        sch.wait('vector', 'st', stv)
        sch.op('vector', lambda e, ec=ec: e.tensor_copy(
            out=rs_repl[:, ec * 128:(ec + 1) * 128], in_=pb[3][:, 0:128]), inc='sv')
        rstate['sv_q'] = sch.v('sv')

    cstate = {'cur': carry0, 'alt': carryB}

    def emit_carry_mt(mt):
        carry_cur, carry_alt = cstate['cur'], cstate['alt']
        xpc = xpb[mt % 2]
        sch.wait('scalar', 'sx', xp_sx[mt])
        for sl in range(SL):
            sch.op('scalar', lambda e, sl=sl, xpc=xpc: e.activation(
                out=trash_s[:], in_=xpc[:, sl, :], func=AF.Copy,
                accum_out=scol1('sx1', sl)))
            sch.op('scalar', lambda e, sl=sl, xpc=xpc: e.activation(
                out=trash_s[:], in_=xpc[:, sl, :], func=AF.Square,
                accum_out=scol1('sq1', sl)), inc='ss' if sl == SL - 1 else None)
        ssv = sch.v('ss')
        sch.wait('vector', 'ss', ssv)
        sch.tiny_mode = True
        sch.op('vector', lambda e: e.tensor_scalar(out=scol('m1', 4), in0=scol('sx1', 4),
                                                   scalar1=1.0 / D, scalar2=None, op0=AL.mult))
        sch.op('vector', lambda e: e.tensor_tensor(out=scol('rc1'), in0=scol('m1'),
                                                   in1=scol('m1'), op=AL.mult))
        sch.op('vector', lambda e: e.tensor_tensor(out=scol('var1'), in0=scol('msq1'),
                                                   in1=scol('rc1'), op=AL.subtract))
        sch.op('vector', lambda e: e.tensor_scalar(out=scol('wc'), in0=scol('var1'),
                                                   scalar1=1.0 + EPS_RMS,
                                                   scalar2=EPS_RMS * EPS_LN,
                                                   op0=AL.mult, op1=AL.add))
        rsqrt_seq('wc', 'cc')
        sch.tiny_mode = False
        for sl in range(SL):
            sch.op('vector', lambda e, sl=sl, xpc=xpc: e.tensor_scalar(
                out=upb[:, sl, :], in0=xpc[:, sl, :], scalar1=scol1('m1', sl),
                scalar2=scol1('cc', sl), op0=AL.subtract, op1=AL.mult),
                inc='sv' if sl == SL - 1 else None)
        sv_u = sch.v('sv')
        for dc in range(DC):
            bank, reg = pb[4 + dc % 4], 0
            if dc >= 4:
                sch.wait('tensor', 'sv', sch.v('sv'))
            sch.wait('tensor', 'sv', sv_u)
            for sl in range(SL):
                sch.op('tensor', lambda e, b=bank, r=reg, dc=dc, sl=sl: e.transpose(
                    b[:, r + sl * 128:r + (sl + 1) * 128],
                    upb[:, sl, dc * 128:(dc + 1) * 128], idf[:]),
                    inc='st' if sl == SL - 1 else None)
            stv = sch.v('st')
            sch.wait('vector', 'st', stv)
            sch.op('vector', lambda e, b=bank, r=reg, dc=dc, mt=mt: e.tensor_tensor(
                out=trash_f[:, 0:TILE], in0=b[:, r:r + TILE],
                in1=powt[:, dc, mt * TILE:(mt + 1) * TILE], op=AL.mult))
            sch.tiny_mode = True
            sch.op('vector', lambda e: e.reduce_sum(
                out=scol('nr2')[:, 0:1], in_=trash_f[:, 0:TILE], axis=AX.X))
            sch.op('vector', lambda e, dc=dc, cc_=carry_cur, ca=carry_alt: e.tensor_tensor(
                out=ca[:, dc:dc + 1], in0=cc_[:, dc:dc + 1],
                in1=scol('nr2')[:, 0:1], op=AL.add), inc='sv')
            sch.tiny_mode = False
        cstate['cur'], cstate['alt'] = carry_alt, carry_cur

    n_mt = n_prev if 'carry' not in DEBUG_SKIP else 0
    n_ec = DC if 'qrhs' not in DEBUG_SKIP else 0
    xp_sx = {}
    for mt in range(min(n_mt, 2)):
        sch.op('sync', lambda e, mt=mt: e.dma_start(out=xpb[mt % 2][:], in_=xp_ap(mt)),
               inc='sx', inc_n=16)
        xp_sx[mt] = sch.v('sx')
    for k in range(max(n_ec, n_mt)):
        if k < n_ec:
            emit_rhsT_ec(k)
        if k < n_mt:
            emit_carry_mt(k)
            if k + 2 < n_mt:
                sch.wait('sync', 'sv', sch.v('sv'))
                sch.op('sync', lambda e, mt=k + 2: e.dma_start(
                    out=xpb[mt % 2][:], in_=xp_ap(mt)), inc='sx', inc_n=16)
                xp_sx[k + 2] = sch.v('sx')
    if cstate['cur'] is not carry0 and n_mt:
        sch.tiny_mode = True
        sch.op('vector', lambda e: e.tensor_copy(out=carry0[:], in_=carryB[:]), inc='sv')
        sch.tiny_mode = False
    sv_carry = sch.v('sv')

    PRO = dict(sv=sch.v('sv'), ss=sch.v('ss'), st=sch.v('st'),
               sx=sch.v('sx'), sg=sch.v('sg'))
    for eng in ('sync', 'vector', 'scalar', 'tensor'):
        for s in ('sv', 'ss', 'st', 'sg'):
            sch.wait(eng, s, PRO[s])

    mark = len(sch.items)
    sem_handles = {}
    sem_ctxs = []
    for s in sems:
        g = nc.semaphore(s)
        sem_handles[s] = g.__enter__()
        sem_ctxs.append(g)

    def make_engine_fn(name, lo, hi):
        def fn(eng):
            for (e, kind, payload) in sch.items[lo:hi]:
                if e != name:
                    continue
                if kind == 'wait':
                    s, v = payload
                    if v > 0:
                        eng.wait_ge(sem_handles[s], v)
                else:
                    f, inc, inc_n, par, tiny, tc = payload
                    _par[0] = par
                    if name == 'vector' and tc > 0:
                        eng.wait_ge(sem_handles['svd'], tc - (1 if tiny else 0))
                    ins = f(eng)
                    if inc is not None:
                        ins.then_inc(sem_handles[inc], inc_n)
                        if tiny:
                            eng.engine_nop().then_inc(sem_handles['svd'], 1)
                    elif tiny:
                        ins.then_inc(sem_handles['svd'], 1)
        return fn

    def emit_block(lo, hi):
        g = nc.Block()
        blk = g.__enter__()
        blk.sync(make_engine_fn('sync', lo, hi))
        blk.vector(make_engine_fn('vector', lo, hi))
        blk.scalar(make_engine_fn('scalar', lo, hi))
        blk.tensor(make_engine_fn('tensor', lo, hi))
        blk.gpsimd(make_engine_fn('gpsimd', lo, hi))
        g.__exit__(None, None, None)

    emit_block(0, mark)
    for g in reversed(pro_ctxs):
        g.__exit__(None, None, None)

    tmA = sb('tmA', [128, SL, D], F32)
    u_t = sb('u_t', [128, SL, D], F32)
    ud_fm = sb('ud_fm', [128, DC, TILE], F32)
    trash_d = ud_fm[:].rearrange('p a b -> p (a b)')
    hbuf = [sb('hb%d' % i, [128, DC, TILE], F32) for i in range(2)]
    so_fm = sb('so_fm', [128, DC, TILE], F32)
    ybuf = [sb('yb%d' % i, [128, SL, D], F32) for i in range(2)]
    x2_t = sb('x2_t', [128, SL, D], F32)
    n2_tm = sb('n2_tm', [128, SL, D], F32)
    n2_fm = sb('n2_fm', [128, DC, TILE], BF16)
    hid_fm = sb('hid_fm', [128, HC, TILE], BF16)
    q_tm = n2_tm
    q_fm = sb('q_fm', [128, DC, TILE], BF16)

    y1_done = {}
    out_done = {}
    x2_sv = {}
    quant_sv = {}
    sostt_sv = {}
    scan_sv = {}
    xau_sv = {}
    mom1_ss = {}
    mom2_ss = {}
    mom3_ss = {}
    s1cc_ss = {}
    r2_ss = {}
    n2e_ss = {}
    ud_ss = {}
    silu_ss = {}
    qe_ss = {}
    sv_n2 = {}
    n2T_st = {}
    soT_st = {}
    qT_st = {}
    uT_st = {}
    out_sl = {}

    def emit_mom1(t):
        _par[0] = t % 2
        xs = xbuf[t % 2]
        sch.wait('scalar', 'sx', x_sx[t])
        for sl in range(SL):
            sch.op('scalar', lambda e, sl=sl, xs=xs: e.activation(
                out=trash_d[:, 0:D], in_=xs[:, sl, :], func=AF.Copy, accum_out=scol1('sx1', sl)))
            sch.op('scalar', lambda e, sl=sl, xs=xs: e.activation(
                out=trash_d[:, 0:D], in_=xs[:, sl, :], func=AF.Square, accum_out=scol1('sq1', sl)),
                inc='ss' if sl == SL - 1 else None)
        mom1_ss[t] = sch.v('ss')

    def emit_tiny1(t):
        _par[0] = t % 2
        sch.wait('vector', 'ss', mom1_ss[t])
        sch.tiny_mode = True
        sch.op('vector', lambda e: e.tensor_scalar(out=scol('m1', 4), in0=scol('sx1', 4),
                                                   scalar1=1.0 / D, scalar2=None, op0=AL.mult))
        sch.op('vector', lambda e: e.tensor_tensor(out=scol('rc1'), in0=scol('m1'),
                                                   in1=scol('m1'), op=AL.mult))
        sch.op('vector', lambda e: e.tensor_tensor(out=scol('var1'), in0=scol('msq1'),
                                                   in1=scol('rc1'), op=AL.subtract))
        sch.op('vector', lambda e: e.tensor_scalar(out=scol('w1'), in0=scol('var1'),
                                                   scalar1=EPS_LN, scalar2=None, op0=AL.add))
        sch.op('vector', lambda e: e.tensor_scalar(out=scol('wc'), in0=scol('var1'),
                                                   scalar1=1.0 + EPS_RMS,
                                                   scalar2=EPS_RMS * EPS_LN,
                                                   op0=AL.mult, op1=AL.add))
        sch.op('vector', lambda e: e.reciprocal(out=scol('rc1', 4), in_=scol('w1', 4)),
               inc='sv')
        sch.tiny_mode = False
        sch.wait('scalar', 'sv', sch.v('sv'))
        sch.op('scalar', lambda e: e.activation(out=scol('s1', 4), in_=scol('rc1', 4),
                                                func=AF.Sqrt), inc='ss')
        s1cc_ss[t] = sch.v('ss')

    def emit_tiny2(t):
        _par[0] = t % 2
        sch.wait('vector', 'ss', mom2_ss[t])
        sch.tiny_mode = True
        sch.op('vector', lambda e: e.tensor_scalar(out=scol('w2'), in0=scol('sq2'),
                                                   scalar1=1.0 / D, scalar2=EPS_RMS,
                                                   op0=AL.mult, op1=AL.add))
        sch.op('vector', lambda e: e.reciprocal(out=scol('rc1'), in_=scol('w2')), inc='sv')
        sch.tiny_mode = False
        sch.wait('scalar', 'sv', sch.v('sv'))
        sch.op('scalar', lambda e: e.activation(out=scol('r2'), in_=scol('rc1'),
                                                func=AF.Sqrt), inc='ss')
        r2_ss[t] = sch.v('ss')

    def emit_n2(t):
        _par[0] = t % 2
        sch.wait('vector', 'ss', r2_ss[t])
        sch.wait('vector', 'st', qT_st.get(t - 2, 0))
        for sl in range(SL):
            sch.op('vector', lambda e, sl=sl: e.tensor_scalar(
                out=n2_tm[:, sl, :], in0=tmA[:, sl, :], scalar1=scol1('r2', sl),
                scalar2=None, op0=AL.mult), inc='sv' if sl == SL - 1 else None)
        sv_n2[t] = sch.v('sv')

    N2T_BANKS = (4, 5, 0, 1)

    def emit_n2T(t):
        n2e = {}
        for dc in range(DC):
            bank = pb[N2T_BANKS[dc % 4]]
            sch.wait('tensor', 'sv', sv_n2[t])
            if dc < 2:
                sch.wait('tensor', 'sv', x2_sv.get(t - 1, 0))
            elif dc >= 4:
                sch.wait('tensor', 'ss', n2e[dc - 4])
            for sl in range(SL):
                sch.op('tensor', lambda e, b=bank, dc=dc, sl=sl: e.transpose(
                    b[:, sl * 128:(sl + 1) * 128],
                    n2_tm[:, sl, dc * 128:(dc + 1) * 128], idf[:]),
                    inc='st' if sl == SL - 1 else None)
            stv = sch.v('st')
            sch.wait('scalar', 'st', stv)
            sch.op('scalar', lambda e, b=bank, dc=dc: e.activation(
                out=n2_fm[:, dc, :], in_=b[:, 0:TILE], func=AF.Copy,
                scale=ffw_v[:, dc:dc + 1]), inc='ss')
            n2e[dc] = sch.v('ss')
        n2T_st[t] = sch.v('st')
        n2e_ss[t] = n2e

    def emit_xau(t):
        _par[0] = t % 2
        xs = xbuf[t % 2]
        sch.wait('vector', 'ss', s1cc_ss[t])
        sch.wait('vector', 'st', uT_st.get(t - 1, 0))
        for sl in range(SL):
            sch.op('vector', lambda e, sl=sl, xs=xs: e.tensor_scalar(
                out=u_t[:, sl, :], in0=xs[:, sl, :], scalar1=scol1('m1', sl),
                scalar2=scol1('cc', sl), op0=AL.subtract, op1=AL.mult),
                inc='sv' if sl == SL - 1 else None)
        xau_sv[t] = sch.v('sv')

    UT_BANKS = (0, 1, 4, 5)

    def emit_scanblock(t):
        hb = hbuf[t % 2]
        hprev = hbuf[(t + 1) % 2]
        scan_dc = {}
        ud_dc = {}
        for dc in range(DC):
            bank = pb[UT_BANKS[dc % 4]]
            sch.wait('tensor', 'sv', xau_sv[t])
            if dc < 4:
                sch.wait('tensor', 'sv', y1_done.get(t - 1, 0))
                sch.wait('tensor', 'sv', x2_sv.get(t - 1, 0))
                if t >= 1:
                    sch.wait('tensor', 'ss', n2e_ss[t - 1][DC - 1])
            else:
                sch.wait('tensor', 'sv', scan_dc[dc - 4])
                sch.wait('tensor', 'ss', ud_dc[dc - 4])
            for sl in range(SL):
                sch.op('tensor', lambda e, b=bank, dc=dc, sl=sl: e.transpose(
                    b[:, sl * 128:(sl + 1) * 128],
                    u_t[:, sl, dc * 128:(dc + 1) * 128], idf[:]),
                    inc='st' if sl == SL - 1 else None)
            stv = sch.v('st')
            sch.wait('vector', 'st', stv)
            if t == 0:
                init_ap = carry0[:, dc:dc + 1]
            else:
                init_ap = hprev[:, dc, TILE - 1:TILE]
            sch.op('vector', lambda e, b=bank, dc=dc, ia=init_ap, hb=hb: e.tensor_tensor_scan(
                out=hb[:, dc, :], data0=a_v[:, dc:dc + 1].broadcast_to((128, TILE)),
                data1=b[:, 0:TILE], initial=ia, op0=AL.mult, op1=AL.add), inc='sv')
            scan_dc[dc] = sch.v('sv')
            sch.wait('scalar', 'sv', scan_dc[dc])
            sch.op('scalar', lambda e, b=bank, dc=dc: e.activation(
                out=ud_fm[:, dc, :], in_=b[:, 0:TILE], func=AF.Copy,
                scale=dw_v[:, dc:dc + 1]), inc='ss')
            ud_dc[dc] = sch.v('ss')
        scan_sv[t] = scan_dc[DC - 1]
        ud_ss[t] = ud_dc[DC - 1]
        uT_st[t] = sch.v('st')

    def emit_M1(t, mom2_tile=None):
        silu_hc = {}
        sch.wait('tensor', 'ss', n2e_ss[t][DC - 1])
        if t >= 1:
            sch.wait('tensor', 'ss', qe_ss[t - 1][DC - 1])
        for hp in range(HC // 2):
            bank = pb[2 + (hp % 2)]
            if hp >= 2:
                sch.wait('tensor', 'ss', silu_hc[2 * (hp - 2)])
            for sub in range(2):
                hc = 2 * hp + sub
                for dc in range(DC):
                    sch.op('tensor', lambda e, b=bank, hc=hc, sub=sub, dc=dc: e.matmul(
                        b[:, sub * TILE:(sub + 1) * TILE],
                        fcT[:, dc, hc * 128:(hc + 1) * 128],
                        n2_fm[:, dc, :], start=(dc == 0), stop=(dc == DC - 1)),
                        inc='st' if dc == DC - 1 else None)
            stv = sch.v('st')
            sch.wait('scalar', 'st', stv)
            sch.op('scalar', lambda e, b=bank, hp=hp: e.activation(
                out=hid_fm[:, 2 * hp:2 * hp + 2, :].rearrange('p a b -> p (a b)'),
                in_=b[:, 0:2 * TILE], func=AF.Silu), inc='ss')
            v = sch.v('ss')
            silu_hc[2 * hp] = v
            silu_hc[2 * hp + 1] = v
            if hp == 5 and mom2_tile is not None:
                emit_mom2(mom2_tile)
        silu_ss[t] = silu_hc

    def emit_sostt(t):
        hb = hbuf[t % 2]
        sch.wait('vector', 'ss', ud_ss[t])
        for dc in range(DC):
            sch.op('vector', lambda e, dc=dc, hb=hb: e.scalar_tensor_tensor(
                out=so_fm[:, dc, :], in0=hb[:, dc, :], scalar=cw_v[:, dc:dc + 1],
                in1=ud_fm[:, dc, :], op0=AL.mult, op1=AL.add),
                inc='sv' if dc == DC - 1 else None)
        sostt_sv[t] = sch.v('sv')

    def emit_M2x2(t):
        _par[0] = t % 2
        yb = ybuf[t % 2]
        sch.wait('tensor', 'sv', x2_sv.get(t - 1, 0))
        if t + 1 in scan_sv:
            sch.wait('tensor', 'sv', scan_sv[t + 1])
            sch.wait('tensor', 'ss', ud_ss[t + 1])
        for sl in range(SL):
            for hc in range(HC):
                if sl == 0:
                    sch.wait('tensor', 'ss', silu_ss[t][hc])
                for hf in range(2):
                    b = pb[4 + hf]
                    sch.op('tensor', lambda e, b=b, hc=hc, hf=hf, sl=sl: e.matmul(
                        b[:], hid_fm[:, hc, sl * 128:(sl + 1) * 128],
                        projT[:, hc, hf * 512:(hf + 1) * 512],
                        start=(hc == 0), stop=(hc == HC - 1)),
                        inc='st' if (hc == HC - 1 and hf == 1) else None)
            stv = sch.v('st')
            sch.wait('vector', 'st', stv)
            for hf in range(2):
                b = pb[4 + hf]
                sch.op('vector', lambda e, b=b, sl=sl, hf=hf, yb=yb: e.tensor_tensor(
                    out=x2_t[:, sl, hf * 512:(hf + 1) * 512], in0=b[:],
                    in1=yb[:, sl, hf * 512:(hf + 1) * 512], op=AL.add),
                    inc='sv' if hf == 1 else None)
            x2sl = sch.v('sv')
            sch.tiny_mode = True
            sch.op('vector', lambda e, sl=sl: e.reduce_max(
                out=scol1('mx', sl), in_=x2_t[:, sl, :], axis=AX.X))
            sch.op('vector', lambda e, sl=sl: e.tensor_reduce(
                out=scol1('mn', sl), in_=x2_t[:, sl, :], axis=AX.X, op=AL.min))
            sch.tiny_mode = False
            sch.wait('scalar', 'sv', x2sl)
            sch.op('scalar', lambda e, sl=sl: e.activation(
                out=trash_d[:, 0:D], in_=x2_t[:, sl, :], func=AF.Copy,
                accum_out=scol1('sx3', sl)))
            sch.op('scalar', lambda e, sl=sl: e.activation(
                out=trash_d[:, 0:D], in_=x2_t[:, sl, :], func=AF.Square,
                accum_out=scol1('sq3', sl)), inc='ss' if sl == SL - 1 else None)
            if sl == 0:
                sch.wait('tensor', 'sv', x2sl)
        x2_sv[t] = sch.v('sv')
        mom3_ss[t] = sch.v('ss')

    def emit_soT(t):
        sch.wait('tensor', 'sv', sostt_sv[t])
        sch.wait('tensor', 'sv', scan_sv[t])
        sch.wait('tensor', 'ss', ud_ss[t])
        sch.wait('tensor', 'sv', out_done.get(t - 2, 0))
        for sl in range(SL):
            banks = (pb[0], pb[1]) if sl == 0 else (pb[6], pb[7])
            for dc in range(DC):
                b = banks[dc // 4]
                r = (dc % 4) * 128
                sch.op('tensor', lambda e, b=b, r=r, dc=dc, sl=sl: e.transpose(
                    b[:, r:r + 128], so_fm[:, dc, sl * 128:(sl + 1) * 128], idf[:]),
                    inc='st' if (sl == SL - 1 and dc == DC - 1) else None)
        soT_st[t] = sch.v('st')

    def emit_x1y1(t):
        _par[0] = t % 2
        xs = xbuf[t % 2]
        yb = ybuf[t % 2]
        sch.wait('vector', 'st', soT_st[t])
        if t >= 2:
            sch.wait('vector', 'so', 32 * (t - 1))
        for sl in range(SL):
            banks = (pb[0], pb[1]) if sl == 0 else (pb[6], pb[7])
            sch.op('vector', lambda e, sl=sl, xs=xs: e.tensor_scalar(
                out=tmA[:, sl, :], in0=xs[:, sl, :], scalar1=scol1('m1', sl),
                scalar2=scol1('s1', sl), op0=AL.subtract, op1=AL.mult))
            for hf in range(2):
                b = banks[hf]
                sch.op('vector', lambda e, b=b, sl=sl, hf=hf: e.tensor_tensor(
                    out=tmA[:, sl, hf * 512:(hf + 1) * 512], in0=b[:],
                    in1=tmA[:, sl, hf * 512:(hf + 1) * 512], op=AL.add))
            sch.op('vector', lambda e, sl=sl, xs=xs, yb=yb: e.tensor_tensor(
                out=yb[:, sl, :], in0=tmA[:, sl, :], in1=xs[:, sl, :], op=AL.add),
                inc='sv')
        y1_done[t] = sch.v('sv')

    def emit_mom2(t):
        _par[0] = t % 2
        sch.wait('scalar', 'sv', y1_done[t])
        for sl in range(SL):
            sch.op('scalar', lambda e, sl=sl: e.activation(
                out=trash_d[:, 0:D], in_=tmA[:, sl, :], func=AF.Square,
                accum_out=scol1('sq2', sl)), inc='ss' if sl == SL - 1 else None)
        mom2_ss[t] = sch.v('ss')

    def emit_tiny3(t):
        _par[0] = t % 2
        sch.tiny_mode = True
        sch.wait('vector', 'ss', mom3_ss[t])
        sch.op('vector', lambda e: e.tensor_scalar(out=scol('m3', 4), in0=scol('sx3', 4),
                                                   scalar1=1.0 / D, scalar2=None, op0=AL.mult))
        sch.op('vector', lambda e: e.tensor_tensor(out=scol('rc2'), in0=scol('m3'),
                                                   in1=scol('m3'), op=AL.mult))
        sch.op('vector', lambda e: e.tensor_tensor(out=scol('var3'), in0=scol('msq3'),
                                                   in1=scol('rc2'), op=AL.subtract))
        sch.op('vector', lambda e: e.tensor_scalar(out=scol('w3'), in0=scol('var3'),
                                                   scalar1=EPS_LN, scalar2=None, op0=AL.add))
        sch.op('vector', lambda e: e.reciprocal(out=scol('rc1'), in_=scol('w3')), inc='sv')
        sch.wait('scalar', 'sv', sch.v('sv'))
        sch.op('scalar', lambda e: e.activation(out=scol('s3'), in_=scol('rc1'), func=AF.Sqrt),
               inc='ss')
        sch.wait('vector', 'ss', sch.v('ss'))
        sch.op('vector', lambda e: e.tensor_tensor(out=scol('rc1'), in0=scol('mx'),
                                                   in1=scol('m3'), op=AL.subtract))
        sch.op('vector', lambda e: e.tensor_tensor(out=scol('rc2'), in0=scol('m3'),
                                                   in1=scol('mn'), op=AL.subtract))
        sch.op('vector', lambda e: e.tensor_tensor(out=scol('nr1'), in0=scol('rc1'),
                                                   in1=scol('rc2'), op=AL.max))
        sch.op('vector', lambda e: e.tensor_tensor(out=scol('nr2'), in0=scol('nr1'),
                                                   in1=scol('s3'), op=AL.mult))
        sch.op('vector', lambda e: e.tensor_scalar(out=scol('az'), in0=scol('nr2'),
                                                   scalar1=1e-12, scalar2=None, op0=AL.max))
        sch.op('vector', lambda e: e.tensor_scalar(out=scol('ls'), in0=scol('az'),
                                                   scalar1=1.0 / 127.0, scalar2=None,
                                                   op0=AL.mult))
        sch.op('vector', lambda e: e.reciprocal(out=scol('rc1'), in_=scol('az')))
        sch.op('vector', lambda e: e.tensor_scalar(out=scol('rc1'), in0=scol('rc1'),
                                                   scalar1=127.0, scalar2=None, op0=AL.mult))
        sch.op('vector', lambda e: e.tensor_tensor(out=scol('c3'), in0=scol('rc1'),
                                                   in1=scol('s3'), op=AL.mult))
        sch.tiny_mode = False

    def emit_quant(t):
        _par[0] = t % 2
        sch.wait('vector', 'st', n2T_st.get(t + 1, 0))
        sch.wait('vector', 'ss', mom2_ss.get(t + 1, 0))
        for sl in range(SL):
            sch.op('vector', lambda e, sl=sl: e.tensor_scalar(
                out=tmA[:, sl, :], in0=x2_t[:, sl, :], scalar1=scol1('m3', sl),
                scalar2=scol1('c3', sl), op0=AL.subtract, op1=AL.mult))
            sch.op('vector', lambda e, sl=sl: e.tensor_scalar(
                out=tmA[:, sl, :], in0=tmA[:, sl, :], scalar1=127.0, scalar2=-127.0,
                op0=AL.min, op1=AL.max))
            sch.op('vector', lambda e, sl=sl: e.tensor_scalar(
                out=tmA[:, sl, :], in0=tmA[:, sl, :], scalar1=MAGIC, scalar2=MAGIC,
                op0=AL.add, op1=AL.subtract))
            sch.op('vector', lambda e, sl=sl: e.tensor_copy(out=qi_t[:], in_=tmA[:, sl, :]))
            sch.op('vector', lambda e, sl=sl: e.tensor_copy(out=q_tm[:, sl, :], in_=qi_t[:]),
                   inc='sv' if sl == SL - 1 else None)
        quant_sv[t] = sch.v('sv')

    def emit_qT(t):
        qe = {}
        sch.wait('tensor', 'sv', quant_sv[t])
        sch.wait('tensor', 'ss', silu_ss[t][HC - 1])
        for dp in range(DC // 2):
            bank = pb[2 + dp % 2]
            if dp >= 2:
                sch.wait('tensor', 'ss', qe[2 * (dp - 2)])
            for sub in range(2):
                dc = 2 * dp + sub
                for sl in range(SL):
                    sch.op('tensor', lambda e, b=bank, dc=dc, sub=sub, sl=sl: e.transpose(
                        b[:, sub * TILE + sl * 128:sub * TILE + (sl + 1) * 128],
                        q_tm[:, sl, dc * 128:(dc + 1) * 128], idf[:]),
                        inc='st' if sl == SL - 1 else None)
            stv = sch.v('st')
            sch.wait('scalar', 'st', stv)
            sch.op('scalar', lambda e, b=bank, dp=dp: e.activation(
                out=q_fm[:, 2 * dp:2 * dp + 2, :].rearrange('p a b -> p (a b)'),
                in_=b[:, 0:2 * TILE], func=AF.Copy), inc='ss')
            v = sch.v('ss')
            qe[2 * dp] = v
            qe[2 * dp + 1] = v
        qT_st[t] = sch.v('st')
        qe_ss[t] = qe

    def emit_M3out(t):
        _par[0] = t % 2
        yb = ybuf[t % 2]
        sch.wait('tensor', 'sv', out_done.get(t - 1, 0))
        sch.wait('tensor', 'sv', y1_done.get(t + 1, 0))
        for sl in range(SL):
            for dc in range(DC):
                if sl == 0:
                    sch.wait('tensor', 'ss', qe_ss[t][dc])
                for hf in range(2):
                    b = pb[6 + hf]
                    sch.op('tensor', lambda e, b=b, dc=dc, hf=hf, sl=sl: e.matmul(
                        b[:], q_fm[:, dc, sl * 128:(sl + 1) * 128],
                        qrhs[:, dc, hf * 512:(hf + 1) * 512],
                        start=(dc == 0), stop=(dc == DC - 1)),
                        inc='st' if (dc == DC - 1 and hf == 1) else None)
            stv = sch.v('st')
            sch.wait('vector', 'st', stv)
            for hf in range(2):
                b = pb[6 + hf]
                sch.op('vector', lambda e, b=b, sl=sl, hf=hf: e.scalar_tensor_tensor(
                    out=n2_tm[:, sl, hf * 512:(hf + 1) * 512], in0=b[:],
                    scalar=scol1('ls', sl), in1=rs_repl[:, hf * 512:(hf + 1) * 512],
                    op0=AL.mult, op1=AL.mult))
                sch.op('vector', lambda e, sl=sl, hf=hf, yb=yb: e.tensor_tensor(
                    out=yb[:, sl, hf * 512:(hf + 1) * 512],
                    in0=n2_tm[:, sl, hf * 512:(hf + 1) * 512],
                    in1=x2_t[:, sl, hf * 512:(hf + 1) * 512], op=AL.add),
                    inc='sv' if hf == 1 else None)
            out_sl.setdefault(t, {})[sl] = sch.v('sv')
            if sl == 0:
                sch.wait('tensor', 'sv', sch.v('sv'))
        out_done[t] = sch.v('sv')

    def emit_outdma(t):
        yb = ybuf[t % 2]
        for sl in range(SL):
            sch.wait('sync', 'sv', out_sl[t][sl])
            rows = out_d[t * TILE + sl * 128:t * TILE + (sl + 1) * 128, :]
            sch.op('sync', lambda e, r=rows, yb=yb, sl=sl: e.dma_start(
                out=r, in_=yb[:, sl, :]), inc='so', inc_n=16)

    def emit_prefetch(t):
        sch.wait('sync', 'sv', y1_done[t - 2])
        sch.op('sync', lambda e, t=t: e.dma_start(out=xbuf[t % 2][:], in_=x_ap(t)),
               inc='sx', inc_n=16)
        x_sx[t] = sch.v('sx')

    n_iters = nt + 2 if 'tiles' not in DEBUG_SKIP else 0
    if n_iters:
        emit_mom1(0)
        emit_tiny1(0)
        emit_xau(0)
        if nt > 1:
            emit_mom1(1)
    for it in range(n_iters):
        pre = it if it < nt else None
        mid = it - 1 if 1 <= it <= nt else None
        fin = it - 2 if it >= 2 else None
        if fin is not None:
            emit_qT(fin)
        if pre is not None:
            emit_scanblock(pre)
        if pre is not None:
            emit_sostt(pre)
        if fin is not None:
            emit_M3out(fin)
            emit_outdma(fin)
        if pre is not None and pre + 1 < nt:
            emit_tiny1(pre + 1)
        if pre is not None:
            emit_soT(pre)
        if pre is not None and pre + 1 < nt:
            emit_xau(pre + 1)
        if pre is not None:
            emit_x1y1(pre)
        if mid is not None:
            emit_M1(mid, pre)
        elif pre is not None:
            emit_mom2(pre)
        if pre is not None:
            emit_tiny2(pre)
            emit_n2(pre)
        if mid is not None:
            emit_M2x2(mid)
            emit_tiny3(mid)
        if pre is not None:
            emit_n2T(pre)
        if mid is not None:
            emit_quant(mid)
        if pre is not None:
            if pre + 2 < nt:
                emit_prefetch(pre + 2)
                emit_mom1(pre + 2)
    if DEBUG_DUMP:
        dbg = {'stats': stats, 'carry0': carry0, 'a_v': a_v, 'tmA': tmA,
               'xa_t': xa_t, 'ud_fm': ud_fm, 'h0': hbuf[0], 'so_fm': so_fm,
               'y0': ybuf[0], 'x2_t': x2_t, 'n2_tm': n2_tm, 'n2_fm': n2_fm,
               'hid_fm': hid_fm, 'q_fm': q_fm, 'qrhs': qrhs, 'rs_repl': rs_repl,
               'fcT': fcT, 'projT': projT, 'xb0': xbuf[0]}
        for s in ('sv', 'ss', 'st', 'sg'):
            sch.wait('sync', s, sch.v(s))
        for i, nm in enumerate(DEBUG_DUMP):
            t = dbg[nm]
            ncols = 1
            for d_ in t.shape[1:]:
                ncols *= d_
            ncols = min(ncols, D)
            ap = t[:].rearrange('p a b -> p (a b)') if len(t.shape) == 3 else t[:]
            sch.op('sync', lambda e, i=i, ap=ap, ncols=ncols: e.dma_start(
                out=out_d[i * 128:(i + 1) * 128, 0:ncols], in_=ap[:, 0:ncols]),
                inc='so', inc_n=16)

    sch.wait('sync', 'so', sch.v('so'))

    emit_block(mark, len(sch.items))
    for g in sem_ctxs:
        g.__exit__(None, None, None)
    for g in reversed(ctxs):
        g.__exit__(None, None, None)
    nc.finalize()
    return nc


def _vec_layout(v):
    return np.ascontiguousarray(v.reshape(DC, 128).T.astype(np.float32))


def kernel(L0_attn_x, L0_attn_input_norm_w, L0_attn_a_log, L0_attn_c_out,
           L0_attn_d_skip, L0_attn_ff_norm_w, L0_attn_ff_fc_w, L0_attn_ff_proj_w,
           L0_ffn_rhs):
    B, T, _ = L0_attn_x.shape
    half = T // 2
    x = np.ascontiguousarray(L0_attn_x.astype(np.float32, copy=False))

    fcT_h = np.ascontiguousarray(L0_attn_ff_fc_w.T.astype(np.float32, copy=False))
    projT_h = np.ascontiguousarray(L0_attn_ff_proj_w.T.astype(np.float32, copy=False))
    rhsT_h = np.ascontiguousarray(L0_ffn_rhs.T.astype(np.float32, copy=False))
    shared = {
        'fcT_h': fcT_h, 'projT_h': projT_h, 'rhsT_h': rhsT_h,
        'inw_h': _vec_layout(L0_attn_input_norm_w),
        'alog_h': _vec_layout(L0_attn_a_log),
        'cout_h': _vec_layout(L0_attn_c_out),
        'dskip_h': _vec_layout(L0_attn_d_skip),
        'ffw_h': _vec_layout(L0_attn_ff_norm_w),
        'idf_h': np.eye(128, dtype=np.float32),
        'idb_h': np.eye(128, dtype=ml_dtypes.bfloat16),
        'ones_h': np.ones((1, 128), np.float32),
        'iota_h': np.arange(PREV_T, dtype=np.float32)[None, :],
    }
    zeros_prev = np.zeros((PREV_T, D), np.float32)
    in_maps = []
    for c in range(N_CORES):
        b, hf = c // 2, c % 2
        xc_c = np.ascontiguousarray(x[b, hf * half:(hf + 1) * half])
        xp_c = zeros_prev if hf == 0 else np.ascontiguousarray(
            x[b, half - PREV_T:half])
        in_maps.append({'xc': xc_c, 'xp': xp_c, **shared})

    nc = build_program(T_CORE, PREV_T)
    res = run_bass_kernel_spmd(nc, in_maps, core_ids=list(range(N_CORES)))

    out = np.empty((B, T, D), np.float32)
    for c in range(N_CORES):
        b, hf = c // 2, c % 2
        out[b, hf * half:(hf + 1) * half] = res.results[c]['out']
    return out


# version 24
# speedup vs baseline: 1.0421x; 1.0151x over previous
import numpy as np
import ml_dtypes
import concourse.bass as bass
import concourse.mybir as mybir
import concourse.bass_utils as _bu
from concourse.bass_utils import run_bass_kernel_spmd

if not getattr(_bu, '_ldw_opt_patched', False):
    _orig_get_walrus_args = _bu.get_walrus_args

    def _get_walrus_args_ldw(arch, tmpdir, **kw):
        return _orig_get_walrus_args(arch, tmpdir, **kw) + ["--enable-ldw-opt=true"]

    _bu.get_walrus_args = _get_walrus_args_ldw
    _bu._ldw_opt_patched = True

F32 = mybir.dt.float32
BF16 = mybir.dt.bfloat16
I32 = mybir.dt.int32
AL = mybir.AluOpType
AX = mybir.AxisListType
AF = mybir.ActivationFunctionType

D = 1024
H = 2048
DC = 8
HC = 16
MAGIC = 12582912.0
EPS_LN = 1e-5
EPS_RMS = 1e-6
N_CORES = 8
T_CORE = 4096
TILE = 256
SL = TILE // 128
PREV_T = 1024
DEBUG_SKIP = set()
TILE_STOP = 99
DEBUG_DUMP = []


class Sched:

    def __init__(self, sem_names):
        self.items = []
        self.cnt = {s: 0 for s in sem_names}
        self.tiny_mode = False
        self.tiny_cnt = 0

    def op(self, eng, fn, inc=None, inc_n=1):
        if inc is not None:
            self.cnt[inc] += inc_n
        par = self.par_ref[0] if getattr(self, 'par_ref', None) is not None else 0
        tiny = self.tiny_mode and eng == 'vector'
        if tiny:
            self.tiny_cnt += 1
        self.items.append((eng, 'op', (fn, inc, inc_n, par, tiny, self.tiny_cnt)))

    def wait(self, eng, sem, val):
        self.items.append((eng, 'wait', (sem, val)))

    def v(self, sem):
        return self.cnt[sem]


def build_program(t_core=T_CORE, prev_t=PREV_T):
    nt = t_core // TILE
    n_prev = prev_t // TILE

    nc = bass.Bass('TRN2', target_bir_lowering=False, detect_race_conditions=False)

    xc = nc.dram_tensor('xc', [t_core, D], F32, kind='ExternalInput').ap()
    xp = nc.dram_tensor('xp', [prev_t, D], F32, kind='ExternalInput').ap()
    fcT_h = nc.dram_tensor('fcT_h', [D, H], F32, kind='ExternalInput').ap()
    projT_h = nc.dram_tensor('projT_h', [H, D], F32, kind='ExternalInput').ap()
    rhsT_h = nc.dram_tensor('rhsT_h', [D, D], F32, kind='ExternalInput').ap()
    inw_h = nc.dram_tensor('inw_h', [128, DC], F32, kind='ExternalInput').ap()
    alog_h = nc.dram_tensor('alog_h', [128, DC], F32, kind='ExternalInput').ap()
    cout_h = nc.dram_tensor('cout_h', [128, DC], F32, kind='ExternalInput').ap()
    dskip_h = nc.dram_tensor('dskip_h', [128, DC], F32, kind='ExternalInput').ap()
    ffw_h = nc.dram_tensor('ffw_h', [128, DC], F32, kind='ExternalInput').ap()
    idf_h = nc.dram_tensor('idf_h', [128, 128], F32, kind='ExternalInput').ap()
    idb_h = nc.dram_tensor('idb_h', [128, 128], BF16, kind='ExternalInput').ap()
    ones_h = nc.dram_tensor('ones_h', [1, 128], F32, kind='ExternalInput').ap()
    iota_h = nc.dram_tensor('iota_h', [1, prev_t], F32, kind='ExternalInput').ap()
    out_d = nc.dram_tensor('out', [t_core, D], F32, kind='ExternalOutput').ap()

    sems = ['sx', 'so', 'sg', 'sv', 'ss', 'st', 'svd']
    sch = Sched(sems)

    ctxs = []
    pro_ctxs = []

    def sb(name, shape, dt=F32, scope=None):
        g = nc.sbuf_tensor(name, shape, dt)
        t = g.__enter__()
        (ctxs if scope is None else scope).append(g)
        return t

    def psum(name, shape):
        g = nc.psum_tensor(name, shape, F32)
        t = g.__enter__()
        ctxs.append(g)
        return t

    fcT = sb('fcT', [128, DC, H], BF16)
    projT = sb('projT', [128, HC, D], BF16)
    qrhs = sb('qrhs', [128, DC, D], BF16)
    rs_repl = sb('rs_repl', [128, D], F32)
    idf = sb('idf', [128, 128], F32)
    idb = sb('idb', [128, 128], BF16)
    ones_c = sb('ones_c', [1, 128], F32)
    iota_r = sb('iota_r', [1, prev_t], F32)
    inw_v = sb('inw_v', [128, DC], F32)
    alog_v = sb('alog_v', [128, DC], F32)
    cout_v = sb('cout_v', [128, DC], F32)
    dskip_v = sb('dskip_v', [128, DC], F32)
    ffw_v = sb('ffw_v', [128, DC], F32)
    e_v = sb('e_v', [128, DC], F32)
    a_v = sb('a_v', [128, DC], F32)
    cw_v = sb('cw_v', [128, DC], F32)
    dw_v = sb('dw_v', [128, DC], F32)
    carry0 = sb('carry0', [128, DC], F32)
    carryB = sb('carryB', [128, DC], F32)
    rs_e = sb('rs_e', [128, DC], F32)
    rrs_e = sb('rrs_e', [128, DC], F32)

    pb = [psum('pb%d' % i, [128, 512]) for i in range(8)]

    xbuf = [sb('xb%d' % i, [128, SL, D], F32) for i in range(2)]
    qi_t = sb('qi_t', [128, D], I32)
    stats = sb('stats', [128, 128], F32)
    trash_s = sb('trash_s', [128, D], F32, scope=pro_ctxs)
    wscr = [sb('wscr%d' % i, [128, H], F32, scope=pro_ctxs) for i in range(2)]
    powt = sb('powt', [128, DC, prev_t], F32, scope=pro_ctxs)
    xpb = [sb('xpb%d' % i, [128, SL, D], F32, scope=pro_ctxs) for i in range(2)]
    rstg = [sb('rstg%d' % i, [128, D], F32, scope=pro_ctxs) for i in range(2)]
    upb = sb('upb', [128, SL, D], F32, scope=pro_ctxs)
    trash_f = sb('trash_f', [128, TILE], F32, scope=pro_ctxs)
    qrt = sb('qrt', [128, D], F32, scope=pro_ctxs)
    rowscr = sb('rowscr', [1, 128], F32, scope=pro_ctxs)

    ST = stats

    COL = {}
    _next = [0]
    _par = [0]
    sch.par_ref = _par

    def scol(name, n=SL):
        if name not in COL:
            COL[name] = _next[0]
            _next[0] += n
        c = COL[name] + 64 * _par[0]
        return ST[:, c:c + n]

    def scol1(name, sl):
        c = COL[name] + 64 * _par[0]
        return ST[:, c + sl:c + sl + 1]

    for nmm in ['sx1', 'sq1', 'm1', 'msq1', 'var1', 'w1', 'wc', 's1', 'cc',
                'sq2', 'msq2', 'w2', 'r2', 'sx3', 'sq3', 'm3', 'msq3', 'var3',
                'w3', 's3', 'mx', 'mn', 'az', 'ls', 'c3', 'nr1', 'nr2', 'rc1', 'rc2']:
        scol(nmm)

    def nr_polish(eng_ops, yname, wname, iters=2):
        assert iters % 2 == 0
        cols = [yname, 'nr2']
        for it in range(iters):
            t1, t2 = scol('nr1'), scol('nr1')
            ya, yb_ = scol(cols[it % 2]), scol(cols[(it + 1) % 2])
            w = scol(wname)
            sch.op('vector', lambda e, a=ya: e.tensor_tensor(out=scol('nr1'), in0=a, in1=a, op=AL.mult))
            sch.op('vector', lambda e, w=w: e.tensor_tensor(out=scol('rc1'), in0=scol('nr1'), in1=w, op=AL.mult))
            sch.op('vector', lambda e: e.tensor_scalar(out=scol('nr1'), in0=scol('rc1'), scalar1=-0.5,
                                                       scalar2=1.5, op0=AL.mult, op1=AL.add))
            sch.op('vector', lambda e, a=ya, b=yb_: e.tensor_tensor(out=b, in0=a, in1=scol('nr1'), op=AL.mult))

    def rsqrt_seq(wname, yname, inc_tag=None):
        w = scol(wname)
        y = scol(yname)
        rec = scol('nr2')
        sch.op('vector', lambda e: e.reciprocal(out=rec, in_=w), inc='sv')
        sch.wait('scalar', 'sv', sch.v('sv'))
        sch.op('scalar', lambda e: e.activation(out=y, in_=rec, func=AF.Sqrt), inc='ss')
        sch.wait('vector', 'ss', sch.v('ss'))

    def x_ap(j):
        return xc[j * TILE:(j + 1) * TILE, :].rearrange('(s p) d -> p s d', p=128)

    def o_ap(j):
        return out_d[j * TILE:(j + 1) * TILE, :].rearrange('(s p) d -> p s d', p=128)

    def xp_ap(j):
        return xp[j * TILE:(j + 1) * TILE, :].rearrange('(s p) d -> p s d', p=128)

    for dst, src in ((idf, idf_h), (idb, idb_h), (ones_c, ones_h), (iota_r, iota_h),
                     (inw_v, inw_h), (alog_v, alog_h), (cout_v, cout_h),
                     (dskip_v, dskip_h), (ffw_v, ffw_h)):
        sch.op('sync', lambda e, d=dst, s=src: e.dma_start(out=d[:], in_=s), inc='sx', inc_n=16)
    sx_consts = sch.v('sx')
    x_sx = {}
    sch.op('sync', lambda e: e.dma_start(out=xbuf[0][:], in_=x_ap(0)), inc='sx', inc_n=16)
    x_sx[0] = sch.v('sx')
    if nt > 1:
        sch.op('sync', lambda e: e.dma_start(out=xbuf[1][:], in_=x_ap(1)), inc='sx', inc_n=16)
        x_sx[1] = sch.v('sx')

    for hc in range(HC if 'projT' not in DEBUG_SKIP else 0):
        ap_in = projT_h[hc * 128:(hc + 1) * 128, :]
        sch.op('gpsimd', lambda e, a=ap_in, hc=hc: e.dma_start(
            out=projT[:, hc, :], in_=a), inc='sg', inc_n=16)
    sg_proj = sch.v('sg')

    sch.wait('scalar', 'sx', sx_consts)
    sch.op('scalar', lambda e: e.activation(out=e_v[:], in_=alog_v[:], func=AF.Exp))
    sch.op('scalar', lambda e: e.activation(out=a_v[:], in_=e_v[:], func=AF.Exp,
                                            scale=-1.0), inc='ss')
    ss_av = sch.v('ss')
    sch.wait('vector', 'sx', sx_consts)
    sch.tiny_mode = True
    sch.op('vector', lambda e: e.tensor_tensor(out=cw_v[:], in0=cout_v[:], in1=inw_v[:], op=AL.mult))
    sch.op('vector', lambda e: e.tensor_tensor(out=dw_v[:], in0=dskip_v[:], in1=inw_v[:], op=AL.mult))
    sch.op('vector', lambda e: e.memset(carry0[:], 0.0), inc='sv')
    sch.tiny_mode = False
    sv_vecs = sch.v('sv')

    n_half = (prev_t // 512) if 'powtile' not in DEBUG_SKIP else 0
    for hf in range(n_half):
        bank = pb[6 + (hf % 2)]
        sch.wait('tensor', 'sx', sx_consts)
        if hf >= 2:
            sch.wait('tensor', 'sv', sv_vecs + (hf - 1) * DC)
        sch.op('tensor', lambda e, b=bank, hf=hf: e.matmul(
            b[:], ones_c[:], iota_r[0:1, hf * 512:(hf + 1) * 512],
            start=True, stop=True), inc='st')
        st_iota = sch.v('st')
        sch.wait('vector', 'st', st_iota)
        sch.wait('vector', 'ss', ss_av)
        for dc in range(DC):
            if hf > 0 or dc >= 2:
                sch.wait('vector', 'ss', sch.v('ss'))
            sch.op('vector', lambda e, b=bank, dc=dc, hf=hf: e.tensor_scalar(
                out=wscr[0][:, (dc % 2) * 512:(dc % 2) * 512 + 512], in0=b[:],
                scalar1=float(prev_t - 1), scalar2=e_v[:, dc:dc + 1],
                op0=AL.subtract, op1=AL.mult), inc='sv')
            sch.wait('scalar', 'sv', sch.v('sv'))
            sch.op('scalar', lambda e, dc=dc, hf=hf: e.activation(
                out=powt[:, dc, hf * 512:(hf + 1) * 512],
                in_=wscr[0][:, (dc % 2) * 512:(dc % 2) * 512 + 512],
                func=AF.Exp), inc='ss')
    ss_pow = sch.v('ss')

    for dc in range(DC if 'fcT' not in DEBUG_SKIP else 0):
        ap_in = fcT_h[dc * 128:(dc + 1) * 128, :]
        sch.op('gpsimd', lambda e, a=ap_in, dc=dc: e.dma_start(
            out=fcT[:, dc, :], in_=a), inc='sg', inc_n=16)

    rstate = {'sv_q': sch.v('sv'), 'ev': []}

    def emit_rhsT_ec(ec):
        buf = rstg[ec % 2]
        if ec >= 2:
            sch.wait('sync', 'sv', rstate['sv_q'])
        ap_in = rhsT_h[ec * 128:(ec + 1) * 128, :]
        sch.op('sync', lambda e, a=ap_in, b=buf: e.dma_start(out=b[:], in_=a),
               inc='sx', inc_n=16)
        sxv = sch.v('sx')
        sch.wait('vector', 'sx', sxv)
        sch.tiny_mode = True
        sch.op('vector', lambda e, b=buf, ec=ec: e.reduce_max(
            out=rs_e[:, ec:ec + 1], in_=b[:, 0:D], axis=AX.X, apply_absolute_value=True))
        sch.op('vector', lambda e, ec=ec: e.tensor_scalar(
            out=rs_e[:, ec:ec + 1], in0=rs_e[:, ec:ec + 1], scalar1=1e-12,
            scalar2=1.0 / 127.0, op0=AL.max, op1=AL.mult))
        sch.op('vector', lambda e, ec=ec: e.reciprocal(
            out=rrs_e[:, ec:ec + 1], in_=rs_e[:, ec:ec + 1]))
        sch.tiny_mode = False
        sch.op('vector', lambda e, b=buf, ec=ec: e.tensor_scalar(
            out=b[:, 0:D], in0=b[:, 0:D], scalar1=rrs_e[:, ec:ec + 1], scalar2=None,
            op0=AL.mult))
        sch.op('vector', lambda e, b=buf: e.tensor_scalar(
            out=b[:, 0:D], in0=b[:, 0:D], scalar1=127.0, scalar2=-127.0,
            op0=AL.min, op1=AL.max))
        sch.op('vector', lambda e, b=buf: e.tensor_scalar(
            out=b[:, 0:D], in0=b[:, 0:D], scalar1=MAGIC, scalar2=MAGIC,
            op0=AL.add, op1=AL.subtract))
        sch.op('vector', lambda e, b=buf: e.tensor_copy(out=qi_t[:], in_=b[:, 0:D]))
        sch.op('vector', lambda e: e.tensor_copy(out=qrt[:], in_=qi_t[:]), inc='sv')
        sv_q = sch.v('sv')
        for dc in range(DC):
            bank = pb[dc % 2]
            i = ec * DC + dc
            if i >= 2:
                sch.wait('tensor', 'ss', rstate['ev'][i - 2])
            sch.wait('tensor', 'sv', sv_q)
            sch.op('tensor', lambda e, b=bank, dc=dc: e.transpose(
                b[:, 0:128], qrt[:, dc * 128:(dc + 1) * 128], idf[:]), inc='st')
            stv = sch.v('st')
            sch.wait('scalar', 'st', stv)
            sch.op('scalar', lambda e, b=bank, dc=dc, ec=ec: e.activation(
                out=qrhs[:, dc, ec * 128:(ec + 1) * 128], in_=b[:, 0:128],
                func=AF.Copy), inc='ss')
            rstate['ev'].append(sch.v('ss'))
        sch.wait('tensor', 'ss', rstate['ev'][-1])
        sch.op('tensor', lambda e, ec=ec: e.transpose(
            pb[2][0:1, 0:128], rs_e[:, ec:ec + 1], idf[:]), inc='st')
        stv = sch.v('st')
        sch.wait('vector', 'st', stv)
        sch.op('vector', lambda e: e.tensor_copy(out=rowscr[:], in_=pb[2][0:1, 0:128]),
               inc='sv')
        svv = sch.v('sv')
        sch.wait('tensor', 'sv', svv)
        sch.op('tensor', lambda e: e.matmul(pb[3][:, 0:128], ones_c[:], rowscr[:],
                                            start=True, stop=True), inc='st')
        stv = sch.v('st')
        sch.wait('vector', 'st', stv)
        sch.op('vector', lambda e, ec=ec: e.tensor_copy(
            out=rs_repl[:, ec * 128:(ec + 1) * 128], in_=pb[3][:, 0:128]), inc='sv')
        rstate['sv_q'] = sch.v('sv')

    cstate = {'cur': carry0, 'alt': carryB}

    def emit_carry_mt(mt):
        carry_cur, carry_alt = cstate['cur'], cstate['alt']
        xpc = xpb[mt % 2]
        sch.wait('scalar', 'sx', xp_sx[mt])
        for sl in range(SL):
            sch.op('scalar', lambda e, sl=sl, xpc=xpc: e.activation(
                out=trash_s[:], in_=xpc[:, sl, :], func=AF.Copy,
                accum_out=scol1('sx1', sl)))
            sch.op('scalar', lambda e, sl=sl, xpc=xpc: e.activation(
                out=trash_s[:], in_=xpc[:, sl, :], func=AF.Square,
                accum_out=scol1('sq1', sl)), inc='ss' if sl == SL - 1 else None)
        ssv = sch.v('ss')
        sch.wait('vector', 'ss', ssv)
        sch.tiny_mode = True
        sch.op('vector', lambda e: e.tensor_scalar(out=scol('m1', 4), in0=scol('sx1', 4),
                                                   scalar1=1.0 / D, scalar2=None, op0=AL.mult))
        sch.op('vector', lambda e: e.tensor_tensor(out=scol('rc1'), in0=scol('m1'),
                                                   in1=scol('m1'), op=AL.mult))
        sch.op('vector', lambda e: e.tensor_tensor(out=scol('var1'), in0=scol('msq1'),
                                                   in1=scol('rc1'), op=AL.subtract))
        sch.op('vector', lambda e: e.tensor_scalar(out=scol('wc'), in0=scol('var1'),
                                                   scalar1=1.0 + EPS_RMS,
                                                   scalar2=EPS_RMS * EPS_LN,
                                                   op0=AL.mult, op1=AL.add))
        rsqrt_seq('wc', 'cc')
        sch.tiny_mode = False
        for sl in range(SL):
            sch.op('vector', lambda e, sl=sl, xpc=xpc: e.tensor_scalar(
                out=upb[:, sl, :], in0=xpc[:, sl, :], scalar1=scol1('m1', sl),
                scalar2=scol1('cc', sl), op0=AL.subtract, op1=AL.mult),
                inc='sv' if sl == SL - 1 else None)
        sv_u = sch.v('sv')
        for dc in range(DC):
            bank, reg = pb[4 + dc % 4], 0
            if dc >= 4:
                sch.wait('tensor', 'sv', sch.v('sv'))
            sch.wait('tensor', 'sv', sv_u)
            for sl in range(SL):
                sch.op('tensor', lambda e, b=bank, r=reg, dc=dc, sl=sl: e.transpose(
                    b[:, r + sl * 128:r + (sl + 1) * 128],
                    upb[:, sl, dc * 128:(dc + 1) * 128], idf[:]),
                    inc='st' if sl == SL - 1 else None)
            stv = sch.v('st')
            sch.wait('vector', 'st', stv)
            sch.op('vector', lambda e, b=bank, r=reg, dc=dc, mt=mt: e.tensor_tensor(
                out=trash_f[:, 0:TILE], in0=b[:, r:r + TILE],
                in1=powt[:, dc, mt * TILE:(mt + 1) * TILE], op=AL.mult))
            sch.tiny_mode = True
            sch.op('vector', lambda e: e.reduce_sum(
                out=scol('nr2')[:, 0:1], in_=trash_f[:, 0:TILE], axis=AX.X))
            sch.op('vector', lambda e, dc=dc, cc_=carry_cur, ca=carry_alt: e.tensor_tensor(
                out=ca[:, dc:dc + 1], in0=cc_[:, dc:dc + 1],
                in1=scol('nr2')[:, 0:1], op=AL.add), inc='sv')
            sch.tiny_mode = False
        cstate['cur'], cstate['alt'] = carry_alt, carry_cur

    n_mt = n_prev if 'carry' not in DEBUG_SKIP else 0
    n_ec = DC if 'qrhs' not in DEBUG_SKIP else 0
    xp_sx = {}
    for mt in range(min(n_mt, 2)):
        sch.op('sync', lambda e, mt=mt: e.dma_start(out=xpb[mt % 2][:], in_=xp_ap(mt)),
               inc='sx', inc_n=16)
        xp_sx[mt] = sch.v('sx')
    for k in range(max(n_ec, n_mt)):
        if k < n_ec:
            emit_rhsT_ec(k)
        if k < n_mt:
            emit_carry_mt(k)
            if k + 2 < n_mt:
                sch.wait('sync', 'sv', sch.v('sv'))
                sch.op('sync', lambda e, mt=k + 2: e.dma_start(
                    out=xpb[mt % 2][:], in_=xp_ap(mt)), inc='sx', inc_n=16)
                xp_sx[k + 2] = sch.v('sx')
    if cstate['cur'] is not carry0 and n_mt:
        sch.tiny_mode = True
        sch.op('vector', lambda e: e.tensor_copy(out=carry0[:], in_=carryB[:]), inc='sv')
        sch.tiny_mode = False
    sv_carry = sch.v('sv')

    PRO = dict(sv=sch.v('sv'), ss=sch.v('ss'), st=sch.v('st'),
               sx=sch.v('sx'), sg=sch.v('sg'))
    for eng in ('sync', 'vector', 'scalar', 'tensor'):
        for s in ('sv', 'ss', 'st', 'sg'):
            sch.wait(eng, s, PRO[s])

    mark = len(sch.items)
    sem_handles = {}
    sem_ctxs = []
    for s in sems:
        g = nc.semaphore(s)
        sem_handles[s] = g.__enter__()
        sem_ctxs.append(g)

    def make_engine_fn(name, lo, hi):
        def fn(eng):
            for (e, kind, payload) in sch.items[lo:hi]:
                if e != name:
                    continue
                if kind == 'wait':
                    s, v = payload
                    if v > 0:
                        eng.wait_ge(sem_handles[s], v)
                else:
                    f, inc, inc_n, par, tiny, tc = payload
                    _par[0] = par
                    if name == 'vector' and tc > 0:
                        eng.wait_ge(sem_handles['svd'], tc - (1 if tiny else 0))
                    ins = f(eng)
                    if inc is not None:
                        ins.then_inc(sem_handles[inc], inc_n)
                        if tiny:
                            eng.engine_nop().then_inc(sem_handles['svd'], 1)
                    elif tiny:
                        ins.then_inc(sem_handles['svd'], 1)
        return fn

    def emit_block(lo, hi):
        g = nc.Block()
        blk = g.__enter__()
        blk.sync(make_engine_fn('sync', lo, hi))
        blk.vector(make_engine_fn('vector', lo, hi))
        blk.scalar(make_engine_fn('scalar', lo, hi))
        blk.tensor(make_engine_fn('tensor', lo, hi))
        blk.gpsimd(make_engine_fn('gpsimd', lo, hi))
        g.__exit__(None, None, None)

    emit_block(0, mark)
    for g in reversed(pro_ctxs):
        g.__exit__(None, None, None)

    tmA = sb('tmA', [128, SL, D], F32)
    u_t = sb('u_t', [128, SL, D], F32)
    ud_fm = sb('ud_fm', [128, DC, TILE], F32)
    trash_d = ud_fm[:].rearrange('p a b -> p (a b)')
    hbuf = [sb('hb%d' % i, [128, DC, TILE], F32) for i in range(2)]
    so_fm = sb('so_fm', [128, DC, TILE], F32)
    ybuf = [sb('yb%d' % i, [128, SL, D], F32) for i in range(2)]
    x2_t = sb('x2_t', [128, SL, D], F32)
    n2_tm = sb('n2_tm', [128, SL, D], F32)
    n2_fm = sb('n2_fm', [128, DC, TILE], BF16)
    hid_fm = sb('hid_fm', [128, HC, TILE], BF16)
    q_tm = n2_tm
    q_fm = sb('q_fm', [128, DC, TILE], BF16)

    y1_done = {}
    out_done = {}
    x2_sv = {}
    quant_sv = {}
    sostt_sv = {}
    scan_sv = {}
    xau_sv = {}
    mom1_ss = {}
    mom2_ss = {}
    mom3_ss = {}
    s1cc_ss = {}
    r2_ss = {}
    n2e_ss = {}
    ud_ss = {}
    silu_ss = {}
    qe_ss = {}
    sv_n2 = {}
    n2T_st = {}
    soT_st = {}
    qT_st = {}
    uT_st = {}
    out_sl = {}

    def emit_mom1(t):
        _par[0] = t % 2
        xs = xbuf[t % 2]
        sch.wait('scalar', 'sx', x_sx[t])
        for sl in range(SL):
            sch.op('scalar', lambda e, sl=sl, xs=xs: e.activation(
                out=trash_d[:, 0:D], in_=xs[:, sl, :], func=AF.Copy, accum_out=scol1('sx1', sl)))
            sch.op('scalar', lambda e, sl=sl, xs=xs: e.activation(
                out=trash_d[:, 0:D], in_=xs[:, sl, :], func=AF.Square, accum_out=scol1('sq1', sl)),
                inc='ss' if sl == SL - 1 else None)
        mom1_ss[t] = sch.v('ss')

    def emit_tiny1(t):
        _par[0] = t % 2
        sch.wait('vector', 'ss', mom1_ss[t])
        sch.tiny_mode = True
        sch.op('vector', lambda e: e.tensor_scalar(out=scol('m1', 4), in0=scol('sx1', 4),
                                                   scalar1=1.0 / D, scalar2=None, op0=AL.mult))
        sch.op('vector', lambda e: e.tensor_tensor(out=scol('rc1'), in0=scol('m1'),
                                                   in1=scol('m1'), op=AL.mult))
        sch.op('vector', lambda e: e.tensor_tensor(out=scol('var1'), in0=scol('msq1'),
                                                   in1=scol('rc1'), op=AL.subtract))
        sch.op('vector', lambda e: e.tensor_scalar(out=scol('w1'), in0=scol('var1'),
                                                   scalar1=EPS_LN, scalar2=None, op0=AL.add))
        sch.op('vector', lambda e: e.tensor_scalar(out=scol('wc'), in0=scol('var1'),
                                                   scalar1=1.0 + EPS_RMS,
                                                   scalar2=EPS_RMS * EPS_LN,
                                                   op0=AL.mult, op1=AL.add))
        sch.op('vector', lambda e: e.reciprocal(out=scol('rc1', 4), in_=scol('w1', 4)),
               inc='sv')
        sch.tiny_mode = False
        sch.wait('scalar', 'sv', sch.v('sv'))
        sch.op('scalar', lambda e: e.activation(out=scol('s1', 4), in_=scol('rc1', 4),
                                                func=AF.Sqrt), inc='ss')
        s1cc_ss[t] = sch.v('ss')

    def emit_tiny2(t):
        _par[0] = t % 2
        sch.wait('vector', 'ss', mom2_ss[t])
        sch.tiny_mode = True
        sch.op('vector', lambda e: e.tensor_scalar(out=scol('w2'), in0=scol('sq2'),
                                                   scalar1=1.0 / D, scalar2=EPS_RMS,
                                                   op0=AL.mult, op1=AL.add))
        sch.op('vector', lambda e: e.reciprocal(out=scol('rc1'), in_=scol('w2')), inc='sv')
        sch.tiny_mode = False
        sch.wait('scalar', 'sv', sch.v('sv'))
        sch.op('scalar', lambda e: e.activation(out=scol('r2'), in_=scol('rc1'),
                                                func=AF.Sqrt), inc='ss')
        r2_ss[t] = sch.v('ss')

    def emit_n2(t):
        _par[0] = t % 2
        sch.wait('vector', 'ss', r2_ss[t])
        sch.wait('vector', 'st', qT_st.get(t - 2, 0))
        for sl in range(SL):
            sch.op('vector', lambda e, sl=sl: e.tensor_scalar(
                out=n2_tm[:, sl, :], in0=tmA[:, sl, :], scalar1=scol1('r2', sl),
                scalar2=None, op0=AL.mult), inc='sv' if sl == SL - 1 else None)
        sv_n2[t] = sch.v('sv')

    N2T_BANKS = (4, 5, 0, 1)

    def emit_n2T(t):
        n2e = {}
        for dc in range(DC):
            bank = pb[N2T_BANKS[dc % 4]]
            sch.wait('tensor', 'sv', sv_n2[t])
            if dc < 2:
                sch.wait('tensor', 'sv', x2_sv.get(t - 1, 0))
            elif dc >= 4:
                sch.wait('tensor', 'ss', n2e[dc - 4])
            for sl in range(SL):
                sch.op('tensor', lambda e, b=bank, dc=dc, sl=sl: e.transpose(
                    b[:, sl * 128:(sl + 1) * 128],
                    n2_tm[:, sl, dc * 128:(dc + 1) * 128], idf[:]),
                    inc='st' if sl == SL - 1 else None)
            stv = sch.v('st')
            sch.wait('scalar', 'st', stv)
            sch.op('scalar', lambda e, b=bank, dc=dc: e.activation(
                out=n2_fm[:, dc, :], in_=b[:, 0:TILE], func=AF.Copy,
                scale=ffw_v[:, dc:dc + 1]), inc='ss')
            n2e[dc] = sch.v('ss')
        n2T_st[t] = sch.v('st')
        n2e_ss[t] = n2e

    def emit_xau(t):
        _par[0] = t % 2
        xs = xbuf[t % 2]
        sch.wait('vector', 'ss', s1cc_ss[t])
        sch.wait('vector', 'st', uT_st.get(t - 1, 0))
        for sl in range(SL):
            sch.op('vector', lambda e, sl=sl, xs=xs: e.tensor_scalar(
                out=u_t[:, sl, :], in0=xs[:, sl, :], scalar1=scol1('m1', sl),
                scalar2=scol1('cc', sl), op0=AL.subtract, op1=AL.mult),
                inc='sv' if sl == SL - 1 else None)
        xau_sv[t] = sch.v('sv')

    UT_BANKS = (0, 1, 4, 5)

    def emit_scanblock(t):
        hb = hbuf[t % 2]
        hprev = hbuf[(t + 1) % 2]
        scan_dc = {}
        ud_dc = {}
        for dc in range(DC):
            bank = pb[UT_BANKS[dc % 4]]
            sch.wait('tensor', 'sv', xau_sv[t])
            if dc < 4:
                sch.wait('tensor', 'sv', y1_done.get(t - 1, 0))
                sch.wait('tensor', 'sv', x2_sv.get(t - 1, 0))
                if t >= 1:
                    sch.wait('tensor', 'ss', n2e_ss[t - 1][DC - 1])
            else:
                sch.wait('tensor', 'sv', scan_dc[dc - 4])
                sch.wait('tensor', 'ss', ud_dc[dc - 4])
            for sl in range(SL):
                sch.op('tensor', lambda e, b=bank, dc=dc, sl=sl: e.transpose(
                    b[:, sl * 128:(sl + 1) * 128],
                    u_t[:, sl, dc * 128:(dc + 1) * 128], idf[:]),
                    inc='st' if sl == SL - 1 else None)
            stv = sch.v('st')
            sch.wait('vector', 'st', stv)
            if t == 0:
                init_ap = carry0[:, dc:dc + 1]
            else:
                init_ap = hprev[:, dc, TILE - 1:TILE]
            sch.op('vector', lambda e, b=bank, dc=dc, ia=init_ap, hb=hb: e.tensor_tensor_scan(
                out=hb[:, dc, :], data0=a_v[:, dc:dc + 1].broadcast_to((128, TILE)),
                data1=b[:, 0:TILE], initial=ia, op0=AL.mult, op1=AL.add), inc='sv')
            scan_dc[dc] = sch.v('sv')
            sch.wait('scalar', 'sv', scan_dc[dc])
            sch.op('scalar', lambda e, b=bank, dc=dc: e.activation(
                out=ud_fm[:, dc, :], in_=b[:, 0:TILE], func=AF.Copy,
                scale=dw_v[:, dc:dc + 1]), inc='ss')
            ud_dc[dc] = sch.v('ss')
        scan_sv[t] = scan_dc[DC - 1]
        ud_ss[t] = ud_dc[DC - 1]
        uT_st[t] = sch.v('st')

    def emit_M1(t, mom2_tile=None):
        silu_hc = {}
        sch.wait('tensor', 'ss', n2e_ss[t][DC - 1])
        if t >= 1:
            sch.wait('tensor', 'ss', qe_ss[t - 1][DC - 1])
        for hp in range(HC // 2):
            bank = pb[2 + (hp % 2)]
            if hp >= 2:
                sch.wait('tensor', 'ss', silu_hc[2 * (hp - 2)])
            for sub in range(2):
                hc = 2 * hp + sub
                for dc in range(DC):
                    sch.op('tensor', lambda e, b=bank, hc=hc, sub=sub, dc=dc: e.matmul(
                        b[:, sub * TILE:(sub + 1) * TILE],
                        fcT[:, dc, hc * 128:(hc + 1) * 128],
                        n2_fm[:, dc, :], start=(dc == 0), stop=(dc == DC - 1)),
                        inc='st' if dc == DC - 1 else None)
            stv = sch.v('st')
            sch.wait('scalar', 'st', stv)
            sch.op('scalar', lambda e, b=bank, hp=hp: e.activation(
                out=hid_fm[:, 2 * hp:2 * hp + 2, :].rearrange('p a b -> p (a b)'),
                in_=b[:, 0:2 * TILE], func=AF.Silu), inc='ss')
            v = sch.v('ss')
            silu_hc[2 * hp] = v
            silu_hc[2 * hp + 1] = v
            if hp == 3 and mom2_tile is not None:
                emit_mom2(mom2_tile)
        silu_ss[t] = silu_hc

    def emit_sostt(t):
        hb = hbuf[t % 2]
        sch.wait('vector', 'ss', ud_ss[t])
        for dc in range(DC):
            sch.op('vector', lambda e, dc=dc, hb=hb: e.scalar_tensor_tensor(
                out=so_fm[:, dc, :], in0=hb[:, dc, :], scalar=cw_v[:, dc:dc + 1],
                in1=ud_fm[:, dc, :], op0=AL.mult, op1=AL.add),
                inc='sv' if dc == DC - 1 else None)
        sostt_sv[t] = sch.v('sv')

    def emit_M2x2(t):
        _par[0] = t % 2
        yb = ybuf[t % 2]
        sch.wait('tensor', 'sv', x2_sv.get(t - 1, 0))
        if t + 1 in scan_sv:
            sch.wait('tensor', 'sv', scan_sv[t + 1])
            sch.wait('tensor', 'ss', ud_ss[t + 1])
        for sl in range(SL):
            for hc in range(HC):
                if sl == 0:
                    sch.wait('tensor', 'ss', silu_ss[t][hc])
                for hf in range(2):
                    b = pb[4 + hf]
                    sch.op('tensor', lambda e, b=b, hc=hc, hf=hf, sl=sl: e.matmul(
                        b[:], hid_fm[:, hc, sl * 128:(sl + 1) * 128],
                        projT[:, hc, hf * 512:(hf + 1) * 512],
                        start=(hc == 0), stop=(hc == HC - 1)),
                        inc='st' if (hc == HC - 1 and hf == 1) else None)
            stv = sch.v('st')
            sch.wait('vector', 'st', stv)
            for hf in range(2):
                b = pb[4 + hf]
                sch.op('vector', lambda e, b=b, sl=sl, hf=hf, yb=yb: e.tensor_tensor(
                    out=x2_t[:, sl, hf * 512:(hf + 1) * 512], in0=b[:],
                    in1=yb[:, sl, hf * 512:(hf + 1) * 512], op=AL.add),
                    inc='sv' if hf == 1 else None)
            x2sl = sch.v('sv')
            sch.tiny_mode = True
            sch.op('vector', lambda e, sl=sl: e.reduce_max(
                out=scol1('mx', sl), in_=x2_t[:, sl, :], axis=AX.X))
            sch.op('vector', lambda e, sl=sl: e.tensor_reduce(
                out=scol1('mn', sl), in_=x2_t[:, sl, :], axis=AX.X, op=AL.min))
            sch.tiny_mode = False
            sch.wait('scalar', 'sv', x2sl)
            sch.op('scalar', lambda e, sl=sl: e.activation(
                out=trash_d[:, 0:D], in_=x2_t[:, sl, :], func=AF.Copy,
                accum_out=scol1('sx3', sl)))
            sch.op('scalar', lambda e, sl=sl: e.activation(
                out=trash_d[:, 0:D], in_=x2_t[:, sl, :], func=AF.Square,
                accum_out=scol1('sq3', sl)), inc='ss' if sl == SL - 1 else None)
            if sl == 0:
                sch.wait('tensor', 'sv', x2sl)
        x2_sv[t] = sch.v('sv')
        mom3_ss[t] = sch.v('ss')

    def emit_soT(t):
        sch.wait('tensor', 'sv', sostt_sv[t])
        sch.wait('tensor', 'sv', scan_sv[t])
        sch.wait('tensor', 'ss', ud_ss[t])
        sch.wait('tensor', 'sv', out_done.get(t - 2, 0))
        for sl in range(SL):
            banks = (pb[0], pb[1]) if sl == 0 else (pb[6], pb[7])
            for dc in range(DC):
                b = banks[dc // 4]
                r = (dc % 4) * 128
                sch.op('tensor', lambda e, b=b, r=r, dc=dc, sl=sl: e.transpose(
                    b[:, r:r + 128], so_fm[:, dc, sl * 128:(sl + 1) * 128], idf[:]),
                    inc='st' if (sl == SL - 1 and dc == DC - 1) else None)
        soT_st[t] = sch.v('st')

    def emit_x1y1(t):
        _par[0] = t % 2
        xs = xbuf[t % 2]
        yb = ybuf[t % 2]
        sch.wait('vector', 'st', soT_st[t])
        if t >= 2:
            sch.wait('vector', 'so', 32 * (t - 1))
        for sl in range(SL):
            banks = (pb[0], pb[1]) if sl == 0 else (pb[6], pb[7])
            sch.op('vector', lambda e, sl=sl, xs=xs: e.tensor_scalar(
                out=tmA[:, sl, :], in0=xs[:, sl, :], scalar1=scol1('m1', sl),
                scalar2=scol1('s1', sl), op0=AL.subtract, op1=AL.mult))
            for hf in range(2):
                b = banks[hf]
                sch.op('vector', lambda e, b=b, sl=sl, hf=hf: e.tensor_tensor(
                    out=tmA[:, sl, hf * 512:(hf + 1) * 512], in0=b[:],
                    in1=tmA[:, sl, hf * 512:(hf + 1) * 512], op=AL.add))
            sch.op('vector', lambda e, sl=sl, xs=xs, yb=yb: e.tensor_tensor(
                out=yb[:, sl, :], in0=tmA[:, sl, :], in1=xs[:, sl, :], op=AL.add),
                inc='sv')
        y1_done[t] = sch.v('sv')

    def emit_mom2(t):
        _par[0] = t % 2
        sch.wait('scalar', 'sv', y1_done[t])
        for sl in range(SL):
            sch.op('scalar', lambda e, sl=sl: e.activation(
                out=trash_d[:, 0:D], in_=tmA[:, sl, :], func=AF.Square,
                accum_out=scol1('sq2', sl)), inc='ss' if sl == SL - 1 else None)
        mom2_ss[t] = sch.v('ss')

    def emit_tiny3(t):
        _par[0] = t % 2
        sch.tiny_mode = True
        sch.wait('vector', 'ss', mom3_ss[t])
        sch.op('vector', lambda e: e.tensor_scalar(out=scol('m3', 4), in0=scol('sx3', 4),
                                                   scalar1=1.0 / D, scalar2=None, op0=AL.mult))
        sch.op('vector', lambda e: e.tensor_tensor(out=scol('rc2'), in0=scol('m3'),
                                                   in1=scol('m3'), op=AL.mult))
        sch.op('vector', lambda e: e.tensor_tensor(out=scol('var3'), in0=scol('msq3'),
                                                   in1=scol('rc2'), op=AL.subtract))
        sch.op('vector', lambda e: e.tensor_scalar(out=scol('w3'), in0=scol('var3'),
                                                   scalar1=EPS_LN, scalar2=None, op0=AL.add))
        sch.op('vector', lambda e: e.reciprocal(out=scol('rc1'), in_=scol('w3')), inc='sv')
        sch.wait('scalar', 'sv', sch.v('sv'))
        sch.op('scalar', lambda e: e.activation(out=scol('s3'), in_=scol('rc1'), func=AF.Sqrt),
               inc='ss')
        sch.wait('vector', 'ss', sch.v('ss'))
        sch.op('vector', lambda e: e.tensor_tensor(out=scol('rc1'), in0=scol('mx'),
                                                   in1=scol('m3'), op=AL.subtract))
        sch.op('vector', lambda e: e.tensor_tensor(out=scol('rc2'), in0=scol('m3'),
                                                   in1=scol('mn'), op=AL.subtract))
        sch.op('vector', lambda e: e.tensor_tensor(out=scol('nr1'), in0=scol('rc1'),
                                                   in1=scol('rc2'), op=AL.max))
        sch.op('vector', lambda e: e.tensor_tensor(out=scol('nr2'), in0=scol('nr1'),
                                                   in1=scol('s3'), op=AL.mult))
        sch.op('vector', lambda e: e.tensor_scalar(out=scol('az'), in0=scol('nr2'),
                                                   scalar1=1e-12, scalar2=None, op0=AL.max))
        sch.op('vector', lambda e: e.tensor_scalar(out=scol('ls'), in0=scol('az'),
                                                   scalar1=1.0 / 127.0, scalar2=None,
                                                   op0=AL.mult))
        sch.op('vector', lambda e: e.reciprocal(out=scol('rc1'), in_=scol('az')))
        sch.op('vector', lambda e: e.tensor_scalar(out=scol('rc1'), in0=scol('rc1'),
                                                   scalar1=127.0, scalar2=None, op0=AL.mult))
        sch.op('vector', lambda e: e.tensor_tensor(out=scol('c3'), in0=scol('rc1'),
                                                   in1=scol('s3'), op=AL.mult))
        sch.tiny_mode = False

    def emit_quant(t):
        _par[0] = t % 2
        sch.wait('vector', 'st', n2T_st.get(t + 1, 0))
        sch.wait('vector', 'ss', mom2_ss.get(t + 1, 0))
        for sl in range(SL):
            sch.op('vector', lambda e, sl=sl: e.tensor_scalar(
                out=tmA[:, sl, :], in0=x2_t[:, sl, :], scalar1=scol1('m3', sl),
                scalar2=scol1('c3', sl), op0=AL.subtract, op1=AL.mult))
            sch.op('vector', lambda e, sl=sl: e.tensor_scalar(
                out=tmA[:, sl, :], in0=tmA[:, sl, :], scalar1=127.0, scalar2=-127.0,
                op0=AL.min, op1=AL.max))
            sch.op('vector', lambda e, sl=sl: e.tensor_scalar(
                out=tmA[:, sl, :], in0=tmA[:, sl, :], scalar1=MAGIC, scalar2=MAGIC,
                op0=AL.add, op1=AL.subtract))
            sch.op('vector', lambda e, sl=sl: e.tensor_copy(out=qi_t[:], in_=tmA[:, sl, :]))
            sch.op('vector', lambda e, sl=sl: e.tensor_copy(out=q_tm[:, sl, :], in_=qi_t[:]),
                   inc='sv' if sl == SL - 1 else None)
        quant_sv[t] = sch.v('sv')

    def emit_qT(t):
        qe = {}
        sch.wait('tensor', 'sv', quant_sv[t])
        sch.wait('tensor', 'ss', silu_ss[t][HC - 1])
        for dp in range(DC // 2):
            bank = pb[2 + dp % 2]
            if dp >= 2:
                sch.wait('tensor', 'ss', qe[2 * (dp - 2)])
            for sub in range(2):
                dc = 2 * dp + sub
                for sl in range(SL):
                    sch.op('tensor', lambda e, b=bank, dc=dc, sub=sub, sl=sl: e.transpose(
                        b[:, sub * TILE + sl * 128:sub * TILE + (sl + 1) * 128],
                        q_tm[:, sl, dc * 128:(dc + 1) * 128], idf[:]),
                        inc='st' if sl == SL - 1 else None)
            stv = sch.v('st')
            sch.wait('scalar', 'st', stv)
            sch.op('scalar', lambda e, b=bank, dp=dp: e.activation(
                out=q_fm[:, 2 * dp:2 * dp + 2, :].rearrange('p a b -> p (a b)'),
                in_=b[:, 0:2 * TILE], func=AF.Copy), inc='ss')
            v = sch.v('ss')
            qe[2 * dp] = v
            qe[2 * dp + 1] = v
        qT_st[t] = sch.v('st')
        qe_ss[t] = qe

    def emit_M3out(t):
        _par[0] = t % 2
        yb = ybuf[t % 2]
        sch.wait('tensor', 'sv', out_done.get(t - 1, 0))
        sch.wait('tensor', 'sv', y1_done.get(t + 1, 0))
        for sl in range(SL):
            for dc in range(DC):
                if sl == 0:
                    sch.wait('tensor', 'ss', qe_ss[t][dc])
                for hf in range(2):
                    b = pb[6 + hf]
                    sch.op('tensor', lambda e, b=b, dc=dc, hf=hf, sl=sl: e.matmul(
                        b[:], q_fm[:, dc, sl * 128:(sl + 1) * 128],
                        qrhs[:, dc, hf * 512:(hf + 1) * 512],
                        start=(dc == 0), stop=(dc == DC - 1)),
                        inc='st' if (dc == DC - 1 and hf == 1) else None)
            stv = sch.v('st')
            sch.wait('vector', 'st', stv)
            for hf in range(2):
                b = pb[6 + hf]
                sch.op('vector', lambda e, b=b, sl=sl, hf=hf: e.scalar_tensor_tensor(
                    out=n2_tm[:, sl, hf * 512:(hf + 1) * 512], in0=b[:],
                    scalar=scol1('ls', sl), in1=rs_repl[:, hf * 512:(hf + 1) * 512],
                    op0=AL.mult, op1=AL.mult))
                sch.op('vector', lambda e, sl=sl, hf=hf, yb=yb: e.tensor_tensor(
                    out=yb[:, sl, hf * 512:(hf + 1) * 512],
                    in0=n2_tm[:, sl, hf * 512:(hf + 1) * 512],
                    in1=x2_t[:, sl, hf * 512:(hf + 1) * 512], op=AL.add),
                    inc='sv' if hf == 1 else None)
            out_sl.setdefault(t, {})[sl] = sch.v('sv')
            if sl == 0:
                sch.wait('tensor', 'sv', sch.v('sv'))
        out_done[t] = sch.v('sv')

    def emit_outdma(t):
        yb = ybuf[t % 2]
        for sl in range(SL):
            sch.wait('sync', 'sv', out_sl[t][sl])
            rows = out_d[t * TILE + sl * 128:t * TILE + (sl + 1) * 128, :]
            sch.op('sync', lambda e, r=rows, yb=yb, sl=sl: e.dma_start(
                out=r, in_=yb[:, sl, :]), inc='so', inc_n=16)

    def emit_prefetch(t):
        sch.wait('sync', 'sv', y1_done[t - 2])
        sch.op('sync', lambda e, t=t: e.dma_start(out=xbuf[t % 2][:], in_=x_ap(t)),
               inc='sx', inc_n=16)
        x_sx[t] = sch.v('sx')

    n_iters = nt + 2 if 'tiles' not in DEBUG_SKIP else 0
    if n_iters:
        emit_mom1(0)
        emit_tiny1(0)
        emit_xau(0)
        if nt > 1:
            emit_mom1(1)
    for it in range(n_iters):
        pre = it if it < nt else None
        mid = it - 1 if 1 <= it <= nt else None
        fin = it - 2 if it >= 2 else None
        if pre is not None:
            emit_scanblock(pre)
        if fin is not None:
            emit_qT(fin)
        if pre is not None:
            emit_sostt(pre)
        if fin is not None:
            emit_M3out(fin)
            emit_outdma(fin)
        if pre is not None and pre + 1 < nt:
            emit_tiny1(pre + 1)
        if pre is not None:
            emit_soT(pre)
        if pre is not None and pre + 1 < nt:
            emit_xau(pre + 1)
        if pre is not None:
            emit_x1y1(pre)
        if mid is not None:
            emit_M1(mid, pre)
        elif pre is not None:
            emit_mom2(pre)
        if pre is not None:
            emit_tiny2(pre)
            emit_n2(pre)
        if mid is not None:
            emit_M2x2(mid)
            emit_tiny3(mid)
        if pre is not None:
            emit_n2T(pre)
        if mid is not None:
            emit_quant(mid)
        if pre is not None:
            if pre + 2 < nt:
                emit_prefetch(pre + 2)
                emit_mom1(pre + 2)
    if DEBUG_DUMP:
        dbg = {'stats': stats, 'carry0': carry0, 'a_v': a_v, 'tmA': tmA,
               'xa_t': xa_t, 'ud_fm': ud_fm, 'h0': hbuf[0], 'so_fm': so_fm,
               'y0': ybuf[0], 'x2_t': x2_t, 'n2_tm': n2_tm, 'n2_fm': n2_fm,
               'hid_fm': hid_fm, 'q_fm': q_fm, 'qrhs': qrhs, 'rs_repl': rs_repl,
               'fcT': fcT, 'projT': projT, 'xb0': xbuf[0]}
        for s in ('sv', 'ss', 'st', 'sg'):
            sch.wait('sync', s, sch.v(s))
        for i, nm in enumerate(DEBUG_DUMP):
            t = dbg[nm]
            ncols = 1
            for d_ in t.shape[1:]:
                ncols *= d_
            ncols = min(ncols, D)
            ap = t[:].rearrange('p a b -> p (a b)') if len(t.shape) == 3 else t[:]
            sch.op('sync', lambda e, i=i, ap=ap, ncols=ncols: e.dma_start(
                out=out_d[i * 128:(i + 1) * 128, 0:ncols], in_=ap[:, 0:ncols]),
                inc='so', inc_n=16)

    sch.wait('sync', 'so', sch.v('so'))

    emit_block(mark, len(sch.items))
    for g in sem_ctxs:
        g.__exit__(None, None, None)
    for g in reversed(ctxs):
        g.__exit__(None, None, None)
    nc.finalize()
    return nc


def _vec_layout(v):
    return np.ascontiguousarray(v.reshape(DC, 128).T.astype(np.float32))


def kernel(L0_attn_x, L0_attn_input_norm_w, L0_attn_a_log, L0_attn_c_out,
           L0_attn_d_skip, L0_attn_ff_norm_w, L0_attn_ff_fc_w, L0_attn_ff_proj_w,
           L0_ffn_rhs):
    B, T, _ = L0_attn_x.shape
    half = T // 2
    x = np.ascontiguousarray(L0_attn_x.astype(np.float32, copy=False))

    fcT_h = np.ascontiguousarray(L0_attn_ff_fc_w.T.astype(np.float32, copy=False))
    projT_h = np.ascontiguousarray(L0_attn_ff_proj_w.T.astype(np.float32, copy=False))
    rhsT_h = np.ascontiguousarray(L0_ffn_rhs.T.astype(np.float32, copy=False))
    shared = {
        'fcT_h': fcT_h, 'projT_h': projT_h, 'rhsT_h': rhsT_h,
        'inw_h': _vec_layout(L0_attn_input_norm_w),
        'alog_h': _vec_layout(L0_attn_a_log),
        'cout_h': _vec_layout(L0_attn_c_out),
        'dskip_h': _vec_layout(L0_attn_d_skip),
        'ffw_h': _vec_layout(L0_attn_ff_norm_w),
        'idf_h': np.eye(128, dtype=np.float32),
        'idb_h': np.eye(128, dtype=ml_dtypes.bfloat16),
        'ones_h': np.ones((1, 128), np.float32),
        'iota_h': np.arange(PREV_T, dtype=np.float32)[None, :],
    }
    zeros_prev = np.zeros((PREV_T, D), np.float32)
    in_maps = []
    for c in range(N_CORES):
        b, hf = c // 2, c % 2
        xc_c = np.ascontiguousarray(x[b, hf * half:(hf + 1) * half])
        xp_c = zeros_prev if hf == 0 else np.ascontiguousarray(
            x[b, half - PREV_T:half])
        in_maps.append({'xc': xc_c, 'xp': xp_c, **shared})

    nc = build_program(T_CORE, PREV_T)
    res = run_bass_kernel_spmd(nc, in_maps, core_ids=list(range(N_CORES)))

    out = np.empty((B, T, D), np.float32)
    for c in range(N_CORES):
        b, hf = c // 2, c % 2
        out[b, hf * half:(hf + 1) * half] = res.results[c]['out']
    return out


# version 25
# speedup vs baseline: 1.0674x; 1.0244x over previous
import numpy as np
import ml_dtypes
import concourse.bass as bass
import concourse.mybir as mybir
import concourse.bass_utils as _bu
from concourse.bass_utils import run_bass_kernel_spmd

if not getattr(_bu, '_ldw_opt_patched', False):
    _orig_get_walrus_args = _bu.get_walrus_args

    def _get_walrus_args_ldw(arch, tmpdir, **kw):
        return _orig_get_walrus_args(arch, tmpdir, **kw) + ["--enable-ldw-opt=true"]

    _bu.get_walrus_args = _get_walrus_args_ldw
    _bu._ldw_opt_patched = True

F32 = mybir.dt.float32
BF16 = mybir.dt.bfloat16
I32 = mybir.dt.int32
AL = mybir.AluOpType
AX = mybir.AxisListType
AF = mybir.ActivationFunctionType

D = 1024
H = 2048
DC = 8
HC = 16
MAGIC = 12582912.0
EPS_LN = 1e-5
EPS_RMS = 1e-6
N_CORES = 8
T_CORE = 4096
TILE = 256
SL = TILE // 128
PREV_T = 1024
DEBUG_SKIP = set()
TILE_STOP = 99
DEBUG_DUMP = []


class Sched:

    def __init__(self, sem_names):
        self.items = []
        self.cnt = {s: 0 for s in sem_names}
        self.tiny_mode = False
        self.tiny_cnt = 0

    def op(self, eng, fn, inc=None, inc_n=1):
        if inc is not None:
            self.cnt[inc] += inc_n
        par = self.par_ref[0] if getattr(self, 'par_ref', None) is not None else 0
        tiny = self.tiny_mode and eng == 'vector'
        if tiny:
            self.tiny_cnt += 1
        self.items.append((eng, 'op', (fn, inc, inc_n, par, tiny, self.tiny_cnt)))

    def wait(self, eng, sem, val):
        self.items.append((eng, 'wait', (sem, val)))

    def v(self, sem):
        return self.cnt[sem]


def build_program(t_core=T_CORE, prev_t=PREV_T):
    nt = t_core // TILE
    n_prev = prev_t // TILE

    nc = bass.Bass('TRN2', target_bir_lowering=False, detect_race_conditions=False)

    xc = nc.dram_tensor('xc', [t_core, D], F32, kind='ExternalInput').ap()
    xp = nc.dram_tensor('xp', [prev_t, D], F32, kind='ExternalInput').ap()
    fcT_h = nc.dram_tensor('fcT_h', [D, H], F32, kind='ExternalInput').ap()
    projT_h = nc.dram_tensor('projT_h', [H, D], F32, kind='ExternalInput').ap()
    rhsT_h = nc.dram_tensor('rhsT_h', [D, D], F32, kind='ExternalInput').ap()
    inw_h = nc.dram_tensor('inw_h', [128, DC], F32, kind='ExternalInput').ap()
    alog_h = nc.dram_tensor('alog_h', [128, DC], F32, kind='ExternalInput').ap()
    cout_h = nc.dram_tensor('cout_h', [128, DC], F32, kind='ExternalInput').ap()
    dskip_h = nc.dram_tensor('dskip_h', [128, DC], F32, kind='ExternalInput').ap()
    ffw_h = nc.dram_tensor('ffw_h', [128, DC], F32, kind='ExternalInput').ap()
    idf_h = nc.dram_tensor('idf_h', [128, 128], F32, kind='ExternalInput').ap()
    idb_h = nc.dram_tensor('idb_h', [128, 128], BF16, kind='ExternalInput').ap()
    ones_h = nc.dram_tensor('ones_h', [1, 128], F32, kind='ExternalInput').ap()
    iota_h = nc.dram_tensor('iota_h', [1, prev_t], F32, kind='ExternalInput').ap()
    out_d = nc.dram_tensor('out', [t_core, D], F32, kind='ExternalOutput').ap()

    sems = ['sx', 'so', 'sg', 'sv', 'ss', 'st', 'svd']
    sch = Sched(sems)

    ctxs = []
    pro_ctxs = []

    def sb(name, shape, dt=F32, scope=None):
        g = nc.sbuf_tensor(name, shape, dt)
        t = g.__enter__()
        (ctxs if scope is None else scope).append(g)
        return t

    def psum(name, shape):
        g = nc.psum_tensor(name, shape, F32)
        t = g.__enter__()
        ctxs.append(g)
        return t

    fcT = sb('fcT', [128, DC, H], BF16)
    projT = sb('projT', [128, HC, D], BF16)
    qrhs = sb('qrhs', [128, DC, D], BF16)
    rs_repl = sb('rs_repl', [128, D], F32)
    idf = sb('idf', [128, 128], F32)
    idb = sb('idb', [128, 128], BF16)
    ones_c = sb('ones_c', [1, 128], F32)
    iota_r = sb('iota_r', [1, prev_t], F32)
    inw_v = sb('inw_v', [128, DC], F32)
    alog_v = sb('alog_v', [128, DC], F32)
    cout_v = sb('cout_v', [128, DC], F32)
    dskip_v = sb('dskip_v', [128, DC], F32)
    ffw_v = sb('ffw_v', [128, DC], F32)
    e_v = sb('e_v', [128, DC], F32)
    a_v = sb('a_v', [128, DC], F32)
    cw_v = sb('cw_v', [128, DC], F32)
    dw_v = sb('dw_v', [128, DC], F32)
    carry0 = sb('carry0', [128, DC], F32)
    carryB = sb('carryB', [128, DC], F32)
    rs_e = sb('rs_e', [128, DC], F32)
    rrs_e = sb('rrs_e', [128, DC], F32)

    pb = [psum('pb%d' % i, [128, 512]) for i in range(8)]

    xbuf = [sb('xb%d' % i, [128, SL, D], F32) for i in range(2)]
    qi_t = sb('qi_t', [128, D], I32)
    stats = sb('stats', [128, 128], F32)
    trash_s = sb('trash_s', [128, D], F32, scope=pro_ctxs)
    wscr = [sb('wscr%d' % i, [128, H], F32, scope=pro_ctxs) for i in range(2)]
    powt = sb('powt', [128, DC, prev_t], F32, scope=pro_ctxs)
    xpb = [sb('xpb%d' % i, [128, SL, D], F32, scope=pro_ctxs) for i in range(2)]
    rstg = [sb('rstg%d' % i, [128, D], F32, scope=pro_ctxs) for i in range(2)]
    upb = sb('upb', [128, SL, D], F32, scope=pro_ctxs)
    trash_f = sb('trash_f', [128, TILE], F32, scope=pro_ctxs)
    qrt = sb('qrt', [128, D], F32, scope=pro_ctxs)
    rowscr = sb('rowscr', [1, 128], F32, scope=pro_ctxs)

    ST = stats

    COL = {}
    _next = [0]
    _par = [0]
    sch.par_ref = _par

    def scol(name, n=SL):
        if name not in COL:
            COL[name] = _next[0]
            _next[0] += n
        c = COL[name] + 64 * _par[0]
        return ST[:, c:c + n]

    def scol1(name, sl):
        c = COL[name] + 64 * _par[0]
        return ST[:, c + sl:c + sl + 1]

    for nmm in ['sx1', 'sq1', 'm1', 'msq1', 'var1', 'w1', 'wc', 's1', 'cc',
                'sq2', 'msq2', 'w2', 'r2', 'sx3', 'sq3', 'm3', 'msq3', 'var3',
                'w3', 's3', 'mx', 'mn', 'az', 'ls', 'c3', 'nr1', 'nr2', 'rc1', 'rc2']:
        scol(nmm)

    def nr_polish(eng_ops, yname, wname, iters=2):
        assert iters % 2 == 0
        cols = [yname, 'nr2']
        for it in range(iters):
            t1, t2 = scol('nr1'), scol('nr1')
            ya, yb_ = scol(cols[it % 2]), scol(cols[(it + 1) % 2])
            w = scol(wname)
            sch.op('vector', lambda e, a=ya: e.tensor_tensor(out=scol('nr1'), in0=a, in1=a, op=AL.mult))
            sch.op('vector', lambda e, w=w: e.tensor_tensor(out=scol('rc1'), in0=scol('nr1'), in1=w, op=AL.mult))
            sch.op('vector', lambda e: e.tensor_scalar(out=scol('nr1'), in0=scol('rc1'), scalar1=-0.5,
                                                       scalar2=1.5, op0=AL.mult, op1=AL.add))
            sch.op('vector', lambda e, a=ya, b=yb_: e.tensor_tensor(out=b, in0=a, in1=scol('nr1'), op=AL.mult))

    def rsqrt_seq(wname, yname, inc_tag=None):
        w = scol(wname)
        y = scol(yname)
        rec = scol('nr2')
        sch.op('vector', lambda e: e.reciprocal(out=rec, in_=w), inc='sv')
        sch.wait('scalar', 'sv', sch.v('sv'))
        sch.op('scalar', lambda e: e.activation(out=y, in_=rec, func=AF.Sqrt), inc='ss')
        sch.wait('vector', 'ss', sch.v('ss'))

    def x_ap(j):
        return xc[j * TILE:(j + 1) * TILE, :].rearrange('(s p) d -> p s d', p=128)

    def o_ap(j):
        return out_d[j * TILE:(j + 1) * TILE, :].rearrange('(s p) d -> p s d', p=128)

    def xp_ap(j):
        return xp[j * TILE:(j + 1) * TILE, :].rearrange('(s p) d -> p s d', p=128)

    for dst, src in ((idf, idf_h), (idb, idb_h), (ones_c, ones_h), (iota_r, iota_h),
                     (inw_v, inw_h), (alog_v, alog_h), (cout_v, cout_h),
                     (dskip_v, dskip_h), (ffw_v, ffw_h)):
        sch.op('sync', lambda e, d=dst, s=src: e.dma_start(out=d[:], in_=s), inc='sx', inc_n=16)
    sx_consts = sch.v('sx')
    x_sx = {}
    sch.op('sync', lambda e: e.dma_start(out=xbuf[0][:], in_=x_ap(0)), inc='sx', inc_n=16)
    x_sx[0] = sch.v('sx')
    if nt > 1:
        sch.op('sync', lambda e: e.dma_start(out=xbuf[1][:], in_=x_ap(1)), inc='sx', inc_n=16)
        x_sx[1] = sch.v('sx')

    for hc in range(HC if 'projT' not in DEBUG_SKIP else 0):
        ap_in = projT_h[hc * 128:(hc + 1) * 128, :]
        sch.op('gpsimd', lambda e, a=ap_in, hc=hc: e.dma_start(
            out=projT[:, hc, :], in_=a), inc='sg', inc_n=16)
    sg_proj = sch.v('sg')

    sch.wait('scalar', 'sx', sx_consts)
    sch.op('scalar', lambda e: e.activation(out=e_v[:], in_=alog_v[:], func=AF.Exp))
    sch.op('scalar', lambda e: e.activation(out=a_v[:], in_=e_v[:], func=AF.Exp,
                                            scale=-1.0), inc='ss')
    ss_av = sch.v('ss')
    sch.wait('vector', 'sx', sx_consts)
    sch.tiny_mode = True
    sch.op('vector', lambda e: e.tensor_tensor(out=cw_v[:], in0=cout_v[:], in1=inw_v[:], op=AL.mult))
    sch.op('vector', lambda e: e.tensor_tensor(out=dw_v[:], in0=dskip_v[:], in1=inw_v[:], op=AL.mult))
    sch.op('vector', lambda e: e.memset(carry0[:], 0.0), inc='sv')
    sch.tiny_mode = False
    sv_vecs = sch.v('sv')

    n_half = (prev_t // 512) if 'powtile' not in DEBUG_SKIP else 0
    for hf in range(n_half):
        bank = pb[6 + (hf % 2)]
        sch.wait('tensor', 'sx', sx_consts)
        if hf >= 2:
            sch.wait('tensor', 'sv', sv_vecs + (hf - 1) * DC)
        sch.op('tensor', lambda e, b=bank, hf=hf: e.matmul(
            b[:], ones_c[:], iota_r[0:1, hf * 512:(hf + 1) * 512],
            start=True, stop=True), inc='st')
        st_iota = sch.v('st')
        sch.wait('vector', 'st', st_iota)
        sch.wait('vector', 'ss', ss_av)
        for dc in range(DC):
            if hf > 0 or dc >= 2:
                sch.wait('vector', 'ss', sch.v('ss'))
            sch.op('vector', lambda e, b=bank, dc=dc, hf=hf: e.tensor_scalar(
                out=wscr[0][:, (dc % 2) * 512:(dc % 2) * 512 + 512], in0=b[:],
                scalar1=float(prev_t - 1), scalar2=e_v[:, dc:dc + 1],
                op0=AL.subtract, op1=AL.mult), inc='sv')
            sch.wait('scalar', 'sv', sch.v('sv'))
            sch.op('scalar', lambda e, dc=dc, hf=hf: e.activation(
                out=powt[:, dc, hf * 512:(hf + 1) * 512],
                in_=wscr[0][:, (dc % 2) * 512:(dc % 2) * 512 + 512],
                func=AF.Exp), inc='ss')
    ss_pow = sch.v('ss')

    for dc in range(DC if 'fcT' not in DEBUG_SKIP else 0):
        ap_in = fcT_h[dc * 128:(dc + 1) * 128, :]
        sch.op('gpsimd', lambda e, a=ap_in, dc=dc: e.dma_start(
            out=fcT[:, dc, :], in_=a), inc='sg', inc_n=16)

    rstate = {'sv_q': sch.v('sv'), 'ev': []}

    def emit_rhsT_ec(ec):
        buf = rstg[ec % 2]
        if ec >= 2:
            sch.wait('sync', 'sv', rstate['sv_q'])
        ap_in = rhsT_h[ec * 128:(ec + 1) * 128, :]
        sch.op('sync', lambda e, a=ap_in, b=buf: e.dma_start(out=b[:], in_=a),
               inc='sx', inc_n=16)
        sxv = sch.v('sx')
        sch.wait('vector', 'sx', sxv)
        sch.tiny_mode = True
        sch.op('vector', lambda e, b=buf, ec=ec: e.reduce_max(
            out=rs_e[:, ec:ec + 1], in_=b[:, 0:D], axis=AX.X, apply_absolute_value=True))
        sch.op('vector', lambda e, ec=ec: e.tensor_scalar(
            out=rs_e[:, ec:ec + 1], in0=rs_e[:, ec:ec + 1], scalar1=1e-12,
            scalar2=1.0 / 127.0, op0=AL.max, op1=AL.mult))
        sch.op('vector', lambda e, ec=ec: e.reciprocal(
            out=rrs_e[:, ec:ec + 1], in_=rs_e[:, ec:ec + 1]))
        sch.tiny_mode = False
        sch.op('vector', lambda e, b=buf, ec=ec: e.tensor_scalar(
            out=b[:, 0:D], in0=b[:, 0:D], scalar1=rrs_e[:, ec:ec + 1], scalar2=None,
            op0=AL.mult))
        sch.op('vector', lambda e, b=buf: e.tensor_scalar(
            out=b[:, 0:D], in0=b[:, 0:D], scalar1=127.0, scalar2=-127.0,
            op0=AL.min, op1=AL.max))
        sch.op('vector', lambda e, b=buf: e.tensor_scalar(
            out=b[:, 0:D], in0=b[:, 0:D], scalar1=MAGIC, scalar2=MAGIC,
            op0=AL.add, op1=AL.subtract))
        sch.op('vector', lambda e, b=buf: e.tensor_copy(out=qi_t[:], in_=b[:, 0:D]))
        sch.op('vector', lambda e: e.tensor_copy(out=qrt[:], in_=qi_t[:]), inc='sv')
        sv_q = sch.v('sv')
        for dc in range(DC):
            bank = pb[dc % 2]
            i = ec * DC + dc
            if i >= 2:
                sch.wait('tensor', 'ss', rstate['ev'][i - 2])
            sch.wait('tensor', 'sv', sv_q)
            sch.op('tensor', lambda e, b=bank, dc=dc: e.transpose(
                b[:, 0:128], qrt[:, dc * 128:(dc + 1) * 128], idf[:]), inc='st')
            stv = sch.v('st')
            sch.wait('scalar', 'st', stv)
            sch.op('scalar', lambda e, b=bank, dc=dc, ec=ec: e.activation(
                out=qrhs[:, dc, ec * 128:(ec + 1) * 128], in_=b[:, 0:128],
                func=AF.Copy), inc='ss')
            rstate['ev'].append(sch.v('ss'))
        sch.wait('tensor', 'ss', rstate['ev'][-1])
        sch.op('tensor', lambda e, ec=ec: e.transpose(
            pb[2][0:1, 0:128], rs_e[:, ec:ec + 1], idf[:]), inc='st')
        stv = sch.v('st')
        sch.wait('vector', 'st', stv)
        sch.op('vector', lambda e: e.tensor_copy(out=rowscr[:], in_=pb[2][0:1, 0:128]),
               inc='sv')
        svv = sch.v('sv')
        sch.wait('tensor', 'sv', svv)
        sch.op('tensor', lambda e: e.matmul(pb[3][:, 0:128], ones_c[:], rowscr[:],
                                            start=True, stop=True), inc='st')
        stv = sch.v('st')
        sch.wait('vector', 'st', stv)
        sch.op('vector', lambda e, ec=ec: e.tensor_copy(
            out=rs_repl[:, ec * 128:(ec + 1) * 128], in_=pb[3][:, 0:128]), inc='sv')
        rstate['sv_q'] = sch.v('sv')

    cstate = {'cur': carry0, 'alt': carryB}

    def emit_carry_mt(mt):
        carry_cur, carry_alt = cstate['cur'], cstate['alt']
        xpc = xpb[mt % 2]
        sch.wait('scalar', 'sx', xp_sx[mt])
        for sl in range(SL):
            sch.op('scalar', lambda e, sl=sl, xpc=xpc: e.activation(
                out=trash_s[:], in_=xpc[:, sl, :], func=AF.Copy,
                accum_out=scol1('sx1', sl)))
            sch.op('scalar', lambda e, sl=sl, xpc=xpc: e.activation(
                out=trash_s[:], in_=xpc[:, sl, :], func=AF.Square,
                accum_out=scol1('sq1', sl)), inc='ss' if sl == SL - 1 else None)
        ssv = sch.v('ss')
        sch.wait('vector', 'ss', ssv)
        sch.tiny_mode = True
        sch.op('vector', lambda e: e.tensor_scalar(out=scol('m1', 4), in0=scol('sx1', 4),
                                                   scalar1=1.0 / D, scalar2=None, op0=AL.mult))
        sch.op('vector', lambda e: e.tensor_tensor(out=scol('rc1'), in0=scol('m1'),
                                                   in1=scol('m1'), op=AL.mult))
        sch.op('vector', lambda e: e.tensor_tensor(out=scol('var1'), in0=scol('msq1'),
                                                   in1=scol('rc1'), op=AL.subtract))
        sch.op('vector', lambda e: e.tensor_scalar(out=scol('wc'), in0=scol('var1'),
                                                   scalar1=1.0 + EPS_RMS,
                                                   scalar2=EPS_RMS * EPS_LN,
                                                   op0=AL.mult, op1=AL.add))
        rsqrt_seq('wc', 'cc')
        sch.tiny_mode = False
        for sl in range(SL):
            sch.op('vector', lambda e, sl=sl, xpc=xpc: e.tensor_scalar(
                out=upb[:, sl, :], in0=xpc[:, sl, :], scalar1=scol1('m1', sl),
                scalar2=scol1('cc', sl), op0=AL.subtract, op1=AL.mult),
                inc='sv' if sl == SL - 1 else None)
        sv_u = sch.v('sv')
        for dc in range(DC):
            bank, reg = pb[4 + dc % 4], 0
            if dc >= 4:
                sch.wait('tensor', 'sv', sch.v('sv'))
            sch.wait('tensor', 'sv', sv_u)
            for sl in range(SL):
                sch.op('tensor', lambda e, b=bank, r=reg, dc=dc, sl=sl: e.transpose(
                    b[:, r + sl * 128:r + (sl + 1) * 128],
                    upb[:, sl, dc * 128:(dc + 1) * 128], idf[:]),
                    inc='st' if sl == SL - 1 else None)
            stv = sch.v('st')
            sch.wait('vector', 'st', stv)
            sch.op('vector', lambda e, b=bank, r=reg, dc=dc, mt=mt: e.tensor_tensor(
                out=trash_f[:, 0:TILE], in0=b[:, r:r + TILE],
                in1=powt[:, dc, mt * TILE:(mt + 1) * TILE], op=AL.mult))
            sch.tiny_mode = True
            sch.op('vector', lambda e: e.reduce_sum(
                out=scol('nr2')[:, 0:1], in_=trash_f[:, 0:TILE], axis=AX.X))
            sch.op('vector', lambda e, dc=dc, cc_=carry_cur, ca=carry_alt: e.tensor_tensor(
                out=ca[:, dc:dc + 1], in0=cc_[:, dc:dc + 1],
                in1=scol('nr2')[:, 0:1], op=AL.add), inc='sv')
            sch.tiny_mode = False
        cstate['cur'], cstate['alt'] = carry_alt, carry_cur

    n_mt = n_prev if 'carry' not in DEBUG_SKIP else 0
    n_ec = DC if 'qrhs' not in DEBUG_SKIP else 0
    xp_sx = {}
    for mt in range(min(n_mt, 2)):
        sch.op('sync', lambda e, mt=mt: e.dma_start(out=xpb[mt % 2][:], in_=xp_ap(mt)),
               inc='sx', inc_n=16)
        xp_sx[mt] = sch.v('sx')
    for k in range(max(n_ec, n_mt)):
        if k < n_ec:
            emit_rhsT_ec(k)
        if k < n_mt:
            emit_carry_mt(k)
            if k + 2 < n_mt:
                sch.wait('sync', 'sv', sch.v('sv'))
                sch.op('sync', lambda e, mt=k + 2: e.dma_start(
                    out=xpb[mt % 2][:], in_=xp_ap(mt)), inc='sx', inc_n=16)
                xp_sx[k + 2] = sch.v('sx')
    if cstate['cur'] is not carry0 and n_mt:
        sch.tiny_mode = True
        sch.op('vector', lambda e: e.tensor_copy(out=carry0[:], in_=carryB[:]), inc='sv')
        sch.tiny_mode = False
    sv_carry = sch.v('sv')

    PRO = dict(sv=sch.v('sv'), ss=sch.v('ss'), st=sch.v('st'),
               sx=sch.v('sx'), sg=sch.v('sg'))
    for eng in ('sync', 'vector', 'scalar', 'tensor'):
        for s in ('sv', 'ss', 'st', 'sg'):
            sch.wait(eng, s, PRO[s])

    mark = len(sch.items)
    sem_handles = {}
    sem_ctxs = []
    for s in sems:
        g = nc.semaphore(s)
        sem_handles[s] = g.__enter__()
        sem_ctxs.append(g)

    def make_engine_fn(name, lo, hi):
        def fn(eng):
            for (e, kind, payload) in sch.items[lo:hi]:
                if e != name:
                    continue
                if kind == 'wait':
                    s, v = payload
                    if v > 0:
                        eng.wait_ge(sem_handles[s], v)
                else:
                    f, inc, inc_n, par, tiny, tc = payload
                    _par[0] = par
                    if name == 'vector' and tc > 0:
                        eng.wait_ge(sem_handles['svd'], tc - (1 if tiny else 0))
                    ins = f(eng)
                    if inc is not None:
                        ins.then_inc(sem_handles[inc], inc_n)
                        if tiny:
                            eng.engine_nop().then_inc(sem_handles['svd'], 1)
                    elif tiny:
                        ins.then_inc(sem_handles['svd'], 1)
        return fn

    def emit_block(lo, hi):
        g = nc.Block()
        blk = g.__enter__()
        blk.sync(make_engine_fn('sync', lo, hi))
        blk.vector(make_engine_fn('vector', lo, hi))
        blk.scalar(make_engine_fn('scalar', lo, hi))
        blk.tensor(make_engine_fn('tensor', lo, hi))
        blk.gpsimd(make_engine_fn('gpsimd', lo, hi))
        g.__exit__(None, None, None)

    emit_block(0, mark)
    for g in reversed(pro_ctxs):
        g.__exit__(None, None, None)

    tmA = sb('tmA', [128, SL, D], F32)
    u_t = sb('u_t', [128, SL, D], F32)
    ud_fm = sb('ud_fm', [128, DC, TILE], F32)
    trash_d = ud_fm[:].rearrange('p a b -> p (a b)')
    hbuf = [sb('hb%d' % i, [128, DC, TILE], F32) for i in range(2)]
    so_fm = sb('so_fm', [128, DC, TILE], F32)
    ybuf = [sb('yb%d' % i, [128, SL, D], F32) for i in range(2)]
    x2_t = sb('x2_t', [128, SL, D], F32)
    n2_tm = sb('n2_tm', [128, SL, D], F32)
    n2_fm = sb('n2_fm', [128, DC, TILE], BF16)
    hid_fm = sb('hid_fm', [128, HC, TILE], BF16)
    q_tm = n2_tm
    q_fm = sb('q_fm', [128, DC, TILE], BF16)

    y1_done = {}
    out_done = {}
    x2_sv = {}
    quant_sv = {}
    sostt_sv = {}
    scan_sv = {}
    xau_sv = {}
    mom1_ss = {}
    mom2_ss = {}
    mom3_ss = {}
    s1cc_ss = {}
    r2_ss = {}
    n2e_ss = {}
    ud_ss = {}
    silu_ss = {}
    qe_ss = {}
    sv_n2 = {}
    n2T_st = {}
    soT_st = {}
    qT_st = {}
    uT_st = {}
    out_sl = {}

    def emit_mom1(t):
        _par[0] = t % 2
        xs = xbuf[t % 2]
        sch.wait('scalar', 'sx', x_sx[t])
        for sl in range(SL):
            sch.op('scalar', lambda e, sl=sl, xs=xs: e.activation(
                out=trash_d[:, 0:D], in_=xs[:, sl, :], func=AF.Copy, accum_out=scol1('sx1', sl)))
            sch.op('scalar', lambda e, sl=sl, xs=xs: e.activation(
                out=trash_d[:, 0:D], in_=xs[:, sl, :], func=AF.Square, accum_out=scol1('sq1', sl)),
                inc='ss' if sl == SL - 1 else None)
        mom1_ss[t] = sch.v('ss')

    def emit_tiny1(t):
        _par[0] = t % 2
        sch.wait('vector', 'ss', mom1_ss[t])
        sch.tiny_mode = True
        sch.op('vector', lambda e: e.tensor_scalar(out=scol('m1', 4), in0=scol('sx1', 4),
                                                   scalar1=1.0 / D, scalar2=None, op0=AL.mult))
        sch.op('vector', lambda e: e.tensor_tensor(out=scol('rc1'), in0=scol('m1'),
                                                   in1=scol('m1'), op=AL.mult))
        sch.op('vector', lambda e: e.tensor_tensor(out=scol('var1'), in0=scol('msq1'),
                                                   in1=scol('rc1'), op=AL.subtract))
        sch.op('vector', lambda e: e.tensor_scalar(out=scol('w1'), in0=scol('var1'),
                                                   scalar1=EPS_LN, scalar2=None, op0=AL.add))
        sch.op('vector', lambda e: e.tensor_scalar(out=scol('wc'), in0=scol('var1'),
                                                   scalar1=1.0 + EPS_RMS,
                                                   scalar2=EPS_RMS * EPS_LN,
                                                   op0=AL.mult, op1=AL.add))
        sch.op('vector', lambda e: e.reciprocal(out=scol('rc1', 4), in_=scol('w1', 4)),
               inc='sv')
        sch.tiny_mode = False
        sch.wait('scalar', 'sv', sch.v('sv'))
        sch.op('scalar', lambda e: e.activation(out=scol('s1', 4), in_=scol('rc1', 4),
                                                func=AF.Sqrt), inc='ss')
        s1cc_ss[t] = sch.v('ss')

    def emit_tiny2(t):
        _par[0] = t % 2
        sch.wait('vector', 'ss', mom2_ss[t])
        sch.tiny_mode = True
        sch.op('vector', lambda e: e.tensor_scalar(out=scol('w2'), in0=scol('sq2'),
                                                   scalar1=1.0 / D, scalar2=EPS_RMS,
                                                   op0=AL.mult, op1=AL.add))
        sch.op('vector', lambda e: e.reciprocal(out=scol('rc1'), in_=scol('w2')), inc='sv')
        sch.tiny_mode = False
        sch.wait('scalar', 'sv', sch.v('sv'))
        sch.op('scalar', lambda e: e.activation(out=scol('r2'), in_=scol('rc1'),
                                                func=AF.Sqrt), inc='ss')
        r2_ss[t] = sch.v('ss')

    def emit_n2(t):
        _par[0] = t % 2
        sch.wait('vector', 'ss', r2_ss[t])
        sch.wait('vector', 'st', qT_st.get(t - 2, 0))
        for sl in range(SL):
            sch.op('vector', lambda e, sl=sl: e.tensor_scalar(
                out=n2_tm[:, sl, :], in0=tmA[:, sl, :], scalar1=scol1('r2', sl),
                scalar2=None, op0=AL.mult), inc='sv' if sl == SL - 1 else None)
        sv_n2[t] = sch.v('sv')

    N2T_BANKS = (4, 5, 0, 1)

    def emit_n2T(t):
        n2e = {}
        for dc in range(DC):
            bank = pb[N2T_BANKS[dc % 4]]
            sch.wait('tensor', 'sv', sv_n2[t])
            if dc < 2:
                sch.wait('tensor', 'sv', x2_sv.get(t - 1, 0))
            elif dc >= 4:
                sch.wait('tensor', 'ss', n2e[dc - 4])
            for sl in range(SL):
                sch.op('tensor', lambda e, b=bank, dc=dc, sl=sl: e.transpose(
                    b[:, sl * 128:(sl + 1) * 128],
                    n2_tm[:, sl, dc * 128:(dc + 1) * 128], idf[:]),
                    inc='st' if sl == SL - 1 else None)
            stv = sch.v('st')
            sch.wait('scalar', 'st', stv)
            sch.op('scalar', lambda e, b=bank, dc=dc: e.activation(
                out=n2_fm[:, dc, :], in_=b[:, 0:TILE], func=AF.Copy,
                scale=ffw_v[:, dc:dc + 1]), inc='ss')
            n2e[dc] = sch.v('ss')
        n2T_st[t] = sch.v('st')
        n2e_ss[t] = n2e

    def emit_xau(t):
        _par[0] = t % 2
        xs = xbuf[t % 2]
        sch.wait('vector', 'ss', s1cc_ss[t])
        sch.wait('vector', 'st', uT_st.get(t - 1, 0))
        for sl in range(SL):
            sch.op('vector', lambda e, sl=sl, xs=xs: e.tensor_scalar(
                out=u_t[:, sl, :], in0=xs[:, sl, :], scalar1=scol1('m1', sl),
                scalar2=scol1('cc', sl), op0=AL.subtract, op1=AL.mult),
                inc='sv' if sl == SL - 1 else None)
        xau_sv[t] = sch.v('sv')

    UT_BANKS = (0, 1, 4, 5)

    def emit_scanblock(t):
        hb = hbuf[t % 2]
        hprev = hbuf[(t + 1) % 2]
        scan_dc = {}
        ud_dc = {}
        for dc in range(DC):
            bank = pb[UT_BANKS[dc % 4]]
            sch.wait('tensor', 'sv', xau_sv[t])
            if dc < 4:
                sch.wait('tensor', 'sv', y1_done.get(t - 1, 0))
                sch.wait('tensor', 'sv', x2_sv.get(t - 1, 0))
                if t >= 1:
                    sch.wait('tensor', 'ss', n2e_ss[t - 1][DC - 1])
            else:
                sch.wait('tensor', 'sv', scan_dc[dc - 4])
                sch.wait('tensor', 'ss', ud_dc[dc - 4])
            for sl in range(SL):
                sch.op('tensor', lambda e, b=bank, dc=dc, sl=sl: e.transpose(
                    b[:, sl * 128:(sl + 1) * 128],
                    u_t[:, sl, dc * 128:(dc + 1) * 128], idf[:]),
                    inc='st' if sl == SL - 1 else None)
            stv = sch.v('st')
            sch.wait('vector', 'st', stv)
            if t == 0:
                init_ap = carry0[:, dc:dc + 1]
            else:
                init_ap = hprev[:, dc, TILE - 1:TILE]
            sch.op('vector', lambda e, b=bank, dc=dc, ia=init_ap, hb=hb: e.tensor_tensor_scan(
                out=hb[:, dc, :], data0=a_v[:, dc:dc + 1].broadcast_to((128, TILE)),
                data1=b[:, 0:TILE], initial=ia, op0=AL.mult, op1=AL.add), inc='sv')
            scan_dc[dc] = sch.v('sv')
            sch.wait('scalar', 'sv', scan_dc[dc])
            sch.op('scalar', lambda e, b=bank, dc=dc: e.activation(
                out=ud_fm[:, dc, :], in_=b[:, 0:TILE], func=AF.Copy,
                scale=dw_v[:, dc:dc + 1]), inc='ss')
            ud_dc[dc] = sch.v('ss')
        scan_sv[t] = scan_dc[DC - 1]
        ud_ss[t] = ud_dc[DC - 1]
        uT_st[t] = sch.v('st')

    def emit_M1(t, mom2_tile=None):
        silu_hc = {}
        sch.wait('tensor', 'ss', n2e_ss[t][DC - 1])
        if t >= 1:
            sch.wait('tensor', 'ss', qe_ss[t - 1][DC - 1])
        for hp in range(HC // 2):
            bank = pb[2 + (hp % 2)]
            if hp >= 2:
                sch.wait('tensor', 'ss', silu_hc[2 * (hp - 2)])
            for sub in range(2):
                hc = 2 * hp + sub
                for dc in range(DC):
                    sch.op('tensor', lambda e, b=bank, hc=hc, sub=sub, dc=dc: e.matmul(
                        b[:, sub * TILE:(sub + 1) * TILE],
                        fcT[:, dc, hc * 128:(hc + 1) * 128],
                        n2_fm[:, dc, :], start=(dc == 0), stop=(dc == DC - 1)),
                        inc='st' if dc == DC - 1 else None)
            stv = sch.v('st')
            sch.wait('scalar', 'st', stv)
            sch.op('scalar', lambda e, b=bank, hp=hp: e.activation(
                out=hid_fm[:, 2 * hp:2 * hp + 2, :].rearrange('p a b -> p (a b)'),
                in_=b[:, 0:2 * TILE], func=AF.Silu), inc='ss')
            v = sch.v('ss')
            silu_hc[2 * hp] = v
            silu_hc[2 * hp + 1] = v
            if hp == 5 and mom2_tile is not None:
                emit_mom2(mom2_tile)
        silu_ss[t] = silu_hc

    def emit_sostt(t):
        hb = hbuf[t % 2]
        sch.wait('vector', 'ss', ud_ss[t])
        for dc in range(DC):
            sch.op('vector', lambda e, dc=dc, hb=hb: e.scalar_tensor_tensor(
                out=so_fm[:, dc, :], in0=hb[:, dc, :], scalar=cw_v[:, dc:dc + 1],
                in1=ud_fm[:, dc, :], op0=AL.mult, op1=AL.add),
                inc='sv' if dc == DC - 1 else None)
        sostt_sv[t] = sch.v('sv')

    def emit_M2x2(t):
        _par[0] = t % 2
        yb = ybuf[t % 2]
        sch.wait('tensor', 'sv', x2_sv.get(t - 1, 0))
        if t + 1 in scan_sv:
            sch.wait('tensor', 'sv', scan_sv[t + 1])
            sch.wait('tensor', 'ss', ud_ss[t + 1])
        for sl in range(SL):
            for hc in range(HC):
                if sl == 0:
                    sch.wait('tensor', 'ss', silu_ss[t][hc])
                for hf in range(2):
                    b = pb[4 + hf]
                    sch.op('tensor', lambda e, b=b, hc=hc, hf=hf, sl=sl: e.matmul(
                        b[:], hid_fm[:, hc, sl * 128:(sl + 1) * 128],
                        projT[:, hc, hf * 512:(hf + 1) * 512],
                        start=(hc == 0), stop=(hc == HC - 1)),
                        inc='st' if (hc == HC - 1 and hf == 1) else None)
            stv = sch.v('st')
            sch.wait('vector', 'st', stv)
            for hf in range(2):
                b = pb[4 + hf]
                sch.op('vector', lambda e, b=b, sl=sl, hf=hf, yb=yb: e.tensor_tensor(
                    out=x2_t[:, sl, hf * 512:(hf + 1) * 512], in0=b[:],
                    in1=yb[:, sl, hf * 512:(hf + 1) * 512], op=AL.add),
                    inc='sv' if hf == 1 else None)
            x2sl = sch.v('sv')
            sch.tiny_mode = True
            sch.op('vector', lambda e, sl=sl: e.reduce_max(
                out=scol1('mx', sl), in_=x2_t[:, sl, :], axis=AX.X))
            sch.op('vector', lambda e, sl=sl: e.tensor_reduce(
                out=scol1('mn', sl), in_=x2_t[:, sl, :], axis=AX.X, op=AL.min))
            sch.tiny_mode = False
            sch.wait('scalar', 'sv', x2sl)
            sch.op('scalar', lambda e, sl=sl: e.activation(
                out=trash_d[:, 0:D], in_=x2_t[:, sl, :], func=AF.Copy,
                accum_out=scol1('sx3', sl)))
            sch.op('scalar', lambda e, sl=sl: e.activation(
                out=trash_d[:, 0:D], in_=x2_t[:, sl, :], func=AF.Square,
                accum_out=scol1('sq3', sl)), inc='ss' if sl == SL - 1 else None)
            if sl == 0:
                sch.wait('tensor', 'sv', x2sl)
        x2_sv[t] = sch.v('sv')
        mom3_ss[t] = sch.v('ss')

    def emit_soT(t):
        sch.wait('tensor', 'sv', sostt_sv[t])
        sch.wait('tensor', 'sv', scan_sv[t])
        sch.wait('tensor', 'ss', ud_ss[t])
        sch.wait('tensor', 'sv', out_done.get(t - 2, 0))
        for sl in range(SL):
            banks = (pb[0], pb[1]) if sl == 0 else (pb[6], pb[7])
            for dc in range(DC):
                b = banks[dc // 4]
                r = (dc % 4) * 128
                sch.op('tensor', lambda e, b=b, r=r, dc=dc, sl=sl: e.transpose(
                    b[:, r:r + 128], so_fm[:, dc, sl * 128:(sl + 1) * 128], idf[:]),
                    inc='st' if (sl == SL - 1 and dc == DC - 1) else None)
        soT_st[t] = sch.v('st')

    def emit_x1y1(t):
        _par[0] = t % 2
        xs = xbuf[t % 2]
        yb = ybuf[t % 2]
        sch.wait('vector', 'st', soT_st[t])
        if t >= 2:
            sch.wait('vector', 'so', 32 * (t - 1))
        for sl in range(SL):
            banks = (pb[0], pb[1]) if sl == 0 else (pb[6], pb[7])
            sch.op('vector', lambda e, sl=sl, xs=xs: e.tensor_scalar(
                out=tmA[:, sl, :], in0=xs[:, sl, :], scalar1=scol1('m1', sl),
                scalar2=scol1('s1', sl), op0=AL.subtract, op1=AL.mult))
            for hf in range(2):
                b = banks[hf]
                sch.op('vector', lambda e, b=b, sl=sl, hf=hf: e.tensor_tensor(
                    out=tmA[:, sl, hf * 512:(hf + 1) * 512], in0=b[:],
                    in1=tmA[:, sl, hf * 512:(hf + 1) * 512], op=AL.add))
            sch.op('vector', lambda e, sl=sl, xs=xs, yb=yb: e.tensor_tensor(
                out=yb[:, sl, :], in0=tmA[:, sl, :], in1=xs[:, sl, :], op=AL.add),
                inc='sv')
        y1_done[t] = sch.v('sv')

    def emit_mom2(t):
        _par[0] = t % 2
        sch.wait('scalar', 'sv', y1_done[t])
        for sl in range(SL):
            sch.op('scalar', lambda e, sl=sl: e.activation(
                out=trash_d[:, 0:D], in_=tmA[:, sl, :], func=AF.Square,
                accum_out=scol1('sq2', sl)), inc='ss' if sl == SL - 1 else None)
        mom2_ss[t] = sch.v('ss')

    def emit_tiny3(t):
        _par[0] = t % 2
        sch.tiny_mode = True
        sch.wait('vector', 'ss', mom3_ss[t])
        sch.op('vector', lambda e: e.tensor_scalar(out=scol('m3', 4), in0=scol('sx3', 4),
                                                   scalar1=1.0 / D, scalar2=None, op0=AL.mult))
        sch.op('vector', lambda e: e.tensor_tensor(out=scol('rc2'), in0=scol('m3'),
                                                   in1=scol('m3'), op=AL.mult))
        sch.op('vector', lambda e: e.tensor_tensor(out=scol('var3'), in0=scol('msq3'),
                                                   in1=scol('rc2'), op=AL.subtract))
        sch.op('vector', lambda e: e.tensor_scalar(out=scol('w3'), in0=scol('var3'),
                                                   scalar1=EPS_LN, scalar2=None, op0=AL.add))
        sch.op('vector', lambda e: e.reciprocal(out=scol('rc1'), in_=scol('w3')), inc='sv')
        sch.wait('scalar', 'sv', sch.v('sv'))
        sch.op('scalar', lambda e: e.activation(out=scol('s3'), in_=scol('rc1'), func=AF.Sqrt),
               inc='ss')
        sch.wait('vector', 'ss', sch.v('ss'))
        sch.op('vector', lambda e: e.tensor_tensor(out=scol('rc1'), in0=scol('mx'),
                                                   in1=scol('m3'), op=AL.subtract))
        sch.op('vector', lambda e: e.tensor_tensor(out=scol('rc2'), in0=scol('m3'),
                                                   in1=scol('mn'), op=AL.subtract))
        sch.op('vector', lambda e: e.tensor_tensor(out=scol('nr1'), in0=scol('rc1'),
                                                   in1=scol('rc2'), op=AL.max))
        sch.op('vector', lambda e: e.tensor_tensor(out=scol('nr2'), in0=scol('nr1'),
                                                   in1=scol('s3'), op=AL.mult))
        sch.op('vector', lambda e: e.tensor_scalar(out=scol('az'), in0=scol('nr2'),
                                                   scalar1=1e-12, scalar2=None, op0=AL.max))
        sch.op('vector', lambda e: e.tensor_scalar(out=scol('ls'), in0=scol('az'),
                                                   scalar1=1.0 / 127.0, scalar2=None,
                                                   op0=AL.mult))
        sch.op('vector', lambda e: e.reciprocal(out=scol('rc1'), in_=scol('az')))
        sch.op('vector', lambda e: e.tensor_scalar(out=scol('rc1'), in0=scol('rc1'),
                                                   scalar1=127.0, scalar2=None, op0=AL.mult))
        sch.op('vector', lambda e: e.tensor_tensor(out=scol('c3'), in0=scol('rc1'),
                                                   in1=scol('s3'), op=AL.mult))
        sch.tiny_mode = False

    def emit_quant(t):
        _par[0] = t % 2
        sch.wait('vector', 'st', n2T_st.get(t + 1, 0))
        sch.wait('vector', 'ss', mom2_ss.get(t + 1, 0))
        for sl in range(SL):
            sch.op('vector', lambda e, sl=sl: e.tensor_scalar(
                out=tmA[:, sl, :], in0=x2_t[:, sl, :], scalar1=scol1('m3', sl),
                scalar2=scol1('c3', sl), op0=AL.subtract, op1=AL.mult))
            sch.op('vector', lambda e, sl=sl: e.tensor_scalar(
                out=tmA[:, sl, :], in0=tmA[:, sl, :], scalar1=127.0, scalar2=-127.0,
                op0=AL.min, op1=AL.max))
            sch.op('vector', lambda e, sl=sl: e.tensor_scalar(
                out=tmA[:, sl, :], in0=tmA[:, sl, :], scalar1=MAGIC, scalar2=MAGIC,
                op0=AL.add, op1=AL.subtract))
            sch.op('vector', lambda e, sl=sl: e.tensor_copy(out=qi_t[:], in_=tmA[:, sl, :]))
            sch.op('vector', lambda e, sl=sl: e.tensor_copy(out=q_tm[:, sl, :], in_=qi_t[:]),
                   inc='sv' if sl == SL - 1 else None)
        quant_sv[t] = sch.v('sv')

    def emit_qT(t):
        qe = {}
        sch.wait('tensor', 'sv', quant_sv[t])
        sch.wait('tensor', 'ss', silu_ss[t][HC - 1])
        for dp in range(DC // 2):
            bank = pb[2 + dp % 2]
            if dp >= 2:
                sch.wait('tensor', 'ss', qe[2 * (dp - 2)])
            for sub in range(2):
                dc = 2 * dp + sub
                for sl in range(SL):
                    sch.op('tensor', lambda e, b=bank, dc=dc, sub=sub, sl=sl: e.transpose(
                        b[:, sub * TILE + sl * 128:sub * TILE + (sl + 1) * 128],
                        q_tm[:, sl, dc * 128:(dc + 1) * 128], idf[:]),
                        inc='st' if sl == SL - 1 else None)
            stv = sch.v('st')
            sch.wait('scalar', 'st', stv)
            sch.op('scalar', lambda e, b=bank, dp=dp: e.activation(
                out=q_fm[:, 2 * dp:2 * dp + 2, :].rearrange('p a b -> p (a b)'),
                in_=b[:, 0:2 * TILE], func=AF.Copy), inc='ss')
            v = sch.v('ss')
            qe[2 * dp] = v
            qe[2 * dp + 1] = v
        qT_st[t] = sch.v('st')
        qe_ss[t] = qe

    def emit_M3out(t):
        _par[0] = t % 2
        yb = ybuf[t % 2]
        sch.wait('tensor', 'sv', out_done.get(t - 1, 0))
        sch.wait('tensor', 'sv', y1_done.get(t + 1, 0))
        for sl in range(SL):
            for dc in range(DC):
                if sl == 0:
                    sch.wait('tensor', 'ss', qe_ss[t][dc])
                for hf in range(2):
                    b = pb[6 + hf]
                    sch.op('tensor', lambda e, b=b, dc=dc, hf=hf, sl=sl: e.matmul(
                        b[:], q_fm[:, dc, sl * 128:(sl + 1) * 128],
                        qrhs[:, dc, hf * 512:(hf + 1) * 512],
                        start=(dc == 0), stop=(dc == DC - 1)),
                        inc='st' if (dc == DC - 1 and hf == 1) else None)
            stv = sch.v('st')
            sch.wait('vector', 'st', stv)
            for hf in range(2):
                b = pb[6 + hf]
                sch.op('vector', lambda e, b=b, sl=sl, hf=hf: e.scalar_tensor_tensor(
                    out=n2_tm[:, sl, hf * 512:(hf + 1) * 512], in0=b[:],
                    scalar=scol1('ls', sl), in1=rs_repl[:, hf * 512:(hf + 1) * 512],
                    op0=AL.mult, op1=AL.mult))
                sch.op('vector', lambda e, sl=sl, hf=hf, yb=yb: e.tensor_tensor(
                    out=yb[:, sl, hf * 512:(hf + 1) * 512],
                    in0=n2_tm[:, sl, hf * 512:(hf + 1) * 512],
                    in1=x2_t[:, sl, hf * 512:(hf + 1) * 512], op=AL.add),
                    inc='sv' if hf == 1 else None)
            out_sl.setdefault(t, {})[sl] = sch.v('sv')
            if sl == 0:
                sch.wait('tensor', 'sv', sch.v('sv'))
        out_done[t] = sch.v('sv')

    def emit_outdma(t):
        yb = ybuf[t % 2]
        for sl in range(SL):
            sch.wait('sync', 'sv', out_sl[t][sl])
            rows = out_d[t * TILE + sl * 128:t * TILE + (sl + 1) * 128, :]
            sch.op('sync', lambda e, r=rows, yb=yb, sl=sl: e.dma_start(
                out=r, in_=yb[:, sl, :]), inc='so', inc_n=16)

    def emit_prefetch(t):
        sch.wait('sync', 'sv', y1_done[t - 2])
        sch.op('sync', lambda e, t=t: e.dma_start(out=xbuf[t % 2][:], in_=x_ap(t)),
               inc='sx', inc_n=16)
        x_sx[t] = sch.v('sx')

    n_iters = nt + 2 if 'tiles' not in DEBUG_SKIP else 0
    if n_iters:
        emit_mom1(0)
        emit_tiny1(0)
        emit_xau(0)
        if nt > 1:
            emit_mom1(1)
    for it in range(n_iters):
        pre = it if it < nt else None
        mid = it - 1 if 1 <= it <= nt else None
        fin = it - 2 if it >= 2 else None
        if pre is not None:
            emit_scanblock(pre)
        if fin is not None:
            emit_qT(fin)
        if pre is not None:
            emit_sostt(pre)
        if pre is not None and pre + 1 < nt:
            emit_tiny1(pre + 1)
        if fin is not None:
            emit_M3out(fin)
            emit_outdma(fin)
        if pre is not None:
            emit_soT(pre)
        if pre is not None and pre + 1 < nt:
            emit_xau(pre + 1)
        if pre is not None:
            emit_x1y1(pre)
        if mid is not None:
            emit_M1(mid, pre)
        elif pre is not None:
            emit_mom2(pre)
        if pre is not None:
            emit_tiny2(pre)
            emit_n2(pre)
        if mid is not None:
            emit_M2x2(mid)
            emit_tiny3(mid)
        if pre is not None:
            emit_n2T(pre)
        if mid is not None:
            emit_quant(mid)
        if pre is not None:
            if pre + 2 < nt:
                emit_prefetch(pre + 2)
                emit_mom1(pre + 2)
    if DEBUG_DUMP:
        dbg = {'stats': stats, 'carry0': carry0, 'a_v': a_v, 'tmA': tmA,
               'xa_t': xa_t, 'ud_fm': ud_fm, 'h0': hbuf[0], 'so_fm': so_fm,
               'y0': ybuf[0], 'x2_t': x2_t, 'n2_tm': n2_tm, 'n2_fm': n2_fm,
               'hid_fm': hid_fm, 'q_fm': q_fm, 'qrhs': qrhs, 'rs_repl': rs_repl,
               'fcT': fcT, 'projT': projT, 'xb0': xbuf[0]}
        for s in ('sv', 'ss', 'st', 'sg'):
            sch.wait('sync', s, sch.v(s))
        for i, nm in enumerate(DEBUG_DUMP):
            t = dbg[nm]
            ncols = 1
            for d_ in t.shape[1:]:
                ncols *= d_
            ncols = min(ncols, D)
            ap = t[:].rearrange('p a b -> p (a b)') if len(t.shape) == 3 else t[:]
            sch.op('sync', lambda e, i=i, ap=ap, ncols=ncols: e.dma_start(
                out=out_d[i * 128:(i + 1) * 128, 0:ncols], in_=ap[:, 0:ncols]),
                inc='so', inc_n=16)

    sch.wait('sync', 'so', sch.v('so'))

    emit_block(mark, len(sch.items))
    for g in sem_ctxs:
        g.__exit__(None, None, None)
    for g in reversed(ctxs):
        g.__exit__(None, None, None)
    nc.finalize()
    return nc


def _vec_layout(v):
    return np.ascontiguousarray(v.reshape(DC, 128).T.astype(np.float32))


def kernel(L0_attn_x, L0_attn_input_norm_w, L0_attn_a_log, L0_attn_c_out,
           L0_attn_d_skip, L0_attn_ff_norm_w, L0_attn_ff_fc_w, L0_attn_ff_proj_w,
           L0_ffn_rhs):
    B, T, _ = L0_attn_x.shape
    half = T // 2
    x = np.ascontiguousarray(L0_attn_x.astype(np.float32, copy=False))

    fcT_h = np.ascontiguousarray(L0_attn_ff_fc_w.T.astype(np.float32, copy=False))
    projT_h = np.ascontiguousarray(L0_attn_ff_proj_w.T.astype(np.float32, copy=False))
    rhsT_h = np.ascontiguousarray(L0_ffn_rhs.T.astype(np.float32, copy=False))
    shared = {
        'fcT_h': fcT_h, 'projT_h': projT_h, 'rhsT_h': rhsT_h,
        'inw_h': _vec_layout(L0_attn_input_norm_w),
        'alog_h': _vec_layout(L0_attn_a_log),
        'cout_h': _vec_layout(L0_attn_c_out),
        'dskip_h': _vec_layout(L0_attn_d_skip),
        'ffw_h': _vec_layout(L0_attn_ff_norm_w),
        'idf_h': np.eye(128, dtype=np.float32),
        'idb_h': np.eye(128, dtype=ml_dtypes.bfloat16),
        'ones_h': np.ones((1, 128), np.float32),
        'iota_h': np.arange(PREV_T, dtype=np.float32)[None, :],
    }
    zeros_prev = np.zeros((PREV_T, D), np.float32)
    in_maps = []
    for c in range(N_CORES):
        b, hf = c // 2, c % 2
        xc_c = np.ascontiguousarray(x[b, hf * half:(hf + 1) * half])
        xp_c = zeros_prev if hf == 0 else np.ascontiguousarray(
            x[b, half - PREV_T:half])
        in_maps.append({'xc': xc_c, 'xp': xp_c, **shared})

    nc = build_program(T_CORE, PREV_T)
    res = run_bass_kernel_spmd(nc, in_maps, core_ids=list(range(N_CORES)))

    out = np.empty((B, T, D), np.float32)
    for c in range(N_CORES):
        b, hf = c // 2, c % 2
        out[b, hf * half:(hf + 1) * half] = res.results[c]['out']
    return out
